# version 05
# baseline (speedup 1.0000x reference)
import numpy as np
import ml_dtypes
import concourse.bass as bass
import concourse.mybir as mybir
from concourse.bass_utils import run_bass_kernel_spmd

F32 = mybir.dt.float32
BF16 = mybir.dt.bfloat16
AF = mybir.ActivationFunctionType
ALU = mybir.AluOpType
AX = mybir.AxisListType

D = 2048
TCORE = 4096
NT = TCORE // 128
DC = 16
E = 32
I = 768
IB = 6
RNG = 512
NRANGE = TCORE // RNG
TB4 = RNG // 128
DB = 4
NQUAD = 8
RG = [[0, 1, 2, 3], [4, 5, 6, 7]]

WHALF = DC * 384
WGU_H = 2 * WHALF
WDOFF = 2 * WGU_H
WDLEN = IB * DB * 512
WCOLS = WDOFF + WDLEN


class P:
    def __init__(self, name, eng, sems):
        self.name = name
        self.eng = eng
        self.sems = sems
        self.c = {k: 0 for k in sems}
        self.last_mm = None
        self.maxw = {}

    def _emit_waits(self, waits):
        for (wq, wv) in waits:
            assert wv <= self.c[wq], (self.name, wq, wv, self.c[wq])
            if self.name == "tensor" and wq == "pe":
                continue
            if wv > self.maxw.get(wq, 0):
                self.maxw[wq] = wv
                self.eng.wait_ge(self.sems[wq], wv)

    def wait(self, who, q, val):
        assert val <= self.c[q], (who, q, val, self.c[q])
        if self.name == who:
            self._emit_waits([(q, val)])

    def dma(self, q, slot, out, in_, waits=(), **kw):
        eng_for = {"sy": "sync", "gp": "gpsimd", "scd": "scalar"}[q]
        prev = self.c[slot]
        self.c[slot] += 16
        if self.name == eng_for:
            self._emit_waits(waits)
            if q == "gp" and prev > 0:
                self._emit_waits([(slot, prev)])
            self.eng.dma_start(out=out, in_=in_, **kw).then_inc(self.sems[slot], 16)
        return (slot, self.c[slot])

    def mm(self, out, lhsT, rhs, start, stop, waits=()):
        if self.name == "tensor":
            self._emit_waits(waits)
            self.last_mm = self.eng.matmul(out, lhsT, rhs, start=start, stop=stop)

    def mm_inc(self):
        self.c["pe"] += 1
        if self.name == "tensor":
            self.last_mm.then_inc(self.sems["pe"], 1)
        return self.c["pe"]

    def transpose(self, out, in_, ident, waits=()):
        if self.name == "tensor":
            self._emit_waits(waits)
            self.last_mm = self.eng.transpose(out, in_, ident)
        return self.mm_inc()

    def vop(self, fn, *a, waits=(), drain=False, inc=True, **kw):
        if inc:
            self.c["v"] += 1
        if self.name == "vector":
            self._emit_waits(waits)
            if drain:
                self.eng.drain()
            ins = getattr(self.eng, fn)(*a, **kw)
            if inc:
                ins.then_inc(self.sems["v"], 1)
        return self.c["v"]

    def sop(self, fn, *a, waits=(), drain=False, inc=True, **kw):
        if inc:
            self.c["sc"] += 1
        if self.name == "scalar":
            self._emit_waits(waits)
            if drain:
                self.eng.drain()
            ins = getattr(self.eng, fn)(*a, **kw)
            if inc:
                ins.then_inc(self.sems["sc"], 1)
        return self.c["sc"]

    def gop(self, fn, *a, waits=(), inc=True, **kw):
        if inc:
            self.c["gpc"] += 1
        if self.name == "gpsimd":
            self._emit_waits(waits)
            ins = getattr(self.eng, fn)(*a, **kw)
            if inc:
                ins.then_inc(self.sems["gpc"], 1)
        return self.c["gpc"]

    def cc(self, kind, op, ins_, outs_, waits=()):
        self.c["cc"] += 1
        if self.name == "gpsimd":
            self._emit_waits(waits)
            self.eng.collective_compute(
                kind, op, replica_groups=RG, ins=ins_, outs=outs_
            ).then_inc(self.sems["cc"], 1)
        return self.c["cc"]


def build_nc(upto="full", dump=(), nranges=NRANGE, skip_attn=False, detect_races=True, nquads=NQUAD):
    nc = bass.Bass("TRN2", target_bir_lowering=False, num_devices=8,
                   detect_race_conditions=detect_races)

    x_d = nc.dram_tensor("x", [TCORE, D], F32, kind="ExternalInput").ap()
    wq_d = nc.dram_tensor("wq", [D, D], F32, kind="ExternalInput").ap()
    wk_d = nc.dram_tensor("wk", [D, D], F32, kind="ExternalInput").ap()
    wv_d = nc.dram_tensor("wv", [D, D], F32, kind="ExternalInput").ap()
    wo_d = nc.dram_tensor("wo", [D, D], F32, kind="ExternalInput").ap()
    gw_d = nc.dram_tensor("gw", [E, D], F32, kind="ExternalInput").ap()
    wg_d = nc.dram_tensor("wg", [E, D, I], F32, kind="ExternalInput").ap()
    wu_d = nc.dram_tensor("wu", [E, D, I], F32, kind="ExternalInput").ap()
    wd_d = nc.dram_tensor("wd", [E, I, D], F32, kind="ExternalInput").ap()
    idf_d = nc.dram_tensor("idf", [128, 128], F32, kind="ExternalInput").ap()
    idb_d = nc.dram_tensor("idb", [128, 128], BF16, kind="ExternalInput").ap()
    oh_d = nc.dram_tensor("oh", [E, E * 128], BF16, kind="ExternalInput").ap()
    out_d = nc.dram_tensor("out", [TCORE, D], F32, kind="ExternalOutput").ap()

    xn_sc = nc.dram_tensor("xn_sc", [NT, 128, D], BF16).ap()
    q_sc = nc.dram_tensor("q_sc", [128, NT, D], BF16).ap()
    k_sc = nc.dram_tensor("k_sc", [128, NT, D], BF16).ap()
    v_sc = nc.dram_tensor("v_sc", [128, DC, TCORE], BF16).ap()
    att_sc = nc.dram_tensor("att_sc", [128, DC, TCORE], BF16).ap()
    s_sc = nc.dram_tensor("s_sc", [NT, 128, D], F32).ap()
    scc_in = nc.dram_tensor("scc_in", [D, D], F32).ap()
    scc_out = nc.dram_tensor("scc_out", [512, D], F32).ap()
    agg_in = nc.dram_tensor("agg_in", [512, D], BF16).ap()
    agg_out = nc.dram_tensor("agg_out", [D, D], BF16).ap()
    wff_a = nc.dram_tensor("wff_a", [E // 2, 128, WCOLS], BF16).ap()
    wff_b = nc.dram_tensor("wff_b", [E // 2, 128, WCOLS], BF16).ap()
    wff_sc = lambda e: (wff_a[e] if e < E // 2 else wff_b[e - E // 2])

    dump_d = {}
    dump_specs = {
        "xn_sc": (xn_sc.rearrange("a p d -> (a p) d"), BF16, [NT * 128, D]),
        "q_sc": (q_sc.rearrange("p c t -> p (c t)"), BF16, [128, NT * D]),
        "k_sc": (k_sc.rearrange("p c t -> p (c t)"), BF16, [128, NT * D]),
        "v_sc": (v_sc.rearrange("p c t -> p (c t)"), BF16, [128, DC * TCORE]),
        "att_sc": (att_sc.rearrange("p c t -> p (c t)"), BF16, [128, DC * TCORE]),
        "s_sc": (s_sc.rearrange("a p d -> (a p) d"), F32, [NT * 128, D]),
        "scc_in": (scc_in, F32, [D, D]),
        "agg_out": (agg_out, BF16, [D, D]),
        "wff0": (wff_sc(0), BF16, [128, WCOLS]),
    }
    for nm in dump:
        dump_d[nm] = nc.dram_tensor("dump_" + nm, dump_specs[nm][2] if nm != "cT"
                                    else [E, TCORE], F32, kind="ExternalOutput").ap()

    from contextlib import ExitStack
    ctx = ExitStack()
    sb = lambda n, s, d: ctx.enter_context(nc.sbuf_tensor(n, s, d))
    psa = lambda n, s, d: ctx.enter_context(nc.psum_tensor(n, s, d))
    mksem = lambda n: ctx.enter_context(nc.semaphore(n))

    A = sb("A", [128, 61440], BF16)
    Aap = A.ap()
    B = sb("B", [128, 30720], BF16)
    Bap = B.ap()
    Bf = Bap.bitcast(F32)
    C = sb("C", [128, 6144], BF16)
    Cap = C.ap()
    Cf = Cap.bitcast(F32)

    identf = sb("identf", [128, 128], F32)
    identb = sb("identb", [128, 128], BF16)
    gwT = sb("gwT", [128, DC * E], F32)
    cT = sb("cT", [E, TCORE], BF16)
    oh_b = sb("oh_b", [E, 512], BF16)
    gw_sb = None
    stat = sb("stat", [128, 16], F32)
    ls = sb("ls", [128, E], F32)
    elb = sb("elb", [128, E], F32)
    vmax8 = sb("vmax8", [128, 8], F32)
    zap = sb("zap", [128, E], F32)
    selm = sb("selm", [128, E], F32)
    e8 = sb("e8", [128, 8], F32)
    ctile = sb("ctile", [128, E], F32)

    pb = [psa(f"pb{i}", [128, 512], F32) for i in range(8)]
    pbb = [pb[i].ap().bitcast(BF16) for i in range(8)]

    DMA_SLOTS = ["cst", "xl0", "xl1", "xs0", "xs1", "ax0", "ax1", "aq0", "aq1",
                 "wc", "bx0", "bx1", "bv0", "bv1", "kk", "qq0", "qq1",
                 "ss0", "ss1", "sa0", "sa1", "sx0", "sx1", "ag0", "ag1", "ab0", "ab1", "vv0", "vv1", "as0",
                 "as1", "ac0", "ac1", "s40", "s41", "xr0", "xr1", "oh1", "wg0",
                 "wg1", "wd0", "wd1", "wd2", "wd3", "wd4", "wd5",
                 "wgc0", "wgc1", "wdc0", "wdc1", "wdc2", "wdc3", "wdc4", "wdc5", "wb0", "wb1", "wdb0", "wdb1", "wdb2", "wdb3", "wdb4", "wdb5", "fs", "fsg", "os", "du1", "du2"]
    sems = {k: mksem(k) for k in ["pe", "v", "sc", "gpc", "cc"] + DMA_SLOTS}

    SUM, NM, NM2, VAR, SSQ, LNV, RSTD, BIAS, NM8, S8, RS8, EPS = range(12)
    st = lambda i: stat[:, i:i + 1]

    env = {}
    gw_sb = Bf[0:E, 13312:15360]

    def walk(p: P):
        tok = {}

        t_eps = p.vop("memset", st(EPS), 1e-5)
        t_idf = p.dma("sy", "cst", identf[:], idf_d)
        t_idb = p.dma("sy", "cst", identb[:], idb_d)
        t_gw = p.dma("sy", "cst", gw_sb[:], gw_d)
        ev_gwT = 0
        for dc in range(DC):
            bank = dc % 2
            w = [t_gw, t_idf]
            if dc >= 2:
                w.append(("v", ev_gwT))
            tk = p.transpose(pb[bank][:, 0:E],
                             gw_sb[:, dc * 128:(dc + 1) * 128],
                             identf[0:E, 0:E], waits=w)
            ev_gwT = p.vop("tensor_copy", gwT[:, dc * E:(dc + 1) * E],
                           pb[bank][:, 0:E], waits=[("pe", tk)])

        xbuf = [Bf[:, 0:2048], Bf[:, 2048:4096]]
        xnf = [Bf[:, 4096:6144], Bf[:, 6144:8192]]
        sqb = Bf[:, 8192:10240]
        xnTf = [Bf[:, 10240:12288], Bf[:, 12288:14336]]
        xnTb = [Cap[:, 0:2048], Cap[:, 2048:4096]]

        for tb in range(NT):
            pg = tb % 2
            w = []
            if tb >= 2:
                w += [("v", tok[("xnf", tb - 2)]), ("sc", tok[("ssq", tb - 2)])]
            t_x = p.dma("sy", f"xl{pg}", xbuf[pg],
                        x_d[tb * 128:(tb + 1) * 128, :], waits=w)
            tok[("rsum", tb)] = p.vop("reduce_sum", st(SUM), xbuf[pg],
                                      waits=[t_x], axis=AX.X)
            w = [t_x]
            if tb >= 1:
                w.append(("sc", tok[("ssq", tb - 1)]))
            tok[("ssq", tb)] = p.sop("activation", sqb, xbuf[pg], AF.Square,
                                     waits=w, accum_out=st(SSQ))
            p.vop("tensor_scalar_mul", st(NM), st(SUM), -1.0 / D, drain=True,
                  inc=False)
            p.vop("tensor_mul", st(NM2), st(NM), st(NM), drain=True, inc=False)
            p.vop("tensor_scalar_mul", st(VAR), st(SSQ), 1.0 / D,
                  waits=[("sc", tok[("ssq", tb)])], drain=True, inc=False)
            tok[("var", tb)] = p.vop("tensor_sub", st(VAR), st(VAR), st(NM2),
                                     drain=True)
            p.sop("activation", st(LNV), st(VAR), AF.Ln, bias=st(EPS),
                  waits=[("v", tok[("var", tb)])], inc=False)
            tok[("rstd", tb)] = p.sop("activation", st(RSTD), st(LNV), AF.Exp,
                                      scale=-0.5, drain=True)
            w = [("sc", tok[("rstd", tb)])]
            if tb >= 2:
                w.append(("pe", tok[("tr", tb - 2)]))
            tok[("xnf", tb)] = p.vop("tensor_scalar", xnf[pg], xbuf[pg],
                                     st(NM), st(RSTD), ALU.add, ALU.mult,
                                     waits=w, drain=True)
            for dc in range(DC):
                bank = dc % 2
                w = [("v", tok[("xnf", tb)])]
                key = ("trev", tb, dc - 2) if dc >= 2 else \
                      (("trev", tb - 1, DC - 2 + dc) if tb >= 1 else None)
                if key:
                    w.append(("v", tok[key]))
                tk = p.transpose(pb[bank][:, 0:128],
                                 xnf[pg][:, dc * 128:(dc + 1) * 128],
                                 identf[:], waits=w)
                if dc == DC - 1:
                    tok[("tr", tb)] = tk
                w = [("pe", tk)]
                if dc == 0 and tb >= 2:
                    w += [("pe", tok[("rt", tb - 2)]), ("sc", tok[("xnTb", tb - 2)])]
                tok[("trev", tb, dc)] = p.vop(
                    "tensor_copy", xnTf[pg][:, dc * 128:(dc + 1) * 128],
                    pb[bank][:, 0:128], waits=w)
            w = [("v", tok[("trev", tb, DC - 1)])]
            if tb >= 2:
                w.append(tok[("xnst", tb - 2)])
            tok[("xnTb", tb)] = p.sop("copy", xnTb[pg], xnTf[pg], waits=w)
            tok[("xnst", tb)] = p.dma(
                "sy", f"xs{pg}", xn_sc[tb], xnTb[pg],
                waits=[("sc", tok[("xnTb", tb)])])
            for dc in range(DC):
                w = [("v", tok[("trev", tb, dc)])]
                if dc == 0 and tb >= 1:
                    w.append(("v", tok[("lg", tb - 1)]))
                p.mm(pb[2][:, 0:E], xnTf[pg][:, dc * 128:(dc + 1) * 128],
                     gwT[:, dc * E:(dc + 1) * E],
                     start=(dc == 0), stop=(dc == DC - 1), waits=w)
            tok[("rt", tb)] = p.mm_inc()
            w = [("pe", tok[("rt", tb)])]
            if tb >= 1:
                w.append(("sc", tok[("el", tb - 1)]))
            tok[("lg", tb)] = p.vop("tensor_copy", ls[:], pb[2][:, 0:E],
                                    waits=w, drain=True)
            p.vop("max", drain=True, inc=False, out=vmax8[:], in_=ls[:])
            p.vop("match_replace", drain=True, inc=False, out=zap[:],
                  in_to_replace=vmax8[:], in_values=ls[:], imm_value=-1e30)
            p.vop("tensor_tensor", selm[:], ls[:], zap[:], ALU.not_equal,
                  drain=True, inc=False)
            tok[("nm8", tb)] = p.vop("tensor_scalar_mul", st(NM8), vmax8[:, 0:1],
                                     -1.0, drain=True)
            tok[("e8", tb)] = p.sop("activation", e8[:], vmax8[:], AF.Exp,
                                    bias=st(NM8), waits=[("v", tok[("nm8", tb)])])
            p.vop("reduce_sum", st(S8), e8[:], axis=AX.X,
                  waits=[("sc", tok[("e8", tb)])], drain=True, inc=False)
            tok[("rs8", tb)] = p.vop("reciprocal", st(RS8), st(S8), drain=True)
            tok[("el", tb)] = p.sop("activation", elb[:], ls[:], AF.Exp,
                                    bias=st(NM8))
            w = [("sc", tok[("el", tb)]), ("v", tok[("rs8", tb)])]
            if tb >= 1:
                w.append(("v", tok[("ctev", tb - 1)]))
            tok[("c", tb)] = p.vop("scalar_tensor_tensor", waits=w, drain=True,
                                   out=ctile[:], in0=elb[:], scalar=st(RS8),
                                   in1=selm[:], op0=ALU.mult, op1=ALU.mult)
            tk = p.transpose(pb[3][0:E, 0:128], ctile[:], identf[:],
                             waits=[("v", tok[("c", tb)])])
            tok[("ctev", tb)] = p.vop("tensor_copy", cT[:, tb * 128:(tb + 1) * 128],
                                      pb[3][0:E, 0:128], waits=[("pe", tk)])

        env.update(locals())
        if upto == "prologue":
            finish(p, tok)
            return

        if skip_attn:
            tok[("a4_mm", NT - 1)] = tok[("rt", NT - 1)]
            tok[("a4_ev", NT - 1)] = tok[("ctev", NT - 1)]
            tok[("a4_st", NT - 1)] = tok[("xnst", NT - 1)]
            tok[("a4_st", NT - 2)] = tok[("xnst", NT - 2)]
            ffn(p, tok)
            finish(p, tok)
            return

        wqh = Aap[:, 0:16384]
        wkh = Aap[:, 16384:32768]
        xnt = [Aap[:, 32768 + i * 2048: 32768 + (i + 1) * 2048] for i in range(2)]
        qkst = [Cap[:, i * 1024:(i + 1) * 1024] for i in range(4)]

        for half in range(2):
            tw = []
            for (wsrc, dstbuf) in ((wq_d, wqh), (wk_d, wkh)):
                for piece in range(2):
                    src = wsrc.rearrange("(c p) f -> p c f", p=128)[
                        :, piece * 8:(piece + 1) * 8,
                        half * 1024:(half + 1) * 1024]
                    dgt = dstbuf.rearrange("p (c f) -> p c f", f=1024)[
                        :, piece * 8:(piece + 1) * 8, :]
                    w = []
                    if half == 1:
                        w.append(("pe", tok[("a1a_mm", 0, NT - 1)]))
                    tw.append(p.dma("gp", "wc", dgt, src, waits=w))
            t_w = tw[-1]
            for tb in range(NT):
                pg = tb % 2
                w = [tok[("xnst", tb)]]
                prev = ("a1a_mm", half, tb - 2) if tb >= 2 else \
                       (("a1a_mm", half - 1, NT - 2 + tb) if half >= 1 else None)
                if prev:
                    w.append(("pe", tok[prev]))
                t_xn = p.dma("sy", f"ax{pg}", xnt[pg], xn_sc[tb], waits=w)
                bs = pg * 4
                for dc in range(DC):
                    w = [t_w, t_xn]
                    if dc == 0:
                        if half == 0 and tb == 0:
                            w.append(("v", tok[("ctev", NT - 1)]))
                        prev = ("a1a_ev", half, tb - 2) if tb >= 2 else \
                               (("a1a_ev", half - 1, NT - 2 + tb) if half >= 1 else None)
                        if prev:
                            w.append(("v", tok[prev]))
                    lhs = xnt[pg][:, dc * 128:(dc + 1) * 128]
                    first = True
                    for (wbuf, bo) in ((wqh, 0), (wkh, 2)):
                        for fb in range(2):
                            p.mm(pb[bs + bo + fb][:], lhs,
                                 wbuf[:, dc * 1024 + fb * 512: dc * 1024 + (fb + 1) * 512],
                                 start=(dc == 0), stop=(dc == DC - 1),
                                 waits=w if (dc == 0 and first) else ())
                            first = False
                tok[("a1a_mm", half, tb)] = p.mm_inc()
                w = [("pe", tok[("a1a_mm", half, tb)])]
                prev = ("a1a_st", half, tb - 2) if tb >= 2 else \
                       (("a1a_st", half - 1, NT - 2 + tb) if half >= 1 else None)
                if prev:
                    w.append(tok[prev])
                if half == 0 and tb < 2:
                    w += [tok[("xnst", NT - 1)], tok[("xnst", NT - 2)]]
                p.vop("tensor_copy", qkst[pg * 2][:, 0:512], pb[bs + 0][:],
                      waits=w, inc=False)
                p.vop("tensor_copy", qkst[pg * 2][:, 512:1024], pb[bs + 1][:],
                      inc=False)
                p.vop("tensor_copy", qkst[pg * 2 + 1][:, 0:512], pb[bs + 2][:],
                      inc=False)
                tok[("a1a_ev", half, tb)] = p.vop(
                    "tensor_copy", qkst[pg * 2 + 1][:, 512:1024], pb[bs + 3][:])
                p.dma("scd", f"aq{pg}",
                      q_sc[:, tb, half * 1024:(half + 1) * 1024],
                      qkst[pg * 2], waits=[("v", tok[("a1a_ev", half, tb)])])
                tok[("a1a_st", half, tb)] = p.dma(
                    "scd", f"aq{pg}",
                    k_sc[:, tb, half * 1024:(half + 1) * 1024],
                    qkst[pg * 2 + 1])

        if upto == "a1a":
            finish(p, tok)
            return

        wvh = Aap[:, 0:16384]
        xn5 = [Aap[:, 16384 + i * 8192: 16384 + (i + 1) * 8192] for i in range(2)]
        vst = [Cap[:, i * 512:(i + 1) * 512] for i in range(8)]
        for half in range(2):
            tw = []
            for piece in range(2):
                src = wv_d.rearrange("(c p) (b f) -> p c b f", p=128, f=128)[
                    :, piece * 8:(piece + 1) * 8, half * 8:(half + 1) * 8, :]
                dgt = wvh.rearrange("p (c b f) -> p c b f", b=8, f=128)[
                    :, piece * 8:(piece + 1) * 8, :, :]
                w = []
                if half == 0:
                    w.append(("pe", tok[("a1a_mm", 1, NT - 1)]))
                else:
                    w.append(("pe", tok[("a1b_mm", 0, 7)]))
                tw.append(p.dma("gp", "wc", dgt, src, waits=w))
            t_w = tw[-1]
            for t5 in range(8):
                pg = t5 % 2
                w = [tok[("xnst", NT - 1)], tok[("xnst", NT - 2)]]
                prev = ("a1b_mm", half, t5 - 2) if t5 >= 2 else \
                       (("a1b_mm", half - 1, 6 + t5) if half >= 1 else None)
                if prev:
                    w.append(("pe", tok[prev]))
                if half == 0 and t5 < 2:
                    w.append(("pe", tok[("a1a_mm", 1, NT - 1)]))
                for i4 in range(4):
                    t_xn = p.dma(
                        "sy", f"bx{pg}",
                        xn5[pg].rearrange("p (c t) -> p c t", t=512)[
                            :, :, i4 * 128:(i4 + 1) * 128],
                        xn_sc[t5 * 4 + i4].rearrange("p (c t) -> p c t", t=128),
                        waits=w if i4 == 0 else ())
                for dc in range(DC):
                    for fb in range(8):
                        w = [t_w, t_xn]
                        if dc == 0:
                            key = ("a1b_ev", half, t5 - 1, fb) if t5 >= 1 else \
                                  (("a1b_ev", half - 1, 7, fb) if half >= 1 else None)
                            if key:
                                w.append(("v", tok[key]))
                            if half == 0 and t5 == 0:
                                w.append(("v", tok[("a1a_ev", 1, NT - 1)]))
                        p.mm(pb[fb][:],
                             wvh[:, (dc * 8 + fb) * 128:(dc * 8 + fb + 1) * 128],
                             xn5[pg][:, dc * 512:(dc + 1) * 512],
                             start=(dc == 0), stop=(dc == DC - 1),
                             waits=w if dc == 0 else ())
                tok[("a1b_mm", half, t5)] = p.mm_inc()
                for fb in range(8):
                    w = [("pe", tok[("a1b_mm", half, t5)])]
                    key = ("a1b_st", half, t5 - 1, 7) if t5 >= 1 else \
                          (("a1b_st", half - 1, 7, 7) if half >= 1 else None)
                    if key:
                        w.append(tok[key])
                    if half == 0 and t5 == 0:
                        w += [tok[("a1a_st", 1, NT - 1)], tok[("a1a_st", 1, NT - 2)]]
                    tok[("a1b_ev", half, t5, fb)] = p.vop(
                        "tensor_copy", vst[fb], pb[fb][:], waits=w)
                    tok[("a1b_st", half, t5, fb)] = p.dma(
                        "sy", f"bv{pg}",
                        v_sc[:, half * 8 + fb, t5 * 512:(t5 + 1) * 512],
                        vst[fb], waits=[("v", tok[("a1b_ev", half, t5, fb)])])

        if upto == "a1b":
            finish(p, tok)
            return

        kb = Aap[:, 0:32768]
        qb = [Aap[:, 32768 + i * 8192: 32768 + (i + 1) * 8192] for i in range(2)]
        sst = [Bf[:, i * 512:(i + 1) * 512] for i in range(8)]
        for kh in range(2):
            w = [tok[("a1b_st", 1, 7, 7)], tok[("a1b_st", 1, 6, 7)],
                 tok[("a1a_st", 1, NT - 1)], tok[("a1a_st", 1, NT - 2)]]
            if kh == 1:
                w.append(("pe", tok[("sc_mm", 0, 3, 3)]))
            t_k = p.dma("sy", "kk", kb.rearrange("p (c f) -> p c f", f=2048),
                        k_sc[:, kh * 16:(kh + 1) * 16, :], waits=w)
            for eq in range(4):
                pg = eq % 2
                w = [tok[("a1a_st", 1, NT - 1)], tok[("a1a_st", 1, NT - 2)]]
                prev = ("sc_mm", kh, eq - 2, 3) if eq >= 2 else \
                       (("sc_mm", kh - 1, 2 + eq, 3) if kh >= 1 else None)
                if prev:
                    w.append(("pe", tok[prev]))
                t_q = p.dma("sy", f"qq{pg}",
                            qb[pg].rearrange("p (c e) -> p c e", e=512),
                            q_sc[:, kh * 16:(kh + 1) * 16, eq * 512:(eq + 1) * 512],
                            waits=w)
                for fb in range(4):
                    bs = ((eq * 4 + fb) % 2) * 4
                    for tc in range(16):
                        w = [t_k, t_q]
                        if tc == 0:
                            key = ("sc_ev", kh, eq, fb - 2) if fb >= 2 else \
                                  (("sc_ev", kh, eq - 1, 2 + fb) if eq >= 1 else
                                   (("sc_ev", kh - 1, 3, 2 + fb) if kh >= 1 else None))
                            if key:
                                w.append(("v", tok[key]))
                            elif fb < 2:
                                w.append(("v", tok[("a1b_ev", 1, 7, 7)]))
                        for eb in range(4):
                            p.mm(pb[bs + eb][:],
                                 qb[pg][:, tc * 512 + eb * 128: tc * 512 + (eb + 1) * 128],
                                 kb[:, tc * 2048 + fb * 512: tc * 2048 + (fb + 1) * 512],
                                 start=(tc == 0), stop=(tc == 15),
                                 waits=w if (tc == 0 and eb == 0) else ())
                    tok[("sc_mm", kh, eq, fb)] = p.mm_inc()
                    w = [("pe", tok[("sc_mm", kh, eq, fb)])]
                    key = ("sc_fin", kh, eq, fb - 2) if fb >= 2 else \
                          (("sc_fin", kh, eq - 1, 2 + fb) if eq >= 1 else
                           (("sc_fin", kh - 1, 3, 2 + fb) if kh >= 1 else None))
                    if key:
                        w.append(tok[key])
                    if kh == 0 and eq == 0 and fb < 2:
                        w += [tok[("a1b_st", 1, 7, 7)], tok[("a1b_st", 1, 6, 7)]]
                    gpar = (eq * 4 + fb) % 2
                    for eb in range(4):
                        p.vop("tensor_copy", sst[gpar * 4 + eb], pb[bs + eb][:],
                              waits=w if eb == 0 else (), inc=(eb == 3))
                    tok[("sc_ev", kh, eq, fb)] = p.c["v"]
                    for eb in range(4):
                        dst = scc_in[(eq * 4 + eb) * 128:(eq * 4 + eb + 1) * 128,
                                     fb * 512:(fb + 1) * 512]
                        wv = [("v", tok[("sc_ev", kh, eq, fb)])]
                        if kh == 0:
                            tt = p.dma("sy", f"ss{gpar}", dst,
                                       sst[gpar * 4 + eb], waits=wv)
                            tok[("sc_fin", kh, eq, fb)] = tt
                        else:
                            wv += tok[("ss_all")]
                            tt = p.dma("gp", f"sa{gpar}", dst,
                                       sst[gpar * 4 + eb], waits=wv,
                                       accum_op=ALU.add)
                            tok[("sc_fin", kh, eq, fb)] = tt
            if kh == 0:
                tok[("ss_all")] = [("ss0", p.c["ss0"]), ("ss1", p.c["ss1"])]

        if upto == "a2":
            finish(p, tok)
            return

        t_rs = p.cc("ReduceScatter", ALU.add, [scc_in], [scc_out],
                    waits=[("ss0", p.c["ss0"]), ("ss1", p.c["ss1"]),
                           ("sa0", p.c["sa0"]), ("sa1", p.c["sa1"])])
        smx = [Bf[:, i * 2048:(i + 1) * 2048] for i in range(2)]
        smy = [Bf[:, 4096 + i * 2048: 4096 + (i + 1) * 2048] for i in range(2)]
        smo = [Bap[:, 16384 + i * 2048: 16384 + (i + 1) * 2048] for i in range(2)]
        for i in range(4):
            pg = i % 2
            w = [("cc", t_rs)]
            if i >= 2:
                w.append(tok[("ag_st", i - 2)])
            t_s = p.dma("sy", f"sx{pg}", smx[pg],
                        scc_out[i * 128:(i + 1) * 128, :], waits=w)
            p.vop("reduce_max", st(SUM), smx[pg], axis=AX.X,
                  waits=[t_s], drain=True, inc=False)
            tok[("smm", i)] = p.vop("tensor_scalar_mul", st(NM), st(SUM), -0.125,
                                    drain=True)
            tok[("sme", i)] = p.sop("activation", smy[pg], smx[pg], AF.Exp,
                                    scale=0.125, bias=st(NM), accum_out=st(S8),
                                    waits=[("v", tok[("smm", i)])])
            tok[("smr", i)] = p.vop("reciprocal", st(RS8), st(S8),
                                    waits=[("sc", tok[("sme", i)])])
            tok[("smA", i)] = p.vop("tensor_scalar_mul", smo[pg], smy[pg], st(RS8),
                                    drain=True)
            tok[("ag_st", i)] = p.dma("scd", f"ag{pg}",
                                      agg_in[i * 128:(i + 1) * 128, :],
                                      smo[pg], waits=[("v", tok[("smA", i)])])
        t_ag = p.cc("AllGather", ALU.bypass, [agg_in], [agg_out],
                    waits=[("ag0", p.c["ag0"]), ("ag1", p.c["ag1"])])

        ATs = Aap[:, 0:32768]
        abuf = [Aap[:, 32768 + i * 2048: 32768 + (i + 1) * 2048] for i in range(2)]
        for ec in range(DC):
            pg = ec % 2
            w = [("cc", t_ag)]
            if ec >= 2:
                w.append(("v", tok[("at_ev", ec - 2, DC - 1)]))
            t_a = p.dma("sy", f"ab{pg}", abuf[pg],
                        agg_out[ec * 128:(ec + 1) * 128, :], waits=w)
            for fc in range(DC):
                bank = fc % 2
                w = [t_a, t_idb]
                key = ("at_ev", ec, fc - 2) if fc >= 2 else \
                      (("at_ev", ec - 1, DC - 2 + fc) if ec >= 1 else None)
                if key:
                    w.append(("v", tok[key]))
                tk = p.transpose(pbb[bank][:, 0:128],
                                 abuf[pg][:, fc * 128:(fc + 1) * 128], identb[:],
                                 waits=w)
                tok[("at_ev", ec, fc)] = p.vop(
                    "tensor_copy", _at_slice(Aap, fc, ec),
                    pbb[bank][:, 0:128], waits=[("pe", tk)])

        vhb = Aap[:, 36864: 36864 + 16384]
        atst = [Cap[:, i * 512:(i + 1) * 512] for i in range(8)]
        t_at_last = tok[("at_ev", DC - 1, DC - 1)]
        for tq in range(4):
            w = [tok[("a1b_st", 1, 7, 7)], tok[("a1b_st", 1, 6, 7)],
                 ("v", t_at_last)]
            if tq >= 1:
                w.append(("pe", tok[("a3_mm", tq - 1, DC - 1)]))
            t_v = p.dma("sy", f"vv{tq % 2}",
                        vhb.rearrange("p (c t) -> p c t", t=1024),
                        v_sc[:, :, tq * 1024:(tq + 1) * 1024], waits=w)
            for eb in range(DC):
                bs = (eb % 2) * 4
                for fc in range(DC):
                    w = [t_v]
                    if fc == 0:
                        key = ("a3_ev", tq, eb - 2) if eb >= 2 else \
                              (("a3_ev", tq - 1, DC - 2 + eb) if tq >= 1 else None)
                        if key:
                            w.append(("v", tok[key]))
                    for t4 in range(2):
                        p.mm(pb[bs + t4][:],
                             _at_slice(Aap, fc, eb),
                             vhb[:, fc * 1024 + t4 * 512: fc * 1024 + (t4 + 1) * 512],
                             start=(fc == 0), stop=(fc == DC - 1),
                             waits=w if (fc == 0 and t4 == 0) else ())
                tok[("a3_mm", tq, eb)] = p.mm_inc()
                w = [("pe", tok[("a3_mm", tq, eb)])]
                key = ("a3_stl", tq, eb - 2) if eb >= 2 else \
                      (("a3_stl", tq - 1, DC - 2 + eb) if tq >= 1 else None)
                if key:
                    w.append(tok[key])
                for t4 in range(2):
                    p.vop("tensor_copy", atst[(eb % 2) * 2 + t4], pb[bs + t4][:],
                          waits=w if t4 == 0 else (), inc=(t4 == 1))
                tok[("a3_ev", tq, eb)] = p.c["v"]
                for t4 in range(2):
                    tok[("a3_stl", tq, eb)] = p.dma(
                        "sy", f"as{eb % 2}",
                        att_sc[:, eb, tq * 1024 + t4 * 512: tq * 1024 + (t4 + 1) * 512],
                        atst[(eb % 2) * 2 + t4],
                        waits=[("v", tok[("a3_ev", tq, eb)])])

        wob = Aap[:, 0:32768]
        acol = [Aap[:, 32768 + i * 2048: 32768 + (i + 1) * 2048] for i in range(2)]
        sstg = [Bf[:, i * 2048:(i + 1) * 2048] for i in range(2)]
        tw = []
        for piece in range(2):
            src = wo_d.rearrange("(c p) f -> p c f", p=128)[
                :, piece * 8:(piece + 1) * 8, :]
            dgt = wob.rearrange("p (c f) -> p c f", f=2048)[
                :, piece * 8:(piece + 1) * 8, :]
            tw.append(p.dma("gp", "wc", dgt, src,
                            waits=[("pe", tok[("a3_mm", 3, DC - 1)])]))
        t_wo = tw[-1]
        for tb in range(NT):
            pg = tb % 2
            w = [tok[("a3_stl", 3, DC - 1)], tok[("a3_stl", 3, DC - 2)]]
            if tb >= 2:
                w.append(("pe", tok[("a4_mm", tb - 2)]))
            t_ac = p.dma("sy", f"ac{pg}",
                         acol[pg].rearrange("p (c t) -> p c t", t=128),
                         att_sc[:, :, tb * 128:(tb + 1) * 128], waits=w)
            bs = pg * 4
            for ec in range(DC):
                w = [t_wo, t_ac]
                if ec == 0:
                    if tb < 2:
                        w.append(("v", tok[("a3_ev", 3, DC - 1)]))
                    else:
                        w.append(("v", tok[("a4_ev", tb - 2)]))
                for db in range(4):
                    p.mm(pb[bs + db][:], acol[pg][:, ec * 128:(ec + 1) * 128],
                         wob[:, ec * 2048 + db * 512: ec * 2048 + (db + 1) * 512],
                         start=(ec == 0), stop=(ec == DC - 1),
                         waits=w if (ec == 0 and db == 0) else ())
            tok[("a4_mm", tb)] = p.mm_inc()
            w = [("pe", tok[("a4_mm", tb)])]
            if tb >= 2:
                w.append(tok[("a4_st", tb - 2)])
            for db in range(4):
                p.vop("tensor_copy", sstg[pg][:, db * 512:(db + 1) * 512],
                      pb[bs + db][:], waits=w if db == 0 else (), inc=(db == 3))
            tok[("a4_ev", tb)] = p.c["v"]
            tok[("a4_st", tb)] = p.dma("sy", f"s4{pg}", s_sc[tb], sstg[pg],
                                       waits=[("v", tok[("a4_ev", tb)])])

        env.update({k: v for k, v in locals().items() if k != "p"})
        if upto == "attn":
            finish(p, tok)
            return

        ffn(p, tok)
        finish(p, tok)

    def ffn(p: P, tok):
        xnr = [Aap[:, i * 8192:(i + 1) * 8192] for i in range(2)]
        wgu = [Aap[:, 16384 + i * 4096: 16384 + (i + 1) * 4096] for i in range(2)]
        hq = Aap[:, 24576: 24576 + 12288]
        wdr = [Aap[:, 36864 + i * 12288: 36864 + (i + 1) * 12288] for i in range(2)]
        yac = Bf[:, 0:8192]
        cbc = Bap[:, 16384:18432]
        fb_g = [Bap[:, 18432 + i * 2048: 18432 + (i + 1) * 2048] for i in range(4)]
        fb_s = Cap[:, 4096:6144]
        fb_x = Bf[:, 13312:15360]
        tsil = [Cf[:, i * 512:(i + 1) * 512] for i in range(2)]
        tuu = [Cf[:, 1024 + i * 512: 1024 + (i + 1) * 512] for i in range(2)]

        a4mm_last = tok[("a4_mm", NT - 1)]
        a4ev_last = ("v", tok[("a4_ev", NT - 1)])
        s4_toks = [tok[("a4_st", NT - 1)], tok[("a4_st", NT - 2)]]

        env.update(dict(yac=yac, fb_g=fb_g, fb_s=fb_s, fb_x=fb_x,
                        s4_toks=s4_toks))
        wgu_n = 0
        wd_n = 0
        NR_RUN = nranges
        gcnt = 0
        bankfree = {b: a4ev_last for b in range(4)}
        stage_free = {0: None, 1: None}
        for r in range(NR_RUN):
            w = [tok[("xnst", NT - 1)], tok[("xnst", NT - 2)]]
            if r < 2:
                w.append(("pe", a4mm_last))
            else:
                w.append(("pe", tok[("gu_mm_last", r - 2)]))
            for i4 in range(4):
                tok[("xnr", r)] = p.dma(
                    "sy", f"xr{r % 2}",
                    xnr[r % 2].rearrange("p (c t) -> p c t", t=512)[
                        :, :, i4 * 128:(i4 + 1) * 128],
                    xn_sc[r * 4 + i4].rearrange("p (c t) -> p c t", t=128),
                    waits=w if i4 == 0 else ())
            if r >= 1:
                fin_silu(p, tok, r - 1)
            for q4 in range(nquads):
                if r >= 1 and 1 <= q4 <= 4:
                    t4f = q4 - 1
                    if t4f >= 1:
                        fin_prod(p, tok, r - 1, t4f - 1)
                    fin_load(p, tok, r - 1, t4f)
                if r >= 1 and q4 == 5:
                    fin_prod(p, tok, r - 1, 3)
                prevq = (r, q4 - 1) if q4 >= 1 else \
                        ((r - 1, nquads - 1) if r >= 1 else None)
                w = []
                if prevq:
                    w.append(("pe", tok[("cb_mm", prevq[0], prevq[1], 3)]))
                t_oh = p.dma("sy", "oh1", oh_b[:],
                             oh_d[:, q4 * 512:(q4 + 1) * 512], waits=w)
                for j in range(4):
                    bank = 4 + (j % 2)
                    w = [t_oh, ("v", tok[("ctev", NT - 1)])]
                    if j < 2:
                        if prevq:
                            w.append(("v", tok[("dn_ev", prevq[0], prevq[1], 1)]))
                        else:
                            w.append(a4ev_last)
                    else:
                        w.append(("sc", tok[("cb_ev", r, q4, j - 2)]))
                    p.mm(pb[bank][:], oh_b[:, j * 128:(j + 1) * 128],
                         cT[:, r * 512:(r + 1) * 512], start=True, stop=True,
                         waits=w)
                    tok[("cb_mm", r, q4, j)] = p.mm_inc()
                    w = [("pe", tok[("cb_mm", r, q4, j)])]
                    if prevq:
                        w.append(("v", tok[("gu_tu", prevq[0], prevq[1], j, IB - 1)]))
                    tok[("cb_ev", r, q4, j)] = p.sop(
                        "copy", cbc[:, j * 512:(j + 1) * 512], pb[bank][:, 0:512],
                        waits=w)
                for j in range(4):
                    e = q4 * 4 + j
                    for ib in range(IB):
                        bufi = wgu_n % 2
                        wgu_n += 1
                        w = []
                        if wgu_n >= 3:
                            w.append(("pe", tok[("gu_3rd_done", wgu_n - 2)]))
                            if r == 0:
                                w.append(tok[("wgu_wb_n", wgu_n - 2)])
                        elif r == 0:
                            w.append(("pe", a4mm_last))
                        if r == 0:
                            src_g = wg_d[e].rearrange("(c p) i -> p c i", p=128)[
                                :, :, ib * 128:(ib + 1) * 128]
                            src_u = wu_d[e].rearrange("(c p) i -> p c i", p=128)[
                                :, :, ib * 128:(ib + 1) * 128]
                            p.dma("gp", f"wgc{bufi}",
                                  wgu[bufi].rearrange("p (c i) -> p c i", i=128)[
                                      :, 0:DC, :], src_g, waits=w)
                            tok[("wgu", wgu_n)] = p.dma(
                                "gp", f"wgc{bufi}",
                                wgu[bufi].rearrange("p (c i) -> p c i", i=128)[
                                    :, DC:2 * DC, :], src_u)
                            tok[("wgu_wb", e, ib)] = p.dma(
                                "scd", f"wb{bufi}",
                                wff_sc(e)[:, ib * 4096:(ib + 1) * 4096],
                                wgu[bufi], waits=[tok[("wgu", wgu_n)]])
                            tok[("wgu_wb_n", wgu_n)] = tok[("wgu_wb", e, ib)]
                        else:
                            if r == 1:
                                w += tok[("wb_final",)]
                            tok[("wgu", wgu_n)] = p.dma(
                                "sy", f"wg{bufi}", wgu[bufi],
                                wff_sc(e)[:, ib * 4096:(ib + 1) * 4096], waits=w)
                        if True:
                            gbank = (2 * gcnt) % 4
                            ubank = (2 * gcnt + 1) % 4
                            gcnt += 1
                            for gu, bank in ((0, gbank), (1, ubank)):
                                w = [tok[("wgu", wgu_n)],
                                     tok[("xnr", r)]]
                                bf = bankfree[bank]
                                if bf:
                                    w.append(bf)
                                off = gu * 2048
                                for dc in range(DC):
                                    p.mm(pb[bank][:],
                                         wgu[bufi][:, off + dc * 128: off + dc * 128 + 128],
                                         xnr[r % 2][:, dc * 512:(dc + 1) * 512],
                                         start=(dc == 0), stop=(dc == DC - 1),
                                         waits=w if dc == 0 else ())
                                tok[("gu_mmt", r, q4, j, ib, gu)] = p.mm_inc()
                            sf = stage_free[ib % 2]
                            w = [("pe", tok[("gu_mmt", r, q4, j, ib, 0)])]
                            if sf:
                                w.append(sf)
                            tok[("gu_sil", r, q4, j, ib)] = p.sop(
                                "activation", tsil[ib % 2], pb[gbank][:], AF.Silu,
                                waits=w)
                            w = [("pe", tok[("gu_mmt", r, q4, j, ib, 1)]),
                                 ("sc", tok[("gu_sil", r, q4, j, ib)])]
                            tok[("gu_tu", r, q4, j, ib)] = p.vop(
                                "tensor_tensor", tuu[ib % 2], pb[ubank][:],
                                cbc[:, j * 512:(j + 1) * 512], ALU.mult, waits=w)
                            w = [("v", tok[("gu_tu", r, q4, j, ib)])]
                            if prevq:
                                w.append(("pe", tok[("dn_mm", prevq[0], prevq[1], 1)]))
                            hv = p.gop(
                                "tensor_tensor",
                                hq[:, (j * IB + ib) * 512:(j * IB + ib + 1) * 512],
                                tsil[ib % 2], tuu[ib % 2], ALU.mult, waits=w)
                            tok[("gu_hv", r, q4, j, ib)] = hv
                            stage_free[ib % 2] = ("gpc", hv)
                            bankfree[gbank] = ("gpc", hv)
                            bankfree[ubank] = ("gpc", hv)
                        tok[("gu_3rd_done", wgu_n)] = tok[("gu_mmt", r, q4, j, ib, 1)]
                tok[("gu_mm_last", r)] = tok[("gu_mmt", r, nquads - 1, 3, IB - 1, 1)] \
                    if q4 == nquads - 1 else tok.get(("gu_mm_last", r), 0)
                for tp in range(2):
                    for j in range(4):
                        e = q4 * 4 + j
                        wdi = wd_n % 2
                        wd_n += 1
                        w = []
                        if wd_n >= 3:
                            w.append(("pe", tok[("dn_grp", wd_n - 2)]))
                            if r == 0:
                                w.append(tok[("wd_wb_n", wd_n - 2)])
                        if r == 0 and tp == 0:
                            tok[("wd_ld", wd_n)] = p.dma(
                                "gp", f"wdc{wdi}", wdr[wdi],
                                wd_d[e].rearrange("(b p) d -> p b d", p=128),
                                waits=w)
                            tok[("wd_wb", e)] = p.dma(
                                "scd", f"wdb{wdi}",
                                wff_sc(e)[:, WDOFF:WDOFF + WDLEN],
                                wdr[wdi], waits=[tok[("wd_ld", wd_n)]])
                            tok[("wd_wb_n", wd_n)] = tok[("wd_wb", e)]
                        else:
                            if r == 0:
                                w.append(tok[("wd_wb", e)])
                            elif r == 1:
                                w += tok[("wb_final",)]
                            tok[("wd_ld", wd_n)] = p.dma(
                                "sy", f"wd{wdi}", wdr[wdi],
                                wff_sc(e)[:, WDOFF:WDOFF + WDLEN], waits=w)
                            if r == 0:
                                tok[("wd_wb_n", wd_n)] = tok[("wd_ld", wd_n)]
                        for ib in range(IB):
                            w0 = [tok[("wd_ld", wd_n)]]
                            if ib == 0:
                                w0.append(("gpc", tok[("gu_hv", r, q4, j, IB - 1)]))
                            if j == 0 and ib == 0:
                                if tp == 0:
                                    w0.append(("sc", tok[("cb_ev", r, q4, 3)]))
                                    w0.append(("gpc", tok[("gu_hv", r, q4, 3, IB - 1)]))
                                    if prevq is None:
                                        w0.append(a4ev_last)
                                else:
                                    w0.append(("v", tok[("dn_ev", r, q4, 0)]))
                            for t4 in (tp * 2, tp * 2 + 1):
                                for db in range(DB):
                                    p.mm(pb[(t4 - tp * 2) * 4 + db][:],
                                         hq[:, (j * IB + ib) * 512 + t4 * 128:
                                            (j * IB + ib) * 512 + (t4 + 1) * 128],
                                         wdr[wdi][:, ib * 2048 + db * 512:
                                                  ib * 2048 + (db + 1) * 512],
                                         start=(j == 0 and ib == 0),
                                         stop=(j == 3 and ib == IB - 1),
                                         waits=w0 if (t4 == tp * 2 and db == 0) else ())
                        tok[("dn_grp", wd_n)] = p.mm_inc()
                    tok[("dn_mm", r, q4, tp)] = tok[("dn_grp", wd_n)]
                    w = [("pe", tok[("dn_mm", r, q4, tp)])]
                    if q4 == 0 and tp == 0:
                        if r == 0:
                            w += s4_toks
                        else:
                            w.append(("sc", tok[("fin_g", r - 1, TB4 - 1)]))
                    first = True
                    for t4 in (tp * 2, tp * 2 + 1):
                        for db in range(DB):
                            dst = yac[:, t4 * 2048 + db * 512: t4 * 2048 + (db + 1) * 512]
                            last = (t4 == tp * 2 + 1 and db == DB - 1)
                            if q4 == 0:
                                p.vop("tensor_copy", dst,
                                      pb[(t4 - tp * 2) * 4 + db][:],
                                      waits=w if first else (), inc=last)
                            else:
                                p.vop("tensor_tensor", dst,
                                      pb[(t4 - tp * 2) * 4 + db][:], dst, ALU.add,
                                      waits=w if first else (), inc=last)
                            first = False
                    tok[("dn_ev", r, q4, tp)] = p.c["v"]
                bankfree[0] = bankfree[1] = bankfree[2] = bankfree[3] = \
                    ("v", tok[("dn_ev", r, q4, 1)])
            if r == 0:
                tok[("wb_final",)] = [("wb0", p.c["wb0"]), ("wb1", p.c["wb1"])] + \
                    [(f"wdb{i}", p.c[f"wdb{i}"]) for i in range(6)]
        fin_silu(p, tok, NR_RUN - 1)
        for t4 in range(TB4):
            fin_load(p, tok, NR_RUN - 1, t4)
            fin_prod(p, tok, NR_RUN - 1, t4)
        return

    def fin_silu(p: P, tok, r):
        for t4 in range(TB4):
            w = [("v", tok[("dn_ev", r, nquads - 1, 1)])]
            if r >= 1:
                w.append(("v", tok[("fin_t", r - 1, t4)]))
            tok[("fin_g", r, t4)] = p.sop(
                "activation", env["fb_g"][t4],
                env["yac"][:, t4 * 2048:(t4 + 1) * 2048], AF.Silu, waits=w)

    def fin_load(p: P, tok, r, t4):
        gtb = r * 4 + t4
        w = [("sc", tok[("fin_g", r, t4)])] + env["s4_toks"]
        if t4 >= 1:
            w.append(("v", tok[("fin_t", r, t4 - 1)]))
        elif r >= 1:
            w.append(("v", tok[("fin_t", r - 1, TB4 - 1)]))
        if t4 >= 1:
            w.append(tok[("out_st", r, t4 - 1)])
        elif r >= 1:
            w.append(tok[("out_st", r - 1, TB4 - 1)])
        tok[("fin_sld", r, t4)] = p.dma("gp", "fsg", env["fb_s"],
                                        s_sc[gtb], waits=w)
        tok[("fin_xld", r, t4)] = p.dma(
            "gp", "fsg", env["fb_x"], x_d[gtb * 128:(gtb + 1) * 128, :])

    def fin_prod(p: P, tok, r, t4):
        gtb = r * 4 + t4
        p.vop("tensor_tensor", env["fb_g"][t4], env["fb_s"], env["fb_g"][t4],
              ALU.mult, waits=[tok[("fin_xld", r, t4)],
                               ("sc", tok[("fin_g", r, t4)])],
              drain=True, inc=False)
        tok[("fin_t", r, t4)] = p.vop("tensor_tensor", env["fb_x"],
                                      env["fb_g"][t4], env["fb_x"],
                                      ALU.add, drain=True)
        tok[("out_st", r, t4)] = p.dma(
            "scd", "os", out_d[gtb * 128:(gtb + 1) * 128, :], env["fb_x"],
            waits=[("v", tok[("fin_t", r, t4)])])

    def _at_slice(Aap_, fc, ec):
        o = fc * 2048 + ec * 128
        return Aap_[:, o:o + 128]

    def finish(p: P, tok):
        if dump:
            dbuf = Bf[:, 0:4096]
            dbufb = Bap[:, 8192:12288]
            bw = [(k, v) for k, v in p.c.items() if v > 0]
            first = True
            prev_tt = None
            t2 = None
            tt = None
            for nm in dump:
                if nm == "cT":
                    w = list(bw) if first else []
                    if tt is not None:
                        w.append(tt)
                    if t2 is not None:
                        w.append(("v", t2))
                    t = p.vop("tensor_copy", Bf[0:E, 0:TCORE], cT[:, :],
                              waits=w, drain=True)
                    first = False
                    prev_tt = p.dma("scd", "du2", dump_d[nm], Bf[0:E, 0:TCORE],
                                    waits=[("v", t)])
                    p.wait("scalar", *prev_tt)
                    t2 = t
                    tt = prev_tt
                    continue
                src, dt, shp = dump_specs[nm]
                rows, cols = shp
                nrt = rows // 128
                nct = (cols + 4095) // 4096
                for rt in range(nrt):
                    for ct in range(nct):
                        cw = min(4096, cols - ct * 4096)
                        sview = src[rt * 128:(rt + 1) * 128,
                                    ct * 4096:ct * 4096 + cw]
                        w = list(bw) if first else []
                        if t2 is not None:
                            w.append(("v", t2))
                        if tt is not None:
                            w.append(tt)
                        if dt == BF16:
                            t1 = p.dma("sy", "du1", dbufb[:, 0:cw], sview, waits=w)
                            t2 = p.vop("tensor_copy", dbuf[:, 0:cw],
                                       dbufb[:, 0:cw], waits=[t1],
                                       drain=True)
                        else:
                            t1 = p.dma("sy", "du1", dbuf[:, 0:cw], sview, waits=w)
                            t2 = p.vop("tensor_copy", dbuf[:, 0:cw],
                                       dbuf[:, 0:cw], waits=[t1],
                                       drain=True)
                        first = False
                        tt = p.dma("scd", "du2",
                                   dump_d[nm][rt * 128:(rt + 1) * 128,
                                              ct * 4096:ct * 4096 + cw],
                                   dbuf[:, 0:cw], waits=[("v", t2)])
                        p.wait("scalar", *tt)
                        prev_tt = tt
        for k, v in p.c.items():
            if v > 0:
                p.wait("sync", k, v)

    blk = ctx.enter_context(nc.Block())

    @blk.sync
    def _(eng):
        walk(P("sync", eng, sems))

    @blk.gpsimd
    def _(eng):
        walk(P("gpsimd", eng, sems))

    @blk.tensor
    def _(eng):
        walk(P("tensor", eng, sems))

    @blk.vector
    def _(eng):
        walk(P("vector", eng, sems))

    @blk.scalar
    def _(eng):
        walk(P("scalar", eng, sems))

    ctx.close()
    return nc


def make_consts():
    idf = np.eye(128, dtype=np.float32)
    idb = np.eye(128, dtype=ml_dtypes.bfloat16)
    oh = np.zeros((E, E * 128), dtype=ml_dtypes.bfloat16)
    for e in range(E):
        oh[e, e * 128:(e + 1) * 128] = 1
    return idf, idb, oh


def make_in_maps(L0_attn_x, L0_attn_Wq, L0_attn_Wk, L0_attn_Wv, L0_attn_Wo,
                 L0_ffn_gate_w, L0_ffn_w_gate, L0_ffn_w_up, L0_ffn_w_down):
    x = np.asarray(L0_attn_x, dtype=np.float32).reshape(-1, D)
    idf, idb, oh = make_consts()
    shared = {
        "wq": np.ascontiguousarray(np.asarray(L0_attn_Wq, np.float32)),
        "wk": np.ascontiguousarray(np.asarray(L0_attn_Wk, np.float32)),
        "wv": np.ascontiguousarray(np.asarray(L0_attn_Wv, np.float32)),
        "wo": np.ascontiguousarray(np.asarray(L0_attn_Wo, np.float32)),
        "gw": np.ascontiguousarray(np.asarray(L0_ffn_gate_w, np.float32)),
        "wg": np.ascontiguousarray(np.asarray(L0_ffn_w_gate, np.float32)),
        "wu": np.ascontiguousarray(np.asarray(L0_ffn_w_up, np.float32)),
        "wd": np.ascontiguousarray(np.asarray(L0_ffn_w_down, np.float32)),
        "idf": idf, "idb": idb, "oh": oh,
    }
    return [dict(shared, x=np.ascontiguousarray(x[c * TCORE:(c + 1) * TCORE]))
            for c in range(8)]


def kernel(L0_attn_x, L0_attn_Wq, L0_attn_Wk, L0_attn_Wv, L0_attn_Wo,
           L0_ffn_gate_w, L0_ffn_w_gate, L0_ffn_w_up, L0_ffn_w_down):
    in_maps = make_in_maps(L0_attn_x, L0_attn_Wq, L0_attn_Wk, L0_attn_Wv,
                           L0_attn_Wo, L0_ffn_gate_w, L0_ffn_w_gate,
                           L0_ffn_w_up, L0_ffn_w_down)
    nc = build_nc()
    res = run_bass_kernel_spmd(nc, in_maps, core_ids=list(range(8)))
    out = np.concatenate([res.results[c]["out"] for c in range(8)], axis=0)
    return out.reshape(2, 16384, D)


# version 06
# speedup vs baseline: 1.0215x; 1.0215x over previous
import numpy as np
import ml_dtypes
import concourse.bass as bass
import concourse.mybir as mybir
from concourse.bass_utils import run_bass_kernel_spmd

F32 = mybir.dt.float32
BF16 = mybir.dt.bfloat16
AF = mybir.ActivationFunctionType
ALU = mybir.AluOpType
AX = mybir.AxisListType

D = 2048
TCORE = 4096
NT = TCORE // 128
DC = 16
E = 32
I = 768
IB = 6
RNG = 512
NRANGE = TCORE // RNG
TB4 = RNG // 128
DB = 4
NQUAD = 8
RG = [[0, 1, 2, 3], [4, 5, 6, 7]]

WHALF = DC * 384
WGU_H = 2 * WHALF
WDOFF = 2 * WGU_H
WDLEN = IB * DB * 512
WCOLS = WDOFF + WDLEN


class P:
    def __init__(self, name, eng, sems):
        self.name = name
        self.eng = eng
        self.sems = sems
        self.c = {k: 0 for k in sems}
        self.last_mm = None
        self.maxw = {}

    def _emit_waits(self, waits):
        for (wq, wv) in waits:
            assert wv <= self.c[wq], (self.name, wq, wv, self.c[wq])
            if self.name == "tensor" and wq == "pe":
                continue
            if wv > self.maxw.get(wq, 0):
                self.maxw[wq] = wv
                self.eng.wait_ge(self.sems[wq], wv)

    def wait(self, who, q, val):
        assert val <= self.c[q], (who, q, val, self.c[q])
        if self.name == who:
            self._emit_waits([(q, val)])

    def dma(self, q, slot, out, in_, waits=(), **kw):
        eng_for = {"sy": "sync", "gp": "gpsimd", "scd": "scalar"}[q]
        prev = self.c[slot]
        self.c[slot] += 16
        if self.name == eng_for:
            self._emit_waits(waits)
            if q == "gp" and prev > 0:
                self._emit_waits([(slot, prev)])
            self.eng.dma_start(out=out, in_=in_, **kw).then_inc(self.sems[slot], 16)
        return (slot, self.c[slot])

    def mm(self, out, lhsT, rhs, start, stop, waits=()):
        if self.name == "tensor":
            self._emit_waits(waits)
            self.last_mm = self.eng.matmul(out, lhsT, rhs, start=start, stop=stop)

    def mm_inc(self):
        self.c["pe"] += 1
        if self.name == "tensor":
            self.last_mm.then_inc(self.sems["pe"], 1)
        return self.c["pe"]

    def transpose(self, out, in_, ident, waits=()):
        if self.name == "tensor":
            self._emit_waits(waits)
            self.last_mm = self.eng.transpose(out, in_, ident)
        return self.mm_inc()

    def vop(self, fn, *a, waits=(), drain=False, inc=True, **kw):
        if inc:
            self.c["v"] += 1
        if self.name == "vector":
            self._emit_waits(waits)
            if drain:
                self.eng.drain()
            ins = getattr(self.eng, fn)(*a, **kw)
            if inc:
                ins.then_inc(self.sems["v"], 1)
        return self.c["v"]

    def sop(self, fn, *a, waits=(), drain=False, inc=True, **kw):
        if inc:
            self.c["sc"] += 1
        if self.name == "scalar":
            self._emit_waits(waits)
            if drain:
                self.eng.drain()
            ins = getattr(self.eng, fn)(*a, **kw)
            if inc:
                ins.then_inc(self.sems["sc"], 1)
        return self.c["sc"]

    def gop(self, fn, *a, waits=(), inc=True, **kw):
        if inc:
            self.c["gpc"] += 1
        if self.name == "gpsimd":
            self._emit_waits(waits)
            ins = getattr(self.eng, fn)(*a, **kw)
            if inc:
                ins.then_inc(self.sems["gpc"], 1)
        return self.c["gpc"]

    def cc(self, kind, op, ins_, outs_, waits=()):
        self.c["cc"] += 1
        if self.name == "gpsimd":
            self._emit_waits(waits)
            self.eng.collective_compute(
                kind, op, replica_groups=RG, ins=ins_, outs=outs_
            ).then_inc(self.sems["cc"], 1)
        return self.c["cc"]


def build_nc(upto="full", dump=(), nranges=NRANGE, skip_attn=False, detect_races=True, nquads=NQUAD):
    nc = bass.Bass("TRN2", target_bir_lowering=False, num_devices=8,
                   detect_race_conditions=detect_races)

    x_d = nc.dram_tensor("x", [TCORE, D], F32, kind="ExternalInput").ap()
    wq_d = nc.dram_tensor("wq", [D, D], F32, kind="ExternalInput").ap()
    wk_d = nc.dram_tensor("wk", [D, D], F32, kind="ExternalInput").ap()
    wv_d = nc.dram_tensor("wv", [D, D], F32, kind="ExternalInput").ap()
    wo_d = nc.dram_tensor("wo", [D, D], F32, kind="ExternalInput").ap()
    gw_d = nc.dram_tensor("gw", [E, D], F32, kind="ExternalInput").ap()
    wg_d = nc.dram_tensor("wg", [E, D, I], F32, kind="ExternalInput").ap()
    wu_d = nc.dram_tensor("wu", [E, D, I], F32, kind="ExternalInput").ap()
    wd_d = nc.dram_tensor("wd", [E, I, D], F32, kind="ExternalInput").ap()
    idf_d = nc.dram_tensor("idf", [128, 128], F32, kind="ExternalInput").ap()
    idb_d = nc.dram_tensor("idb", [128, 128], BF16, kind="ExternalInput").ap()
    oh_d = nc.dram_tensor("oh", [E, E * 128], BF16, kind="ExternalInput").ap()
    out_d = nc.dram_tensor("out", [TCORE, D], F32, kind="ExternalOutput").ap()

    xn_sc = nc.dram_tensor("xn_sc", [NT, 128, D], BF16).ap()
    q_sc = nc.dram_tensor("q_sc", [128, NT, D], BF16).ap()
    k_sc = nc.dram_tensor("k_sc", [128, NT, D], BF16).ap()
    v_sc = nc.dram_tensor("v_sc", [128, DC, TCORE], BF16).ap()
    att_sc = nc.dram_tensor("att_sc", [128, DC, TCORE], BF16).ap()
    s_sc = nc.dram_tensor("s_sc", [NT, 128, D], F32).ap()
    scc_in = nc.dram_tensor("scc_in", [D, D], F32).ap()
    scc_out = nc.dram_tensor("scc_out", [512, D], F32).ap()
    agg_in = nc.dram_tensor("agg_in", [512, D], BF16).ap()
    agg_out = nc.dram_tensor("agg_out", [D, D], BF16).ap()
    wff_a = nc.dram_tensor("wff_a", [E // 2, 128, WCOLS], BF16).ap()
    wff_b = nc.dram_tensor("wff_b", [E // 2, 128, WCOLS], BF16).ap()
    wff_sc = lambda e: (wff_a[e] if e < E // 2 else wff_b[e - E // 2])

    dump_d = {}
    dump_specs = {
        "xn_sc": (xn_sc.rearrange("a p d -> (a p) d"), BF16, [NT * 128, D]),
        "q_sc": (q_sc.rearrange("p c t -> p (c t)"), BF16, [128, NT * D]),
        "k_sc": (k_sc.rearrange("p c t -> p (c t)"), BF16, [128, NT * D]),
        "v_sc": (v_sc.rearrange("p c t -> p (c t)"), BF16, [128, DC * TCORE]),
        "att_sc": (att_sc.rearrange("p c t -> p (c t)"), BF16, [128, DC * TCORE]),
        "s_sc": (s_sc.rearrange("a p d -> (a p) d"), F32, [NT * 128, D]),
        "scc_in": (scc_in, F32, [D, D]),
        "agg_out": (agg_out, BF16, [D, D]),
        "wff0": (wff_sc(0), BF16, [128, WCOLS]),
    }
    for nm in dump:
        dump_d[nm] = nc.dram_tensor("dump_" + nm, dump_specs[nm][2] if nm != "cT"
                                    else [E, TCORE], F32, kind="ExternalOutput").ap()

    from contextlib import ExitStack
    ctx = ExitStack()
    sb = lambda n, s, d: ctx.enter_context(nc.sbuf_tensor(n, s, d))
    psa = lambda n, s, d: ctx.enter_context(nc.psum_tensor(n, s, d))
    mksem = lambda n: ctx.enter_context(nc.semaphore(n))

    A = sb("A", [128, 61440], BF16)
    Aap = A.ap()
    B = sb("B", [128, 30720], BF16)
    Bap = B.ap()
    Bf = Bap.bitcast(F32)
    C = sb("C", [128, 6144], BF16)
    Cap = C.ap()
    Cf = Cap.bitcast(F32)

    identf = sb("identf", [128, 128], F32)
    identb = sb("identb", [128, 128], BF16)
    gwT = sb("gwT", [128, DC * E], F32)
    cT = sb("cT", [E, TCORE], BF16)
    oh_b = sb("oh_b", [E, 512], BF16)
    gw_sb = None
    stat = sb("stat", [128, 16], F32)
    ls = sb("ls", [128, E], F32)
    elb = sb("elb", [128, E], F32)
    vmax8 = sb("vmax8", [128, 8], F32)
    zap = sb("zap", [128, E], F32)
    selm = sb("selm", [128, E], F32)
    e8 = sb("e8", [128, 8], F32)
    ctile = sb("ctile", [128, E], F32)

    pb = [psa(f"pb{i}", [128, 512], F32) for i in range(8)]
    pbb = [pb[i].ap().bitcast(BF16) for i in range(8)]

    DMA_SLOTS = ["cst", "xl0", "xl1", "xs0", "xs1", "ax0", "ax1", "aq0", "aq1",
                 "wc", "bx0", "bx1", "bv0", "bv1", "kk", "qq0", "qq1",
                 "ss0", "ss1", "sa0", "sa1", "sx0", "sx1", "ag0", "ag1", "ab0", "ab1", "vv0", "vv1", "as0",
                 "as1", "ac0", "ac1", "s40", "s41", "xr0", "xr1", "oh1", "wg0",
                 "wg1", "wd0", "wd1", "wd2", "wd3", "wd4", "wd5",
                 "wgc0", "wgc1", "wgc2", "wgc3", "wdc0", "wdc1", "wdc2", "wdc3", "wb0", "wb1", "wdb0", "wdb1", "wdb2", "wdb3", "wdb4", "wdb5", "fs", "fsg", "os", "du1", "du2"]
    sems = {k: mksem(k) for k in ["pe", "v", "sc", "gpc", "cc"] + DMA_SLOTS}

    SUM, NM, NM2, VAR, SSQ, LNV, RSTD, BIAS, NM8, S8, RS8, EPS = range(12)
    st = lambda i: stat[:, i:i + 1]

    env = {}
    gw_sb = Bf[0:E, 13312:15360]

    def walk(p: P):
        tok = {}

        t_eps = p.vop("memset", st(EPS), 1e-5)
        t_idf = p.dma("sy", "cst", identf[:], idf_d)
        t_idb = p.dma("sy", "cst", identb[:], idb_d)
        t_gw = p.dma("sy", "cst", gw_sb[:], gw_d)
        ev_gwT = 0
        for dc in range(DC):
            bank = dc % 2
            w = [t_gw, t_idf]
            if dc >= 2:
                w.append(("v", ev_gwT))
            tk = p.transpose(pb[bank][:, 0:E],
                             gw_sb[:, dc * 128:(dc + 1) * 128],
                             identf[0:E, 0:E], waits=w)
            ev_gwT = p.vop("tensor_copy", gwT[:, dc * E:(dc + 1) * E],
                           pb[bank][:, 0:E], waits=[("pe", tk)])

        xbuf = [Bf[:, 0:2048], Bf[:, 2048:4096]]
        xnf = [Bf[:, 4096:6144], Bf[:, 6144:8192]]
        sqb = Bf[:, 8192:10240]
        xnTf = [Bf[:, 10240:12288], Bf[:, 12288:14336]]
        xnTb = [Cap[:, 0:2048], Cap[:, 2048:4096]]

        for tb in range(NT):
            pg = tb % 2
            w = []
            if tb >= 2:
                w += [("v", tok[("xnf", tb - 2)]), ("sc", tok[("ssq", tb - 2)])]
            t_x = p.dma("sy", f"xl{pg}", xbuf[pg],
                        x_d[tb * 128:(tb + 1) * 128, :], waits=w)
            tok[("rsum", tb)] = p.vop("reduce_sum", st(SUM), xbuf[pg],
                                      waits=[t_x], axis=AX.X)
            w = [t_x]
            if tb >= 1:
                w.append(("sc", tok[("ssq", tb - 1)]))
            tok[("ssq", tb)] = p.sop("activation", sqb, xbuf[pg], AF.Square,
                                     waits=w, accum_out=st(SSQ))
            p.vop("tensor_scalar_mul", st(NM), st(SUM), -1.0 / D, drain=True,
                  inc=False)
            p.vop("tensor_mul", st(NM2), st(NM), st(NM), drain=True, inc=False)
            p.vop("tensor_scalar_mul", st(VAR), st(SSQ), 1.0 / D,
                  waits=[("sc", tok[("ssq", tb)])], drain=True, inc=False)
            tok[("var", tb)] = p.vop("tensor_sub", st(VAR), st(VAR), st(NM2),
                                     drain=True)
            p.sop("activation", st(LNV), st(VAR), AF.Ln, bias=st(EPS),
                  waits=[("v", tok[("var", tb)])], inc=False)
            tok[("rstd", tb)] = p.sop("activation", st(RSTD), st(LNV), AF.Exp,
                                      scale=-0.5, drain=True)
            w = [("sc", tok[("rstd", tb)])]
            if tb >= 2:
                w.append(("pe", tok[("tr", tb - 2)]))
            tok[("xnf", tb)] = p.vop("tensor_scalar", xnf[pg], xbuf[pg],
                                     st(NM), st(RSTD), ALU.add, ALU.mult,
                                     waits=w, drain=True)
            for dc in range(DC):
                bank = dc % 2
                w = [("v", tok[("xnf", tb)])]
                key = ("trev", tb, dc - 2) if dc >= 2 else \
                      (("trev", tb - 1, DC - 2 + dc) if tb >= 1 else None)
                if key:
                    w.append(("v", tok[key]))
                tk = p.transpose(pb[bank][:, 0:128],
                                 xnf[pg][:, dc * 128:(dc + 1) * 128],
                                 identf[:], waits=w)
                if dc == DC - 1:
                    tok[("tr", tb)] = tk
                w = [("pe", tk)]
                if dc == 0 and tb >= 2:
                    w += [("pe", tok[("rt", tb - 2)]), ("sc", tok[("xnTb", tb - 2)])]
                tok[("trev", tb, dc)] = p.vop(
                    "tensor_copy", xnTf[pg][:, dc * 128:(dc + 1) * 128],
                    pb[bank][:, 0:128], waits=w)
            w = [("v", tok[("trev", tb, DC - 1)])]
            if tb >= 2:
                w.append(tok[("xnst", tb - 2)])
            tok[("xnTb", tb)] = p.sop("copy", xnTb[pg], xnTf[pg], waits=w)
            tok[("xnst", tb)] = p.dma(
                "sy", f"xs{pg}", xn_sc[tb], xnTb[pg],
                waits=[("sc", tok[("xnTb", tb)])])
            for dc in range(DC):
                w = [("v", tok[("trev", tb, dc)])]
                if dc == 0 and tb >= 1:
                    w.append(("v", tok[("lg", tb - 1)]))
                p.mm(pb[2][:, 0:E], xnTf[pg][:, dc * 128:(dc + 1) * 128],
                     gwT[:, dc * E:(dc + 1) * E],
                     start=(dc == 0), stop=(dc == DC - 1), waits=w)
            tok[("rt", tb)] = p.mm_inc()
            w = [("pe", tok[("rt", tb)])]
            if tb >= 1:
                w.append(("sc", tok[("el", tb - 1)]))
            tok[("lg", tb)] = p.vop("tensor_copy", ls[:], pb[2][:, 0:E],
                                    waits=w, drain=True)
            p.vop("max", drain=True, inc=False, out=vmax8[:], in_=ls[:])
            p.vop("match_replace", drain=True, inc=False, out=zap[:],
                  in_to_replace=vmax8[:], in_values=ls[:], imm_value=-1e30)
            p.vop("tensor_tensor", selm[:], ls[:], zap[:], ALU.not_equal,
                  drain=True, inc=False)
            tok[("nm8", tb)] = p.vop("tensor_scalar_mul", st(NM8), vmax8[:, 0:1],
                                     -1.0, drain=True)
            tok[("e8", tb)] = p.sop("activation", e8[:], vmax8[:], AF.Exp,
                                    bias=st(NM8), waits=[("v", tok[("nm8", tb)])])
            p.vop("reduce_sum", st(S8), e8[:], axis=AX.X,
                  waits=[("sc", tok[("e8", tb)])], drain=True, inc=False)
            tok[("rs8", tb)] = p.vop("reciprocal", st(RS8), st(S8), drain=True)
            tok[("el", tb)] = p.sop("activation", elb[:], ls[:], AF.Exp,
                                    bias=st(NM8))
            w = [("sc", tok[("el", tb)]), ("v", tok[("rs8", tb)])]
            if tb >= 1:
                w.append(("v", tok[("ctev", tb - 1)]))
            tok[("c", tb)] = p.vop("scalar_tensor_tensor", waits=w, drain=True,
                                   out=ctile[:], in0=elb[:], scalar=st(RS8),
                                   in1=selm[:], op0=ALU.mult, op1=ALU.mult)
            tk = p.transpose(pb[3][0:E, 0:128], ctile[:], identf[:],
                             waits=[("v", tok[("c", tb)])])
            tok[("ctev", tb)] = p.vop("tensor_copy", cT[:, tb * 128:(tb + 1) * 128],
                                      pb[3][0:E, 0:128], waits=[("pe", tk)])

        env.update(locals())
        if upto == "prologue":
            finish(p, tok)
            return

        if skip_attn:
            tok[("a4_mm", NT - 1)] = tok[("rt", NT - 1)]
            tok[("a4_ev", NT - 1)] = tok[("ctev", NT - 1)]
            tok[("a4_st", NT - 1)] = tok[("xnst", NT - 1)]
            tok[("a4_st", NT - 2)] = tok[("xnst", NT - 2)]
            ffn(p, tok)
            finish(p, tok)
            return

        wqh = Aap[:, 0:16384]
        wkh = Aap[:, 16384:32768]
        xnt = [Aap[:, 32768 + i * 2048: 32768 + (i + 1) * 2048] for i in range(2)]
        qkst = [Cap[:, i * 1024:(i + 1) * 1024] for i in range(4)]

        for half in range(2):
            tw = []
            for (wsrc, dstbuf) in ((wq_d, wqh), (wk_d, wkh)):
                for piece in range(2):
                    src = wsrc.rearrange("(c p) f -> p c f", p=128)[
                        :, piece * 8:(piece + 1) * 8,
                        half * 1024:(half + 1) * 1024]
                    dgt = dstbuf.rearrange("p (c f) -> p c f", f=1024)[
                        :, piece * 8:(piece + 1) * 8, :]
                    w = []
                    if half == 1:
                        w.append(("pe", tok[("a1a_mm", 0, NT - 1)]))
                    tw.append(p.dma("gp", "wc", dgt, src, waits=w))
            t_w = tw[-1]
            for tb in range(NT):
                pg = tb % 2
                w = [tok[("xnst", tb)]]
                prev = ("a1a_mm", half, tb - 2) if tb >= 2 else \
                       (("a1a_mm", half - 1, NT - 2 + tb) if half >= 1 else None)
                if prev:
                    w.append(("pe", tok[prev]))
                t_xn = p.dma("sy", f"ax{pg}", xnt[pg], xn_sc[tb], waits=w)
                bs = pg * 4
                for dc in range(DC):
                    w = [t_w, t_xn]
                    if dc == 0:
                        if half == 0 and tb == 0:
                            w.append(("v", tok[("ctev", NT - 1)]))
                        prev = ("a1a_ev", half, tb - 2) if tb >= 2 else \
                               (("a1a_ev", half - 1, NT - 2 + tb) if half >= 1 else None)
                        if prev:
                            w.append(("v", tok[prev]))
                    lhs = xnt[pg][:, dc * 128:(dc + 1) * 128]
                    first = True
                    for (wbuf, bo) in ((wqh, 0), (wkh, 2)):
                        for fb in range(2):
                            p.mm(pb[bs + bo + fb][:], lhs,
                                 wbuf[:, dc * 1024 + fb * 512: dc * 1024 + (fb + 1) * 512],
                                 start=(dc == 0), stop=(dc == DC - 1),
                                 waits=w if (dc == 0 and first) else ())
                            first = False
                tok[("a1a_mm", half, tb)] = p.mm_inc()
                w = [("pe", tok[("a1a_mm", half, tb)])]
                prev = ("a1a_st", half, tb - 2) if tb >= 2 else \
                       (("a1a_st", half - 1, NT - 2 + tb) if half >= 1 else None)
                if prev:
                    w.append(tok[prev])
                if half == 0 and tb < 2:
                    w += [tok[("xnst", NT - 1)], tok[("xnst", NT - 2)]]
                p.vop("tensor_copy", qkst[pg * 2][:, 0:512], pb[bs + 0][:],
                      waits=w, inc=False)
                p.vop("tensor_copy", qkst[pg * 2][:, 512:1024], pb[bs + 1][:],
                      inc=False)
                p.vop("tensor_copy", qkst[pg * 2 + 1][:, 0:512], pb[bs + 2][:],
                      inc=False)
                tok[("a1a_ev", half, tb)] = p.vop(
                    "tensor_copy", qkst[pg * 2 + 1][:, 512:1024], pb[bs + 3][:])
                p.dma("scd", f"aq{pg}",
                      q_sc[:, tb, half * 1024:(half + 1) * 1024],
                      qkst[pg * 2], waits=[("v", tok[("a1a_ev", half, tb)])])
                tok[("a1a_st", half, tb)] = p.dma(
                    "scd", f"aq{pg}",
                    k_sc[:, tb, half * 1024:(half + 1) * 1024],
                    qkst[pg * 2 + 1])

        if upto == "a1a":
            finish(p, tok)
            return

        wvh = Aap[:, 0:16384]
        xn5 = [Aap[:, 16384 + i * 8192: 16384 + (i + 1) * 8192] for i in range(2)]
        vst = [Cap[:, i * 512:(i + 1) * 512] for i in range(8)]
        for half in range(2):
            tw = []
            for piece in range(2):
                src = wv_d.rearrange("(c p) (b f) -> p c b f", p=128, f=128)[
                    :, piece * 8:(piece + 1) * 8, half * 8:(half + 1) * 8, :]
                dgt = wvh.rearrange("p (c b f) -> p c b f", b=8, f=128)[
                    :, piece * 8:(piece + 1) * 8, :, :]
                w = []
                if half == 0:
                    w.append(("pe", tok[("a1a_mm", 1, NT - 1)]))
                else:
                    w.append(("pe", tok[("a1b_mm", 0, 7)]))
                tw.append(p.dma("gp", "wc", dgt, src, waits=w))
            t_w = tw[-1]
            for t5 in range(8):
                pg = t5 % 2
                w = [tok[("xnst", NT - 1)], tok[("xnst", NT - 2)]]
                prev = ("a1b_mm", half, t5 - 2) if t5 >= 2 else \
                       (("a1b_mm", half - 1, 6 + t5) if half >= 1 else None)
                if prev:
                    w.append(("pe", tok[prev]))
                if half == 0 and t5 < 2:
                    w.append(("pe", tok[("a1a_mm", 1, NT - 1)]))
                for i4 in range(4):
                    t_xn = p.dma(
                        "sy", f"bx{pg}",
                        xn5[pg].rearrange("p (c t) -> p c t", t=512)[
                            :, :, i4 * 128:(i4 + 1) * 128],
                        xn_sc[t5 * 4 + i4].rearrange("p (c t) -> p c t", t=128),
                        waits=w if i4 == 0 else ())
                for dc in range(DC):
                    for fb in range(8):
                        w = [t_w, t_xn]
                        if dc == 0:
                            key = ("a1b_ev", half, t5 - 1, fb) if t5 >= 1 else \
                                  (("a1b_ev", half - 1, 7, fb) if half >= 1 else None)
                            if key:
                                w.append(("v", tok[key]))
                            if half == 0 and t5 == 0:
                                w.append(("v", tok[("a1a_ev", 1, NT - 1)]))
                        p.mm(pb[fb][:],
                             wvh[:, (dc * 8 + fb) * 128:(dc * 8 + fb + 1) * 128],
                             xn5[pg][:, dc * 512:(dc + 1) * 512],
                             start=(dc == 0), stop=(dc == DC - 1),
                             waits=w if dc == 0 else ())
                tok[("a1b_mm", half, t5)] = p.mm_inc()
                for fb in range(8):
                    w = [("pe", tok[("a1b_mm", half, t5)])]
                    key = ("a1b_st", half, t5 - 1, 7) if t5 >= 1 else \
                          (("a1b_st", half - 1, 7, 7) if half >= 1 else None)
                    if key:
                        w.append(tok[key])
                    if half == 0 and t5 == 0:
                        w += [tok[("a1a_st", 1, NT - 1)], tok[("a1a_st", 1, NT - 2)]]
                    tok[("a1b_ev", half, t5, fb)] = p.vop(
                        "tensor_copy", vst[fb], pb[fb][:], waits=w)
                    tok[("a1b_st", half, t5, fb)] = p.dma(
                        "sy", f"bv{pg}",
                        v_sc[:, half * 8 + fb, t5 * 512:(t5 + 1) * 512],
                        vst[fb], waits=[("v", tok[("a1b_ev", half, t5, fb)])])

        if upto == "a1b":
            finish(p, tok)
            return

        kb = Aap[:, 0:32768]
        qb = [Aap[:, 32768 + i * 8192: 32768 + (i + 1) * 8192] for i in range(2)]
        sst = [Bf[:, i * 512:(i + 1) * 512] for i in range(8)]
        for kh in range(2):
            w = [tok[("a1b_st", 1, 7, 7)], tok[("a1b_st", 1, 6, 7)],
                 tok[("a1a_st", 1, NT - 1)], tok[("a1a_st", 1, NT - 2)]]
            if kh == 1:
                w.append(("pe", tok[("sc_mm", 0, 3, 3)]))
            t_k = p.dma("sy", "kk", kb.rearrange("p (c f) -> p c f", f=2048),
                        k_sc[:, kh * 16:(kh + 1) * 16, :], waits=w)
            for eq in range(4):
                pg = eq % 2
                w = [tok[("a1a_st", 1, NT - 1)], tok[("a1a_st", 1, NT - 2)]]
                prev = ("sc_mm", kh, eq - 2, 3) if eq >= 2 else \
                       (("sc_mm", kh - 1, 2 + eq, 3) if kh >= 1 else None)
                if prev:
                    w.append(("pe", tok[prev]))
                t_q = p.dma("sy", f"qq{pg}",
                            qb[pg].rearrange("p (c e) -> p c e", e=512),
                            q_sc[:, kh * 16:(kh + 1) * 16, eq * 512:(eq + 1) * 512],
                            waits=w)
                for fb in range(4):
                    bs = ((eq * 4 + fb) % 2) * 4
                    for tc in range(16):
                        w = [t_k, t_q]
                        if tc == 0:
                            key = ("sc_ev", kh, eq, fb - 2) if fb >= 2 else \
                                  (("sc_ev", kh, eq - 1, 2 + fb) if eq >= 1 else
                                   (("sc_ev", kh - 1, 3, 2 + fb) if kh >= 1 else None))
                            if key:
                                w.append(("v", tok[key]))
                            elif fb < 2:
                                w.append(("v", tok[("a1b_ev", 1, 7, 7)]))
                        for eb in range(4):
                            p.mm(pb[bs + eb][:],
                                 qb[pg][:, tc * 512 + eb * 128: tc * 512 + (eb + 1) * 128],
                                 kb[:, tc * 2048 + fb * 512: tc * 2048 + (fb + 1) * 512],
                                 start=(tc == 0), stop=(tc == 15),
                                 waits=w if (tc == 0 and eb == 0) else ())
                    tok[("sc_mm", kh, eq, fb)] = p.mm_inc()
                    w = [("pe", tok[("sc_mm", kh, eq, fb)])]
                    key = ("sc_fin", kh, eq, fb - 2) if fb >= 2 else \
                          (("sc_fin", kh, eq - 1, 2 + fb) if eq >= 1 else
                           (("sc_fin", kh - 1, 3, 2 + fb) if kh >= 1 else None))
                    if key:
                        w.append(tok[key])
                    if kh == 0 and eq == 0 and fb < 2:
                        w += [tok[("a1b_st", 1, 7, 7)], tok[("a1b_st", 1, 6, 7)]]
                    gpar = (eq * 4 + fb) % 2
                    for eb in range(4):
                        p.vop("tensor_copy", sst[gpar * 4 + eb], pb[bs + eb][:],
                              waits=w if eb == 0 else (), inc=(eb == 3))
                    tok[("sc_ev", kh, eq, fb)] = p.c["v"]
                    for eb in range(4):
                        dst = scc_in[(eq * 4 + eb) * 128:(eq * 4 + eb + 1) * 128,
                                     fb * 512:(fb + 1) * 512]
                        wv = [("v", tok[("sc_ev", kh, eq, fb)])]
                        if kh == 0:
                            tt = p.dma("sy", f"ss{gpar}", dst,
                                       sst[gpar * 4 + eb], waits=wv)
                            tok[("sc_fin", kh, eq, fb)] = tt
                        else:
                            wv += tok[("ss_all")]
                            tt = p.dma("gp", f"sa{gpar}", dst,
                                       sst[gpar * 4 + eb], waits=wv,
                                       accum_op=ALU.add)
                            tok[("sc_fin", kh, eq, fb)] = tt
            if kh == 0:
                tok[("ss_all")] = [("ss0", p.c["ss0"]), ("ss1", p.c["ss1"])]

        if upto == "a2":
            finish(p, tok)
            return

        t_rs = p.cc("ReduceScatter", ALU.add, [scc_in], [scc_out],
                    waits=[("ss0", p.c["ss0"]), ("ss1", p.c["ss1"]),
                           ("sa0", p.c["sa0"]), ("sa1", p.c["sa1"])])
        smx = [Bf[:, i * 2048:(i + 1) * 2048] for i in range(2)]
        smy = [Bf[:, 4096 + i * 2048: 4096 + (i + 1) * 2048] for i in range(2)]
        smo = [Bap[:, 16384 + i * 2048: 16384 + (i + 1) * 2048] for i in range(2)]
        for i in range(4):
            pg = i % 2
            w = [("cc", t_rs)]
            if i >= 2:
                w.append(tok[("ag_st", i - 2)])
            t_s = p.dma("sy", f"sx{pg}", smx[pg],
                        scc_out[i * 128:(i + 1) * 128, :], waits=w)
            p.vop("reduce_max", st(SUM), smx[pg], axis=AX.X,
                  waits=[t_s], drain=True, inc=False)
            tok[("smm", i)] = p.vop("tensor_scalar_mul", st(NM), st(SUM), -0.125,
                                    drain=True)
            tok[("sme", i)] = p.sop("activation", smy[pg], smx[pg], AF.Exp,
                                    scale=0.125, bias=st(NM), accum_out=st(S8),
                                    waits=[("v", tok[("smm", i)])])
            tok[("smr", i)] = p.vop("reciprocal", st(RS8), st(S8),
                                    waits=[("sc", tok[("sme", i)])])
            tok[("smA", i)] = p.vop("tensor_scalar_mul", smo[pg], smy[pg], st(RS8),
                                    drain=True)
            tok[("ag_st", i)] = p.dma("scd", f"ag{pg}",
                                      agg_in[i * 128:(i + 1) * 128, :],
                                      smo[pg], waits=[("v", tok[("smA", i)])])
        t_ag = p.cc("AllGather", ALU.bypass, [agg_in], [agg_out],
                    waits=[("ag0", p.c["ag0"]), ("ag1", p.c["ag1"])])

        ATs = Aap[:, 0:32768]
        abuf = [Aap[:, 32768 + i * 2048: 32768 + (i + 1) * 2048] for i in range(2)]
        for ec in range(DC):
            pg = ec % 2
            w = [("cc", t_ag)]
            if ec >= 2:
                w.append(("v", tok[("at_ev", ec - 2, DC - 1)]))
            t_a = p.dma("sy", f"ab{pg}", abuf[pg],
                        agg_out[ec * 128:(ec + 1) * 128, :], waits=w)
            for fc in range(DC):
                bank = fc % 2
                w = [t_a, t_idb]
                key = ("at_ev", ec, fc - 2) if fc >= 2 else \
                      (("at_ev", ec - 1, DC - 2 + fc) if ec >= 1 else None)
                if key:
                    w.append(("v", tok[key]))
                tk = p.transpose(pbb[bank][:, 0:128],
                                 abuf[pg][:, fc * 128:(fc + 1) * 128], identb[:],
                                 waits=w)
                tok[("at_ev", ec, fc)] = p.vop(
                    "tensor_copy", _at_slice(Aap, fc, ec),
                    pbb[bank][:, 0:128], waits=[("pe", tk)])

        vhb = Aap[:, 36864: 36864 + 16384]
        atst = [Cap[:, i * 512:(i + 1) * 512] for i in range(8)]
        t_at_last = tok[("at_ev", DC - 1, DC - 1)]
        for tq in range(4):
            w = [tok[("a1b_st", 1, 7, 7)], tok[("a1b_st", 1, 6, 7)],
                 ("v", t_at_last)]
            if tq >= 1:
                w.append(("pe", tok[("a3_mm", tq - 1, DC - 1)]))
            t_v = p.dma("sy", f"vv{tq % 2}",
                        vhb.rearrange("p (c t) -> p c t", t=1024),
                        v_sc[:, :, tq * 1024:(tq + 1) * 1024], waits=w)
            for eb in range(DC):
                bs = (eb % 2) * 4
                for fc in range(DC):
                    w = [t_v]
                    if fc == 0:
                        key = ("a3_ev", tq, eb - 2) if eb >= 2 else \
                              (("a3_ev", tq - 1, DC - 2 + eb) if tq >= 1 else None)
                        if key:
                            w.append(("v", tok[key]))
                    for t4 in range(2):
                        p.mm(pb[bs + t4][:],
                             _at_slice(Aap, fc, eb),
                             vhb[:, fc * 1024 + t4 * 512: fc * 1024 + (t4 + 1) * 512],
                             start=(fc == 0), stop=(fc == DC - 1),
                             waits=w if (fc == 0 and t4 == 0) else ())
                tok[("a3_mm", tq, eb)] = p.mm_inc()
                w = [("pe", tok[("a3_mm", tq, eb)])]
                key = ("a3_stl", tq, eb - 2) if eb >= 2 else \
                      (("a3_stl", tq - 1, DC - 2 + eb) if tq >= 1 else None)
                if key:
                    w.append(tok[key])
                for t4 in range(2):
                    p.vop("tensor_copy", atst[(eb % 2) * 2 + t4], pb[bs + t4][:],
                          waits=w if t4 == 0 else (), inc=(t4 == 1))
                tok[("a3_ev", tq, eb)] = p.c["v"]
                for t4 in range(2):
                    tok[("a3_stl", tq, eb)] = p.dma(
                        "sy", f"as{eb % 2}",
                        att_sc[:, eb, tq * 1024 + t4 * 512: tq * 1024 + (t4 + 1) * 512],
                        atst[(eb % 2) * 2 + t4],
                        waits=[("v", tok[("a3_ev", tq, eb)])])

        wob = Aap[:, 0:32768]
        acol = [Aap[:, 32768 + i * 2048: 32768 + (i + 1) * 2048] for i in range(2)]
        sstg = [Bf[:, i * 2048:(i + 1) * 2048] for i in range(2)]
        tw = []
        for piece in range(2):
            src = wo_d.rearrange("(c p) f -> p c f", p=128)[
                :, piece * 8:(piece + 1) * 8, :]
            dgt = wob.rearrange("p (c f) -> p c f", f=2048)[
                :, piece * 8:(piece + 1) * 8, :]
            tw.append(p.dma("gp", "wc", dgt, src,
                            waits=[("pe", tok[("a3_mm", 3, DC - 1)])]))
        t_wo = tw[-1]
        for tb in range(NT):
            pg = tb % 2
            w = [tok[("a3_stl", 3, DC - 1)], tok[("a3_stl", 3, DC - 2)]]
            if tb >= 2:
                w.append(("pe", tok[("a4_mm", tb - 2)]))
            t_ac = p.dma("sy", f"ac{pg}",
                         acol[pg].rearrange("p (c t) -> p c t", t=128),
                         att_sc[:, :, tb * 128:(tb + 1) * 128], waits=w)
            bs = pg * 4
            for ec in range(DC):
                w = [t_wo, t_ac]
                if ec == 0:
                    if tb < 2:
                        w.append(("v", tok[("a3_ev", 3, DC - 1)]))
                    else:
                        w.append(("v", tok[("a4_ev", tb - 2)]))
                for db in range(4):
                    p.mm(pb[bs + db][:], acol[pg][:, ec * 128:(ec + 1) * 128],
                         wob[:, ec * 2048 + db * 512: ec * 2048 + (db + 1) * 512],
                         start=(ec == 0), stop=(ec == DC - 1),
                         waits=w if (ec == 0 and db == 0) else ())
            tok[("a4_mm", tb)] = p.mm_inc()
            w = [("pe", tok[("a4_mm", tb)])]
            if tb >= 2:
                w.append(tok[("a4_st", tb - 2)])
            for db in range(4):
                p.vop("tensor_copy", sstg[pg][:, db * 512:(db + 1) * 512],
                      pb[bs + db][:], waits=w if db == 0 else (), inc=(db == 3))
            tok[("a4_ev", tb)] = p.c["v"]
            tok[("a4_st", tb)] = p.dma("sy", f"s4{pg}", s_sc[tb], sstg[pg],
                                       waits=[("v", tok[("a4_ev", tb)])])

        env.update({k: v for k, v in locals().items() if k != "p"})
        if upto == "attn":
            finish(p, tok)
            return

        ffn(p, tok)
        finish(p, tok)

    def ffn(p: P, tok):
        xnr = [Aap[:, i * 8192:(i + 1) * 8192] for i in range(2)]
        wgu = [Aap[:, 16384 + i * 4096: 16384 + (i + 1) * 4096] for i in range(2)]
        hq = Aap[:, 24576: 24576 + 12288]
        wdr = [Aap[:, 36864 + i * 12288: 36864 + (i + 1) * 12288] for i in range(2)]
        yac = Bf[:, 0:8192]
        cbc = Bap[:, 16384:18432]
        fb_g = [Bap[:, 18432 + i * 2048: 18432 + (i + 1) * 2048] for i in range(4)]
        fb_s = Cap[:, 4096:6144]
        fb_x = Bf[:, 13312:15360]
        tsil = [Cf[:, i * 512:(i + 1) * 512] for i in range(2)]
        tuu = [Cf[:, 1024 + i * 512: 1024 + (i + 1) * 512] for i in range(2)]

        a4mm_last = tok[("a4_mm", NT - 1)]
        a4ev_last = ("v", tok[("a4_ev", NT - 1)])
        s4_toks = [tok[("a4_st", NT - 1)], tok[("a4_st", NT - 2)]]

        env.update(dict(yac=yac, fb_g=fb_g, fb_s=fb_s, fb_x=fb_x,
                        s4_toks=s4_toks))
        wgu_n = 0
        wd_n = 0
        NR_RUN = nranges
        gcnt = 0
        bankfree = {b: a4ev_last for b in range(4)}
        stage_free = {0: None, 1: None}
        for r in range(NR_RUN):
            w = [tok[("xnst", NT - 1)], tok[("xnst", NT - 2)]]
            if r < 2:
                w.append(("pe", a4mm_last))
            else:
                w.append(("pe", tok[("gu_mm_last", r - 2)]))
            for i4 in range(4):
                tok[("xnr", r)] = p.dma(
                    "sy", f"xr{r % 2}",
                    xnr[r % 2].rearrange("p (c t) -> p c t", t=512)[
                        :, :, i4 * 128:(i4 + 1) * 128],
                    xn_sc[r * 4 + i4].rearrange("p (c t) -> p c t", t=128),
                    waits=w if i4 == 0 else ())
            if r >= 1:
                fin_silu(p, tok, r - 1)
            for q4 in range(nquads):
                if r >= 1 and 1 <= q4 <= 4:
                    t4f = q4 - 1
                    if t4f >= 1:
                        fin_prod(p, tok, r - 1, t4f - 1)
                    fin_load(p, tok, r - 1, t4f)
                if r >= 1 and q4 == 5:
                    fin_prod(p, tok, r - 1, 3)
                prevq = (r, q4 - 1) if q4 >= 1 else \
                        ((r - 1, nquads - 1) if r >= 1 else None)
                w = []
                if prevq:
                    w.append(("pe", tok[("cb_mm", prevq[0], prevq[1], 3)]))
                t_oh = p.dma("sy", "oh1", oh_b[:],
                             oh_d[:, q4 * 512:(q4 + 1) * 512], waits=w)
                for j in range(4):
                    bank = 4 + (j % 2)
                    w = [t_oh, ("v", tok[("ctev", NT - 1)])]
                    if j < 2:
                        if prevq:
                            w.append(("v", tok[("dn_ev", prevq[0], prevq[1], 1)]))
                        else:
                            w.append(a4ev_last)
                    else:
                        w.append(("sc", tok[("cb_ev", r, q4, j - 2)]))
                    p.mm(pb[bank][:], oh_b[:, j * 128:(j + 1) * 128],
                         cT[:, r * 512:(r + 1) * 512], start=True, stop=True,
                         waits=w)
                    tok[("cb_mm", r, q4, j)] = p.mm_inc()
                    w = [("pe", tok[("cb_mm", r, q4, j)])]
                    if prevq:
                        w.append(("v", tok[("gu_tu", prevq[0], prevq[1], j, IB - 1)]))
                    tok[("cb_ev", r, q4, j)] = p.sop(
                        "copy", cbc[:, j * 512:(j + 1) * 512], pb[bank][:, 0:512],
                        waits=w)
                for j in range(4):
                    e = q4 * 4 + j
                    for ib in range(IB):
                        bufi = wgu_n % 2
                        wgu_n += 1
                        w = []
                        if wgu_n >= 3:
                            w.append(("pe", tok[("gu_3rd_done", wgu_n - 2)]))
                            if r == 0:
                                w.append(tok[("wgu_wb_n", wgu_n - 2)])
                        elif r == 0:
                            w.append(("pe", a4mm_last))
                        if r == 0:
                            src_g = wg_d[e].rearrange("(c p) i -> p c i", p=128)[
                                :, :, ib * 128:(ib + 1) * 128]
                            src_u = wu_d[e].rearrange("(c p) i -> p c i", p=128)[
                                :, :, ib * 128:(ib + 1) * 128]
                            tok[("wgu_gc", wgu_n)] = p.dma(
                                "gp", f"wgc{2 * bufi}",
                                wgu[bufi].rearrange("p (c i) -> p c i", i=128)[
                                    :, 0:DC, :], src_g, waits=w)
                            tok[("wgu", wgu_n)] = p.dma(
                                "gp", f"wgc{2 * bufi + 1}",
                                wgu[bufi].rearrange("p (c i) -> p c i", i=128)[
                                    :, DC:2 * DC, :], src_u,
                                waits=[tok[("wgu_g", wgu_n)]] if False else ())
                            tok[("wgu_wb", e, ib)] = p.dma(
                                "scd", f"wb{bufi}",
                                wff_sc(e)[:, ib * 4096:(ib + 1) * 4096],
                                wgu[bufi],
                                waits=[tok[("wgu", wgu_n)], tok[("wgu_gc", wgu_n)]])
                            tok[("wgu_wb_n", wgu_n)] = tok[("wgu_wb", e, ib)]
                        else:
                            if r == 1:
                                w += tok[("wb_final",)]
                            tok[("wgu", wgu_n)] = p.dma(
                                "sy", f"wg{bufi}", wgu[bufi],
                                wff_sc(e)[:, ib * 4096:(ib + 1) * 4096], waits=w)
                        if True:
                            gbank = (2 * gcnt) % 4
                            ubank = (2 * gcnt + 1) % 4
                            gcnt += 1
                            for gu, bank in ((0, gbank), (1, ubank)):
                                w = [tok[("wgu", wgu_n)],
                                     tok[("xnr", r)]]
                                if r == 0:
                                    w.append(tok[("wgu_gc", wgu_n)])
                                bf = bankfree[bank]
                                if bf:
                                    w.append(bf)
                                off = gu * 2048
                                for dc in range(DC):
                                    p.mm(pb[bank][:],
                                         wgu[bufi][:, off + dc * 128: off + dc * 128 + 128],
                                         xnr[r % 2][:, dc * 512:(dc + 1) * 512],
                                         start=(dc == 0), stop=(dc == DC - 1),
                                         waits=w if dc == 0 else ())
                                tok[("gu_mmt", r, q4, j, ib, gu)] = p.mm_inc()
                            sf = stage_free[ib % 2]
                            w = [("pe", tok[("gu_mmt", r, q4, j, ib, 0)])]
                            if sf:
                                w.append(sf)
                            tok[("gu_sil", r, q4, j, ib)] = p.sop(
                                "activation", tsil[ib % 2], pb[gbank][:], AF.Silu,
                                waits=w)
                            w = [("pe", tok[("gu_mmt", r, q4, j, ib, 1)]),
                                 ("sc", tok[("gu_sil", r, q4, j, ib)])]
                            tok[("gu_tu", r, q4, j, ib)] = p.vop(
                                "tensor_tensor", tuu[ib % 2], pb[ubank][:],
                                cbc[:, j * 512:(j + 1) * 512], ALU.mult, waits=w)
                            w = [("v", tok[("gu_tu", r, q4, j, ib)])]
                            if prevq:
                                w.append(("pe", tok[("dn_mm", prevq[0], prevq[1], 1)]))
                            hv = p.gop(
                                "tensor_tensor",
                                hq[:, (j * IB + ib) * 512:(j * IB + ib + 1) * 512],
                                tsil[ib % 2], tuu[ib % 2], ALU.mult, waits=w)
                            tok[("gu_hv", r, q4, j, ib)] = hv
                            stage_free[ib % 2] = ("gpc", hv)
                            bankfree[gbank] = ("gpc", hv)
                            bankfree[ubank] = ("gpc", hv)
                        tok[("gu_3rd_done", wgu_n)] = tok[("gu_mmt", r, q4, j, ib, 1)]
                tok[("gu_mm_last", r)] = tok[("gu_mmt", r, nquads - 1, 3, IB - 1, 1)] \
                    if q4 == nquads - 1 else tok.get(("gu_mm_last", r), 0)
                for tp in range(2):
                    for j in range(4):
                        e = q4 * 4 + j
                        wdi = wd_n % 2
                        wd_n += 1
                        w = []
                        if wd_n >= 3:
                            w.append(("pe", tok[("dn_grp", wd_n - 2)]))
                            if r == 0:
                                w.append(tok[("wd_wb_n", wd_n - 2)])
                        if r == 0 and tp == 0:
                            tok[("wd_ld", wd_n)] = p.dma(
                                "gp", f"wdc{wdi}", wdr[wdi],
                                wd_d[e].rearrange("(b p) d -> p b d", p=128),
                                waits=w)
                            tok[("wd_wb", e)] = p.dma(
                                "scd", f"wdb{wdi}",
                                wff_sc(e)[:, WDOFF:WDOFF + WDLEN],
                                wdr[wdi], waits=[tok[("wd_ld", wd_n)]])
                            tok[("wd_wb_n", wd_n)] = tok[("wd_wb", e)]
                        else:
                            if r == 0:
                                w.append(tok[("wd_wb", e)])
                            elif r == 1:
                                w += tok[("wb_final",)]
                            tok[("wd_ld", wd_n)] = p.dma(
                                "sy", f"wd{wdi}", wdr[wdi],
                                wff_sc(e)[:, WDOFF:WDOFF + WDLEN], waits=w)
                            if r == 0:
                                tok[("wd_wb_n", wd_n)] = tok[("wd_ld", wd_n)]
                        for ib in range(IB):
                            w0 = [tok[("wd_ld", wd_n)]]
                            if ib == 0:
                                w0.append(("gpc", tok[("gu_hv", r, q4, j, IB - 1)]))
                            if j == 0 and ib == 0:
                                if tp == 0:
                                    w0.append(("sc", tok[("cb_ev", r, q4, 3)]))
                                    w0.append(("gpc", tok[("gu_hv", r, q4, 3, IB - 1)]))
                                    if prevq is None:
                                        w0.append(a4ev_last)
                                else:
                                    w0.append(("v", tok[("dn_ev", r, q4, 0)]))
                            for t4 in (tp * 2, tp * 2 + 1):
                                for db in range(DB):
                                    p.mm(pb[(t4 - tp * 2) * 4 + db][:],
                                         hq[:, (j * IB + ib) * 512 + t4 * 128:
                                            (j * IB + ib) * 512 + (t4 + 1) * 128],
                                         wdr[wdi][:, ib * 2048 + db * 512:
                                                  ib * 2048 + (db + 1) * 512],
                                         start=(j == 0 and ib == 0),
                                         stop=(j == 3 and ib == IB - 1),
                                         waits=w0 if (t4 == tp * 2 and db == 0) else ())
                        tok[("dn_grp", wd_n)] = p.mm_inc()
                    tok[("dn_mm", r, q4, tp)] = tok[("dn_grp", wd_n)]
                    w = [("pe", tok[("dn_mm", r, q4, tp)])]
                    if q4 == 0 and tp == 0:
                        if r == 0:
                            w += s4_toks
                        else:
                            w.append(("sc", tok[("fin_g", r - 1, TB4 - 1)]))
                    first = True
                    for t4 in (tp * 2, tp * 2 + 1):
                        for db in range(DB):
                            dst = yac[:, t4 * 2048 + db * 512: t4 * 2048 + (db + 1) * 512]
                            last = (t4 == tp * 2 + 1 and db == DB - 1)
                            if q4 == 0:
                                p.vop("tensor_copy", dst,
                                      pb[(t4 - tp * 2) * 4 + db][:],
                                      waits=w if first else (), inc=last)
                            else:
                                p.vop("tensor_tensor", dst,
                                      pb[(t4 - tp * 2) * 4 + db][:], dst, ALU.add,
                                      waits=w if first else (), inc=last)
                            first = False
                    tok[("dn_ev", r, q4, tp)] = p.c["v"]
                bankfree[0] = bankfree[1] = bankfree[2] = bankfree[3] = \
                    ("v", tok[("dn_ev", r, q4, 1)])
            if r == 0:
                tok[("wb_final",)] = [("wb0", p.c["wb0"]), ("wb1", p.c["wb1"])] + \
                    [(f"wdb{i}", p.c[f"wdb{i}"]) for i in range(6)]
        fin_silu(p, tok, NR_RUN - 1)
        for t4 in range(TB4):
            fin_load(p, tok, NR_RUN - 1, t4)
            fin_prod(p, tok, NR_RUN - 1, t4)
        return

    def fin_silu(p: P, tok, r):
        for t4 in range(TB4):
            w = [("v", tok[("dn_ev", r, nquads - 1, 1)])]
            if r >= 1:
                w.append(("v", tok[("fin_t", r - 1, t4)]))
            tok[("fin_g", r, t4)] = p.sop(
                "activation", env["fb_g"][t4],
                env["yac"][:, t4 * 2048:(t4 + 1) * 2048], AF.Silu, waits=w)

    def fin_load(p: P, tok, r, t4):
        gtb = r * 4 + t4
        w = [("sc", tok[("fin_g", r, t4)])] + env["s4_toks"]
        if t4 >= 1:
            w.append(("v", tok[("fin_t", r, t4 - 1)]))
        elif r >= 1:
            w.append(("v", tok[("fin_t", r - 1, TB4 - 1)]))
        if t4 >= 1:
            w.append(tok[("out_st", r, t4 - 1)])
        elif r >= 1:
            w.append(tok[("out_st", r - 1, TB4 - 1)])
        tok[("fin_sld", r, t4)] = p.dma("gp", "fsg", env["fb_s"],
                                        s_sc[gtb], waits=w)
        tok[("fin_xld", r, t4)] = p.dma(
            "gp", "fsg", env["fb_x"], x_d[gtb * 128:(gtb + 1) * 128, :])

    def fin_prod(p: P, tok, r, t4):
        gtb = r * 4 + t4
        p.vop("tensor_tensor", env["fb_g"][t4], env["fb_s"], env["fb_g"][t4],
              ALU.mult, waits=[tok[("fin_xld", r, t4)],
                               ("sc", tok[("fin_g", r, t4)])],
              drain=True, inc=False)
        tok[("fin_t", r, t4)] = p.vop("tensor_tensor", env["fb_x"],
                                      env["fb_g"][t4], env["fb_x"],
                                      ALU.add, drain=True)
        tok[("out_st", r, t4)] = p.dma(
            "scd", "os", out_d[gtb * 128:(gtb + 1) * 128, :], env["fb_x"],
            waits=[("v", tok[("fin_t", r, t4)])])

    def _at_slice(Aap_, fc, ec):
        o = fc * 2048 + ec * 128
        return Aap_[:, o:o + 128]

    def finish(p: P, tok):
        if dump:
            dbuf = Bf[:, 0:4096]
            dbufb = Bap[:, 8192:12288]
            bw = [(k, v) for k, v in p.c.items() if v > 0]
            first = True
            prev_tt = None
            t2 = None
            tt = None
            for nm in dump:
                if nm == "cT":
                    w = list(bw) if first else []
                    if tt is not None:
                        w.append(tt)
                    if t2 is not None:
                        w.append(("v", t2))
                    t = p.vop("tensor_copy", Bf[0:E, 0:TCORE], cT[:, :],
                              waits=w, drain=True)
                    first = False
                    prev_tt = p.dma("scd", "du2", dump_d[nm], Bf[0:E, 0:TCORE],
                                    waits=[("v", t)])
                    p.wait("scalar", *prev_tt)
                    t2 = t
                    tt = prev_tt
                    continue
                src, dt, shp = dump_specs[nm]
                rows, cols = shp
                nrt = rows // 128
                nct = (cols + 4095) // 4096
                for rt in range(nrt):
                    for ct in range(nct):
                        cw = min(4096, cols - ct * 4096)
                        sview = src[rt * 128:(rt + 1) * 128,
                                    ct * 4096:ct * 4096 + cw]
                        w = list(bw) if first else []
                        if t2 is not None:
                            w.append(("v", t2))
                        if tt is not None:
                            w.append(tt)
                        if dt == BF16:
                            t1 = p.dma("sy", "du1", dbufb[:, 0:cw], sview, waits=w)
                            t2 = p.vop("tensor_copy", dbuf[:, 0:cw],
                                       dbufb[:, 0:cw], waits=[t1],
                                       drain=True)
                        else:
                            t1 = p.dma("sy", "du1", dbuf[:, 0:cw], sview, waits=w)
                            t2 = p.vop("tensor_copy", dbuf[:, 0:cw],
                                       dbuf[:, 0:cw], waits=[t1],
                                       drain=True)
                        first = False
                        tt = p.dma("scd", "du2",
                                   dump_d[nm][rt * 128:(rt + 1) * 128,
                                              ct * 4096:ct * 4096 + cw],
                                   dbuf[:, 0:cw], waits=[("v", t2)])
                        p.wait("scalar", *tt)
                        prev_tt = tt
        for k, v in p.c.items():
            if v > 0:
                p.wait("sync", k, v)

    blk = ctx.enter_context(nc.Block())

    @blk.sync
    def _(eng):
        walk(P("sync", eng, sems))

    @blk.gpsimd
    def _(eng):
        walk(P("gpsimd", eng, sems))

    @blk.tensor
    def _(eng):
        walk(P("tensor", eng, sems))

    @blk.vector
    def _(eng):
        walk(P("vector", eng, sems))

    @blk.scalar
    def _(eng):
        walk(P("scalar", eng, sems))

    ctx.close()
    return nc


def make_consts():
    idf = np.eye(128, dtype=np.float32)
    idb = np.eye(128, dtype=ml_dtypes.bfloat16)
    oh = np.zeros((E, E * 128), dtype=ml_dtypes.bfloat16)
    for e in range(E):
        oh[e, e * 128:(e + 1) * 128] = 1
    return idf, idb, oh


def make_in_maps(L0_attn_x, L0_attn_Wq, L0_attn_Wk, L0_attn_Wv, L0_attn_Wo,
                 L0_ffn_gate_w, L0_ffn_w_gate, L0_ffn_w_up, L0_ffn_w_down):
    x = np.asarray(L0_attn_x, dtype=np.float32).reshape(-1, D)
    idf, idb, oh = make_consts()
    shared = {
        "wq": np.ascontiguousarray(np.asarray(L0_attn_Wq, np.float32)),
        "wk": np.ascontiguousarray(np.asarray(L0_attn_Wk, np.float32)),
        "wv": np.ascontiguousarray(np.asarray(L0_attn_Wv, np.float32)),
        "wo": np.ascontiguousarray(np.asarray(L0_attn_Wo, np.float32)),
        "gw": np.ascontiguousarray(np.asarray(L0_ffn_gate_w, np.float32)),
        "wg": np.ascontiguousarray(np.asarray(L0_ffn_w_gate, np.float32)),
        "wu": np.ascontiguousarray(np.asarray(L0_ffn_w_up, np.float32)),
        "wd": np.ascontiguousarray(np.asarray(L0_ffn_w_down, np.float32)),
        "idf": idf, "idb": idb, "oh": oh,
    }
    return [dict(shared, x=np.ascontiguousarray(x[c * TCORE:(c + 1) * TCORE]))
            for c in range(8)]


def kernel(L0_attn_x, L0_attn_Wq, L0_attn_Wk, L0_attn_Wv, L0_attn_Wo,
           L0_ffn_gate_w, L0_ffn_w_gate, L0_ffn_w_up, L0_ffn_w_down):
    in_maps = make_in_maps(L0_attn_x, L0_attn_Wq, L0_attn_Wk, L0_attn_Wv,
                           L0_attn_Wo, L0_ffn_gate_w, L0_ffn_w_gate,
                           L0_ffn_w_up, L0_ffn_w_down)
    nc = build_nc()
    res = run_bass_kernel_spmd(nc, in_maps, core_ids=list(range(8)))
    out = np.concatenate([res.results[c]["out"] for c in range(8)], axis=0)
    return out.reshape(2, 16384, D)


# version 07 (final)
# speedup vs baseline: 1.0983x; 1.0752x over previous
import numpy as np
import ml_dtypes
import concourse.bass as bass
import concourse.mybir as mybir
from concourse.bass_utils import run_bass_kernel_spmd

F32 = mybir.dt.float32
BF16 = mybir.dt.bfloat16
AF = mybir.ActivationFunctionType
ALU = mybir.AluOpType
AX = mybir.AxisListType

D = 2048
TCORE = 4096
NT = TCORE // 128
DC = 16
E = 32
I = 768
IB = 6
RNG = 512
NRANGE = TCORE // RNG
TB4 = RNG // 128
DB = 4
NQUAD = 8
RG = [[0, 1, 2, 3], [4, 5, 6, 7]]

WHALF = DC * 384
WGU_H = 2 * WHALF
WDOFF = 2 * WGU_H
WDLEN = IB * DB * 512
WCOLS = WDOFF + WDLEN


class P:
    def __init__(self, name, eng, sems):
        self.name = name
        self.eng = eng
        self.sems = sems
        self.c = {k: 0 for k in sems}
        self.last_mm = None
        self.maxw = {}

    def _emit_waits(self, waits):
        for (wq, wv) in waits:
            assert wv <= self.c[wq], (self.name, wq, wv, self.c[wq])
            if self.name == "tensor" and wq == "pe":
                continue
            if wv > self.maxw.get(wq, 0):
                self.maxw[wq] = wv
                self.eng.wait_ge(self.sems[wq], wv)

    def wait(self, who, q, val):
        assert val <= self.c[q], (who, q, val, self.c[q])
        if self.name == who:
            self._emit_waits([(q, val)])

    def dma(self, q, slot, out, in_, waits=(), **kw):
        eng_for = {"sy": "sync", "gp": "gpsimd", "scd": "scalar"}[q]
        prev = self.c[slot]
        self.c[slot] += 16
        if self.name == eng_for:
            self._emit_waits(waits)
            if q == "gp" and prev > 0:
                self._emit_waits([(slot, prev)])
            self.eng.dma_start(out=out, in_=in_, **kw).then_inc(self.sems[slot], 16)
        return (slot, self.c[slot])

    def mm(self, out, lhsT, rhs, start, stop, waits=()):
        if self.name == "tensor":
            self._emit_waits(waits)
            self.last_mm = self.eng.matmul(out, lhsT, rhs, start=start, stop=stop)

    def mm_inc(self):
        self.c["pe"] += 1
        if self.name == "tensor":
            self.last_mm.then_inc(self.sems["pe"], 1)
        return self.c["pe"]

    def transpose(self, out, in_, ident, waits=()):
        if self.name == "tensor":
            self._emit_waits(waits)
            self.last_mm = self.eng.transpose(out, in_, ident)
        return self.mm_inc()

    def vop(self, fn, *a, waits=(), drain=False, inc=True, **kw):
        if inc:
            self.c["v"] += 1
        if self.name == "vector":
            self._emit_waits(waits)
            if drain:
                self.eng.drain()
            ins = getattr(self.eng, fn)(*a, **kw)
            if inc:
                ins.then_inc(self.sems["v"], 1)
        return self.c["v"]

    def sop(self, fn, *a, waits=(), drain=False, inc=True, **kw):
        if inc:
            self.c["sc"] += 1
        if self.name == "scalar":
            self._emit_waits(waits)
            if drain:
                self.eng.drain()
            ins = getattr(self.eng, fn)(*a, **kw)
            if inc:
                ins.then_inc(self.sems["sc"], 1)
        return self.c["sc"]

    def gop(self, fn, *a, waits=(), inc=True, **kw):
        if inc:
            self.c["gpc"] += 1
        if self.name == "gpsimd":
            self._emit_waits(waits)
            ins = getattr(self.eng, fn)(*a, **kw)
            if inc:
                ins.then_inc(self.sems["gpc"], 1)
        return self.c["gpc"]

    def cc(self, kind, op, ins_, outs_, waits=()):
        self.c["cc"] += 1
        if self.name == "gpsimd":
            self._emit_waits(waits)
            self.eng.collective_compute(
                kind, op, replica_groups=RG, ins=ins_, outs=outs_
            ).then_inc(self.sems["cc"], 1)
        return self.c["cc"]


def build_nc(upto="full", dump=(), nranges=NRANGE, skip_attn=False, detect_races=True, nquads=NQUAD):
    nc = bass.Bass("TRN2", target_bir_lowering=False, num_devices=8,
                   detect_race_conditions=detect_races)

    x_d = nc.dram_tensor("x", [TCORE, D], F32, kind="ExternalInput").ap()
    wq_d = nc.dram_tensor("wq", [D, D], F32, kind="ExternalInput").ap()
    wk_d = nc.dram_tensor("wk", [D, D], F32, kind="ExternalInput").ap()
    wv_d = nc.dram_tensor("wv", [D, D], F32, kind="ExternalInput").ap()
    wo_d = nc.dram_tensor("wo", [D, D], F32, kind="ExternalInput").ap()
    gw_d = nc.dram_tensor("gw", [E, D], F32, kind="ExternalInput").ap()
    wg_d = nc.dram_tensor("wg", [E, D, I], F32, kind="ExternalInput").ap()
    wu_d = nc.dram_tensor("wu", [E, D, I], F32, kind="ExternalInput").ap()
    wd_d = nc.dram_tensor("wd", [E, I, D], F32, kind="ExternalInput").ap()
    idf_d = nc.dram_tensor("idf", [128, 128], F32, kind="ExternalInput").ap()
    idb_d = nc.dram_tensor("idb", [128, 128], BF16, kind="ExternalInput").ap()
    oh_d = nc.dram_tensor("oh", [E, E * 128], BF16, kind="ExternalInput").ap()
    out_d = nc.dram_tensor("out", [TCORE, D], F32, kind="ExternalOutput").ap()

    xn_sc = nc.dram_tensor("xn_sc", [NT, 128, D], BF16).ap()
    q_sc = nc.dram_tensor("q_sc", [128, NT, D], BF16).ap()
    k_sc = nc.dram_tensor("k_sc", [128, NT, D], BF16).ap()
    v_sc = nc.dram_tensor("v_sc", [128, DC, TCORE], BF16).ap()
    att_sc = nc.dram_tensor("att_sc", [128, DC, TCORE], BF16).ap()
    s_sc = nc.dram_tensor("s_sc", [NT, 128, D], F32).ap()
    scc_in = nc.dram_tensor("scc_in", [D, D], F32).ap()
    scc_out = nc.dram_tensor("scc_out", [512, D], F32).ap()
    agg_in = nc.dram_tensor("agg_in", [512, D], BF16).ap()
    agg_out = nc.dram_tensor("agg_out", [D, D], BF16).ap()
    wff_a = nc.dram_tensor("wff_a", [E // 2, 128, WCOLS], BF16).ap()
    wff_b = nc.dram_tensor("wff_b", [E // 2, 128, WCOLS], BF16).ap()
    wff_sc = lambda e: (wff_a[e] if e < E // 2 else wff_b[e - E // 2])

    dump_d = {}
    dump_specs = {
        "xn_sc": (xn_sc.rearrange("a p d -> (a p) d"), BF16, [NT * 128, D]),
        "q_sc": (q_sc.rearrange("p c t -> p (c t)"), BF16, [128, NT * D]),
        "k_sc": (k_sc.rearrange("p c t -> p (c t)"), BF16, [128, NT * D]),
        "v_sc": (v_sc.rearrange("p c t -> p (c t)"), BF16, [128, DC * TCORE]),
        "att_sc": (att_sc.rearrange("p c t -> p (c t)"), BF16, [128, DC * TCORE]),
        "s_sc": (s_sc.rearrange("a p d -> (a p) d"), F32, [NT * 128, D]),
        "scc_in": (scc_in, F32, [D, D]),
        "agg_out": (agg_out, BF16, [D, D]),
        "wff0": (wff_sc(0), BF16, [128, WCOLS]),
    }
    for nm in dump:
        dump_d[nm] = nc.dram_tensor("dump_" + nm, dump_specs[nm][2] if nm != "cT"
                                    else [E, TCORE], F32, kind="ExternalOutput").ap()

    from contextlib import ExitStack
    ctx = ExitStack()
    sb = lambda n, s, d: ctx.enter_context(nc.sbuf_tensor(n, s, d))
    psa = lambda n, s, d: ctx.enter_context(nc.psum_tensor(n, s, d))
    mksem = lambda n: ctx.enter_context(nc.semaphore(n))

    A = sb("A", [128, 61440], BF16)
    Aap = A.ap()
    B = sb("B", [128, 30720], BF16)
    Bap = B.ap()
    Bf = Bap.bitcast(F32)
    C = sb("C", [128, 6144], BF16)
    Cap = C.ap()
    Cf = Cap.bitcast(F32)

    identf = sb("identf", [128, 128], F32)
    identb = sb("identb", [128, 128], BF16)
    gwT = sb("gwT", [128, DC * E], F32)
    cT = sb("cT", [E, TCORE], BF16)
    oh_b = sb("oh_b", [E, 512], BF16)
    gw_sb = None
    stat = sb("stat", [128, 16], F32)
    ls = sb("ls", [128, E], F32)
    elb = sb("elb", [128, E], F32)
    vmax8 = sb("vmax8", [128, 8], F32)
    zap = sb("zap", [128, E], F32)
    selm = sb("selm", [128, E], F32)
    e8 = sb("e8", [128, 8], F32)
    ctile = sb("ctile", [128, E], F32)

    pb = [psa(f"pb{i}", [128, 512], F32) for i in range(8)]
    pbb = [pb[i].ap().bitcast(BF16) for i in range(8)]

    DMA_SLOTS = ["cst", "xl0", "xl1", "xs0", "xs1", "ax0", "ax1", "aq0", "aq1",
                 "wc", "bx0", "bx1", "bv0", "bv1", "kk", "qq0", "qq1",
                 "ss0", "ss1", "sa0", "sa1", "sx0", "sx1", "ag0", "ag1", "ab0", "ab1", "vv0", "vv1", "as0",
                 "as1", "ac0", "ac1", "s40", "s41", "xr0", "xr1", "oh1", "wg0",
                 "wg1", "wd0", "wd1", "wd2", "wd3", "wd4", "wd5",
                 "cva0", "cva1", "cvb0", "cvb1", "cw0", "cw1", "fs", "fsg", "os", "du1", "du2"]
    sems = {k: mksem(k) for k in ["pe", "v", "sc", "gpc", "cc"] + DMA_SLOTS}

    SUM, NM, NM2, VAR, SSQ, LNV, RSTD, BIAS, NM8, S8, RS8, EPS = range(12)
    st = lambda i: stat[:, i:i + 1]

    env = {}
    gw_sb = Bf[0:E, 13312:15360]

    def walk(p: P):
        tok = {}

        t_eps = p.vop("memset", st(EPS), 1e-5)
        t_idf = p.dma("sy", "cst", identf[:], idf_d)
        t_idb = p.dma("sy", "cst", identb[:], idb_d)
        t_gw = p.dma("sy", "cst", gw_sb[:], gw_d)
        ev_gwT = 0
        for dc in range(DC):
            bank = dc % 2
            w = [t_gw, t_idf]
            if dc >= 2:
                w.append(("v", ev_gwT))
            tk = p.transpose(pb[bank][:, 0:E],
                             gw_sb[:, dc * 128:(dc + 1) * 128],
                             identf[0:E, 0:E], waits=w)
            ev_gwT = p.vop("tensor_copy", gwT[:, dc * E:(dc + 1) * E],
                           pb[bank][:, 0:E], waits=[("pe", tk)])

        if upto != "prologue" and not skip_attn:
            wqh0 = Aap[:, 0:16384]
            wkh0 = Aap[:, 16384:32768]
            for (wsrc, dstbuf) in ((wq_d, wqh0), (wk_d, wkh0)):
                for piece in range(2):
                    s_ = wsrc.rearrange("(c p) f -> p c f", p=128)[
                        :, piece * 8:(piece + 1) * 8, 0:1024]
                    d_ = dstbuf.rearrange("p (c f) -> p c f", f=1024)[
                        :, piece * 8:(piece + 1) * 8, :]
                    tok[("a1a_w0",)] = p.dma("gp", "wc", d_, s_)

        xbuf = [Bf[:, 0:2048], Bf[:, 2048:4096]]
        xnf = [Bf[:, 4096:6144], Bf[:, 6144:8192]]
        sqb = Bf[:, 8192:10240]
        xnTf = [Bf[:, 10240:12288], Bf[:, 12288:14336]]
        xnTb = [Cap[:, 0:2048], Cap[:, 2048:4096]]

        for tb in range(NT):
            pg = tb % 2
            w = []
            if tb >= 2:
                w += [("v", tok[("xnf", tb - 2)]), ("sc", tok[("ssq", tb - 2)])]
            t_x = p.dma("sy", f"xl{pg}", xbuf[pg],
                        x_d[tb * 128:(tb + 1) * 128, :], waits=w)
            tok[("rsum", tb)] = p.vop("reduce_sum", st(SUM), xbuf[pg],
                                      waits=[t_x], axis=AX.X)
            w = [t_x]
            if tb >= 1:
                w.append(("sc", tok[("ssq", tb - 1)]))
            tok[("ssq", tb)] = p.sop("activation", sqb, xbuf[pg], AF.Square,
                                     waits=w, accum_out=st(SSQ))
            p.vop("tensor_scalar_mul", st(NM), st(SUM), -1.0 / D, drain=True,
                  inc=False)
            p.vop("tensor_mul", st(NM2), st(NM), st(NM), drain=True, inc=False)
            p.vop("tensor_scalar_mul", st(VAR), st(SSQ), 1.0 / D,
                  waits=[("sc", tok[("ssq", tb)])], drain=True, inc=False)
            tok[("var", tb)] = p.vop("tensor_sub", st(VAR), st(VAR), st(NM2),
                                     drain=True)
            p.sop("activation", st(LNV), st(VAR), AF.Ln, bias=st(EPS),
                  waits=[("v", tok[("var", tb)])], inc=False)
            tok[("rstd", tb)] = p.sop("activation", st(RSTD), st(LNV), AF.Exp,
                                      scale=-0.5, drain=True)
            w = [("sc", tok[("rstd", tb)])]
            if tb >= 2:
                w.append(("pe", tok[("tr", tb - 2)]))
            tok[("xnf", tb)] = p.vop("tensor_scalar", xnf[pg], xbuf[pg],
                                     st(NM), st(RSTD), ALU.add, ALU.mult,
                                     waits=w, drain=True)
            for dc in range(DC):
                bank = dc % 2
                w = [("v", tok[("xnf", tb)])]
                key = ("trev", tb, dc - 2) if dc >= 2 else \
                      (("trev", tb - 1, DC - 2 + dc) if tb >= 1 else None)
                if key:
                    w.append(("v", tok[key]))
                tk = p.transpose(pb[bank][:, 0:128],
                                 xnf[pg][:, dc * 128:(dc + 1) * 128],
                                 identf[:], waits=w)
                if dc == DC - 1:
                    tok[("tr", tb)] = tk
                w = [("pe", tk)]
                if dc == 0 and tb >= 2:
                    w += [("pe", tok[("rt", tb - 2)]), ("sc", tok[("xnTb", tb - 2)])]
                tok[("trev", tb, dc)] = p.vop(
                    "tensor_copy", xnTf[pg][:, dc * 128:(dc + 1) * 128],
                    pb[bank][:, 0:128], waits=w)
            w = [("v", tok[("trev", tb, DC - 1)])]
            if tb >= 2:
                w.append(tok[("xnst", tb - 2)])
            tok[("xnTb", tb)] = p.sop("copy", xnTb[pg], xnTf[pg], waits=w)
            tok[("xnst", tb)] = p.dma(
                "sy", f"xs{pg}", xn_sc[tb], xnTb[pg],
                waits=[("sc", tok[("xnTb", tb)])])
            for dc in range(DC):
                w = [("v", tok[("trev", tb, dc)])]
                if dc == 0 and tb >= 1:
                    w.append(("v", tok[("lg", tb - 1)]))
                p.mm(pb[2][:, 0:E], xnTf[pg][:, dc * 128:(dc + 1) * 128],
                     gwT[:, dc * E:(dc + 1) * E],
                     start=(dc == 0), stop=(dc == DC - 1), waits=w)
            tok[("rt", tb)] = p.mm_inc()
            w = [("pe", tok[("rt", tb)])]
            if tb >= 1:
                w.append(("sc", tok[("el", tb - 1)]))
            tok[("lg", tb)] = p.vop("tensor_copy", ls[:], pb[2][:, 0:E],
                                    waits=w, drain=True)
            p.vop("max", drain=True, inc=False, out=vmax8[:], in_=ls[:])
            p.vop("match_replace", drain=True, inc=False, out=zap[:],
                  in_to_replace=vmax8[:], in_values=ls[:], imm_value=-1e30)
            p.vop("tensor_tensor", selm[:], ls[:], zap[:], ALU.not_equal,
                  drain=True, inc=False)
            tok[("nm8", tb)] = p.vop("tensor_scalar_mul", st(NM8), vmax8[:, 0:1],
                                     -1.0, drain=True)
            tok[("e8", tb)] = p.sop("activation", e8[:], vmax8[:], AF.Exp,
                                    bias=st(NM8), waits=[("v", tok[("nm8", tb)])])
            p.vop("reduce_sum", st(S8), e8[:], axis=AX.X,
                  waits=[("sc", tok[("e8", tb)])], drain=True, inc=False)
            tok[("rs8", tb)] = p.vop("reciprocal", st(RS8), st(S8), drain=True)
            tok[("el", tb)] = p.sop("activation", elb[:], ls[:], AF.Exp,
                                    bias=st(NM8))
            w = [("sc", tok[("el", tb)]), ("v", tok[("rs8", tb)])]
            if tb >= 1:
                w.append(("v", tok[("ctev", tb - 1)]))
            tok[("c", tb)] = p.vop("scalar_tensor_tensor", waits=w, drain=True,
                                   out=ctile[:], in0=elb[:], scalar=st(RS8),
                                   in1=selm[:], op0=ALU.mult, op1=ALU.mult)
            tk = p.transpose(pb[3][0:E, 0:128], ctile[:], identf[:],
                             waits=[("v", tok[("c", tb)])])
            tok[("ctev", tb)] = p.vop("tensor_copy", cT[:, tb * 128:(tb + 1) * 128],
                                      pb[3][0:E, 0:128], waits=[("pe", tk)])
            if upto != "prologue":
                conv_step(p, tok, 0.30 * (tb + 1) / NT)

        env.update(locals())
        if upto == "prologue":
            finish(p, tok)
            return

        if skip_attn:
            tok[("a4_mm", NT - 1)] = tok[("rt", NT - 1)]
            tok[("a4_ev", NT - 1)] = tok[("ctev", NT - 1)]
            tok[("a4_st", NT - 1)] = tok[("xnst", NT - 1)]
            tok[("a4_st", NT - 2)] = tok[("xnst", NT - 2)]
            ffn(p, tok)
            finish(p, tok)
            return

        wqh = Aap[:, 0:16384]
        wkh = Aap[:, 16384:32768]
        xnt = [Aap[:, 32768 + i * 2048: 32768 + (i + 1) * 2048] for i in range(2)]
        qkst = [Cap[:, i * 1024:(i + 1) * 1024] for i in range(4)]

        for half in range(2):
            if half == 0:
                t_w = tok[("a1a_w0",)]
            else:
                for (wsrc, dstbuf) in ((wq_d, wqh), (wk_d, wkh)):
                    for piece in range(2):
                        src = wsrc.rearrange("(c p) f -> p c f", p=128)[
                            :, piece * 8:(piece + 1) * 8,
                            half * 1024:(half + 1) * 1024]
                        dgt = dstbuf.rearrange("p (c f) -> p c f", f=1024)[
                            :, piece * 8:(piece + 1) * 8, :]
                        t_w = p.dma("gp", "wc", dgt, src,
                                    waits=[("pe", tok[("a1a_mm", 0, NT - 1)])])
            for tb in range(NT):
                pg = tb % 2
                w = [tok[("xnst", tb)]]
                prev = ("a1a_mm", half, tb - 2) if tb >= 2 else \
                       (("a1a_mm", half - 1, NT - 2 + tb) if half >= 1 else None)
                if prev:
                    w.append(("pe", tok[prev]))
                t_xn = p.dma("sy", f"ax{pg}", xnt[pg], xn_sc[tb], waits=w)
                bs = pg * 4
                for dc in range(DC):
                    w = [t_w, t_xn]
                    if dc == 0:
                        if half == 0 and tb == 0:
                            w.append(("v", tok[("ctev", NT - 1)]))
                        prev = ("a1a_ev", half, tb - 2) if tb >= 2 else \
                               (("a1a_ev", half - 1, NT - 2 + tb) if half >= 1 else None)
                        if prev:
                            w.append(("v", tok[prev]))
                    lhs = xnt[pg][:, dc * 128:(dc + 1) * 128]
                    first = True
                    for (wbuf, bo) in ((wqh, 0), (wkh, 2)):
                        for fb in range(2):
                            p.mm(pb[bs + bo + fb][:], lhs,
                                 wbuf[:, dc * 1024 + fb * 512: dc * 1024 + (fb + 1) * 512],
                                 start=(dc == 0), stop=(dc == DC - 1),
                                 waits=w if (dc == 0 and first) else ())
                            first = False
                tok[("a1a_mm", half, tb)] = p.mm_inc()
                w = [("pe", tok[("a1a_mm", half, tb)])]
                prev = ("a1a_st", half, tb - 2) if tb >= 2 else \
                       (("a1a_st", half - 1, NT - 2 + tb) if half >= 1 else None)
                if prev:
                    w.append(tok[prev])
                if half == 0 and tb < 2:
                    w += [tok[("xnst", NT - 1)], tok[("xnst", NT - 2)]]
                p.vop("tensor_copy", qkst[pg * 2][:, 0:512], pb[bs + 0][:],
                      waits=w, inc=False)
                p.vop("tensor_copy", qkst[pg * 2][:, 512:1024], pb[bs + 1][:],
                      inc=False)
                p.vop("tensor_copy", qkst[pg * 2 + 1][:, 0:512], pb[bs + 2][:],
                      inc=False)
                tok[("a1a_ev", half, tb)] = p.vop(
                    "tensor_copy", qkst[pg * 2 + 1][:, 512:1024], pb[bs + 3][:])
                p.dma("scd", f"aq{pg}",
                      q_sc[:, tb, half * 1024:(half + 1) * 1024],
                      qkst[pg * 2], waits=[("v", tok[("a1a_ev", half, tb)])])
                tok[("a1a_st", half, tb)] = p.dma(
                    "scd", f"aq{pg}",
                    k_sc[:, tb, half * 1024:(half + 1) * 1024],
                    qkst[pg * 2 + 1])
                conv_step(p, tok, 0.30 + 0.25 * (half * NT + tb + 1) / (2 * NT))

        if upto == "a1a":
            finish(p, tok)
            return

        wvh = Aap[:, 0:16384]
        xn5 = [Aap[:, 16384 + i * 8192: 16384 + (i + 1) * 8192] for i in range(2)]
        vst = [Cap[:, i * 512:(i + 1) * 512] for i in range(8)]
        for half in range(2):
            tw = []
            for piece in range(2):
                src = wv_d.rearrange("(c p) (b f) -> p c b f", p=128, f=128)[
                    :, piece * 8:(piece + 1) * 8, half * 8:(half + 1) * 8, :]
                dgt = wvh.rearrange("p (c b f) -> p c b f", b=8, f=128)[
                    :, piece * 8:(piece + 1) * 8, :, :]
                w = []
                if half == 0:
                    w.append(("pe", tok[("a1a_mm", 1, NT - 1)]))
                else:
                    w.append(("pe", tok[("a1b_mm", 0, 7)]))
                tw.append(p.dma("gp", "wc", dgt, src, waits=w))
            t_w = tw[-1]
            for t5 in range(8):
                pg = t5 % 2
                w = [tok[("xnst", NT - 1)], tok[("xnst", NT - 2)]]
                prev = ("a1b_mm", half, t5 - 2) if t5 >= 2 else \
                       (("a1b_mm", half - 1, 6 + t5) if half >= 1 else None)
                if prev:
                    w.append(("pe", tok[prev]))
                if half == 0 and t5 < 2:
                    w.append(("pe", tok[("a1a_mm", 1, NT - 1)]))
                for i4 in range(4):
                    t_xn = p.dma(
                        "sy", f"bx{pg}",
                        xn5[pg].rearrange("p (c t) -> p c t", t=512)[
                            :, :, i4 * 128:(i4 + 1) * 128],
                        xn_sc[t5 * 4 + i4].rearrange("p (c t) -> p c t", t=128),
                        waits=w if i4 == 0 else ())
                for dc in range(DC):
                    for fb in range(8):
                        w = [t_w, t_xn]
                        if dc == 0:
                            key = ("a1b_ev", half, t5 - 1, fb) if t5 >= 1 else \
                                  (("a1b_ev", half - 1, 7, fb) if half >= 1 else None)
                            if key:
                                w.append(("v", tok[key]))
                            if half == 0 and t5 == 0:
                                w.append(("v", tok[("a1a_ev", 1, NT - 1)]))
                        p.mm(pb[fb][:],
                             wvh[:, (dc * 8 + fb) * 128:(dc * 8 + fb + 1) * 128],
                             xn5[pg][:, dc * 512:(dc + 1) * 512],
                             start=(dc == 0), stop=(dc == DC - 1),
                             waits=w if dc == 0 else ())
                tok[("a1b_mm", half, t5)] = p.mm_inc()
                for fb in range(8):
                    w = [("pe", tok[("a1b_mm", half, t5)])]
                    key = ("a1b_st", half, t5 - 1, 7) if t5 >= 1 else \
                          (("a1b_st", half - 1, 7, 7) if half >= 1 else None)
                    if key:
                        w.append(tok[key])
                    if half == 0 and t5 == 0:
                        w += [tok[("a1a_st", 1, NT - 1)], tok[("a1a_st", 1, NT - 2)]]
                    tok[("a1b_ev", half, t5, fb)] = p.vop(
                        "tensor_copy", vst[fb], pb[fb][:], waits=w)
                    tok[("a1b_st", half, t5, fb)] = p.dma(
                        "sy", f"bv{pg}",
                        v_sc[:, half * 8 + fb, t5 * 512:(t5 + 1) * 512],
                        vst[fb], waits=[("v", tok[("a1b_ev", half, t5, fb)])])
                conv_step(p, tok, 0.55 + 0.10 * (half * 8 + t5 + 1) / 16)

        if upto == "a1b":
            finish(p, tok)
            return

        kb = Aap[:, 0:32768]
        qb = [Aap[:, 32768 + i * 8192: 32768 + (i + 1) * 8192] for i in range(2)]
        sst = [Bf[:, i * 512:(i + 1) * 512] for i in range(8)]
        for kh in range(2):
            w = [tok[("a1b_st", 1, 7, 7)], tok[("a1b_st", 1, 6, 7)],
                 tok[("a1a_st", 1, NT - 1)], tok[("a1a_st", 1, NT - 2)]]
            if kh == 1:
                w.append(("pe", tok[("sc_mm", 0, 3, 3)]))
            t_k = p.dma("sy", "kk", kb.rearrange("p (c f) -> p c f", f=2048),
                        k_sc[:, kh * 16:(kh + 1) * 16, :], waits=w)
            for eq in range(4):
                pg = eq % 2
                w = [tok[("a1a_st", 1, NT - 1)], tok[("a1a_st", 1, NT - 2)]]
                prev = ("sc_mm", kh, eq - 2, 3) if eq >= 2 else \
                       (("sc_mm", kh - 1, 2 + eq, 3) if kh >= 1 else None)
                if prev:
                    w.append(("pe", tok[prev]))
                t_q = p.dma("sy", f"qq{pg}",
                            qb[pg].rearrange("p (c e) -> p c e", e=512),
                            q_sc[:, kh * 16:(kh + 1) * 16, eq * 512:(eq + 1) * 512],
                            waits=w)
                for fb in range(4):
                    bs = ((eq * 4 + fb) % 2) * 4
                    for tc in range(16):
                        w = [t_k, t_q]
                        if tc == 0:
                            key = ("sc_ev", kh, eq, fb - 2) if fb >= 2 else \
                                  (("sc_ev", kh, eq - 1, 2 + fb) if eq >= 1 else
                                   (("sc_ev", kh - 1, 3, 2 + fb) if kh >= 1 else None))
                            if key:
                                w.append(("v", tok[key]))
                            elif fb < 2:
                                w.append(("v", tok[("a1b_ev", 1, 7, 7)]))
                        for eb in range(4):
                            p.mm(pb[bs + eb][:],
                                 qb[pg][:, tc * 512 + eb * 128: tc * 512 + (eb + 1) * 128],
                                 kb[:, tc * 2048 + fb * 512: tc * 2048 + (fb + 1) * 512],
                                 start=(tc == 0), stop=(tc == 15),
                                 waits=w if (tc == 0 and eb == 0) else ())
                    tok[("sc_mm", kh, eq, fb)] = p.mm_inc()
                    w = [("pe", tok[("sc_mm", kh, eq, fb)])]
                    key = ("sc_fin", kh, eq, fb - 2) if fb >= 2 else \
                          (("sc_fin", kh, eq - 1, 2 + fb) if eq >= 1 else
                           (("sc_fin", kh - 1, 3, 2 + fb) if kh >= 1 else None))
                    if key:
                        w.append(tok[key])
                    if kh == 0 and eq == 0 and fb < 2:
                        w += [tok[("a1b_st", 1, 7, 7)], tok[("a1b_st", 1, 6, 7)]]
                    gpar = (eq * 4 + fb) % 2
                    for eb in range(4):
                        p.vop("tensor_copy", sst[gpar * 4 + eb], pb[bs + eb][:],
                              waits=w if eb == 0 else (), inc=(eb == 3))
                    tok[("sc_ev", kh, eq, fb)] = p.c["v"]
                    for eb in range(4):
                        dst = scc_in[(eq * 4 + eb) * 128:(eq * 4 + eb + 1) * 128,
                                     fb * 512:(fb + 1) * 512]
                        wv = [("v", tok[("sc_ev", kh, eq, fb)])]
                        if kh == 0:
                            tt = p.dma("sy", f"ss{gpar}", dst,
                                       sst[gpar * 4 + eb], waits=wv)
                            tok[("sc_fin", kh, eq, fb)] = tt
                        else:
                            wv += tok[("ss_all")]
                            tt = p.dma("gp", f"sa{gpar}", dst,
                                       sst[gpar * 4 + eb], waits=wv,
                                       accum_op=ALU.add)
                            tok[("sc_fin", kh, eq, fb)] = tt
                    conv_step(p, tok, 0.65 + 0.12 *
                              (kh * 16 + eq * 4 + fb + 1) / 32)
            if kh == 0:
                tok[("ss_all")] = [("ss0", p.c["ss0"]), ("ss1", p.c["ss1"])]

        if upto == "a2":
            finish(p, tok)
            return

        t_rs = p.cc("ReduceScatter", ALU.add, [scc_in], [scc_out],
                    waits=[("ss0", p.c["ss0"]), ("ss1", p.c["ss1"]),
                           ("sa0", p.c["sa0"]), ("sa1", p.c["sa1"])])
        smx = [Bf[:, i * 2048:(i + 1) * 2048] for i in range(2)]
        smy = [Bf[:, 4096 + i * 2048: 4096 + (i + 1) * 2048] for i in range(2)]
        smo = [Bap[:, 16384 + i * 2048: 16384 + (i + 1) * 2048] for i in range(2)]
        for i in range(4):
            pg = i % 2
            w = [("cc", t_rs)]
            if i >= 2:
                w.append(tok[("ag_st", i - 2)])
            t_s = p.dma("sy", f"sx{pg}", smx[pg],
                        scc_out[i * 128:(i + 1) * 128, :], waits=w)
            p.vop("reduce_max", st(SUM), smx[pg], axis=AX.X,
                  waits=[t_s], drain=True, inc=False)
            tok[("smm", i)] = p.vop("tensor_scalar_mul", st(NM), st(SUM), -0.125,
                                    drain=True)
            tok[("sme", i)] = p.sop("activation", smy[pg], smx[pg], AF.Exp,
                                    scale=0.125, bias=st(NM), accum_out=st(S8),
                                    waits=[("v", tok[("smm", i)])])
            tok[("smr", i)] = p.vop("reciprocal", st(RS8), st(S8),
                                    waits=[("sc", tok[("sme", i)])])
            tok[("smA", i)] = p.vop("tensor_scalar_mul", smo[pg], smy[pg], st(RS8),
                                    drain=True)
            tok[("ag_st", i)] = p.dma("scd", f"ag{pg}",
                                      agg_in[i * 128:(i + 1) * 128, :],
                                      smo[pg], waits=[("v", tok[("smA", i)])])
        t_ag = p.cc("AllGather", ALU.bypass, [agg_in], [agg_out],
                    waits=[("ag0", p.c["ag0"]), ("ag1", p.c["ag1"])])

        ATs = Aap[:, 0:32768]
        abuf = [Aap[:, 32768 + i * 2048: 32768 + (i + 1) * 2048] for i in range(2)]
        for ec in range(DC):
            pg = ec % 2
            w = [("cc", t_ag)]
            if ec >= 2:
                w.append(("v", tok[("at_ev", ec - 2, DC - 1)]))
            t_a = p.dma("sy", f"ab{pg}", abuf[pg],
                        agg_out[ec * 128:(ec + 1) * 128, :], waits=w)
            for fc in range(DC):
                bank = fc % 2
                w = [t_a, t_idb]
                key = ("at_ev", ec, fc - 2) if fc >= 2 else \
                      (("at_ev", ec - 1, DC - 2 + fc) if ec >= 1 else None)
                if key:
                    w.append(("v", tok[key]))
                tk = p.transpose(pbb[bank][:, 0:128],
                                 abuf[pg][:, fc * 128:(fc + 1) * 128], identb[:],
                                 waits=w)
                tok[("at_ev", ec, fc)] = p.vop(
                    "tensor_copy", _at_slice(Aap, fc, ec),
                    pbb[bank][:, 0:128], waits=[("pe", tk)])
            conv_step(p, tok, 0.77 + 0.03 * (ec + 1) / DC)

        vhb = Aap[:, 36864: 36864 + 16384]
        atst = [Cap[:, i * 512:(i + 1) * 512] for i in range(8)]
        t_at_last = tok[("at_ev", DC - 1, DC - 1)]
        for tq in range(4):
            w = [tok[("a1b_st", 1, 7, 7)], tok[("a1b_st", 1, 6, 7)],
                 ("v", t_at_last)]
            if tq >= 1:
                w.append(("pe", tok[("a3_mm", tq - 1, DC - 1)]))
            t_v = p.dma("sy", f"vv{tq % 2}",
                        vhb.rearrange("p (c t) -> p c t", t=1024),
                        v_sc[:, :, tq * 1024:(tq + 1) * 1024], waits=w)
            for eb in range(DC):
                bs = (eb % 2) * 4
                for fc in range(DC):
                    w = [t_v]
                    if fc == 0:
                        key = ("a3_ev", tq, eb - 2) if eb >= 2 else \
                              (("a3_ev", tq - 1, DC - 2 + eb) if tq >= 1 else None)
                        if key:
                            w.append(("v", tok[key]))
                    for t4 in range(2):
                        p.mm(pb[bs + t4][:],
                             _at_slice(Aap, fc, eb),
                             vhb[:, fc * 1024 + t4 * 512: fc * 1024 + (t4 + 1) * 512],
                             start=(fc == 0), stop=(fc == DC - 1),
                             waits=w if (fc == 0 and t4 == 0) else ())
                tok[("a3_mm", tq, eb)] = p.mm_inc()
                w = [("pe", tok[("a3_mm", tq, eb)])]
                key = ("a3_stl", tq, eb - 2) if eb >= 2 else \
                      (("a3_stl", tq - 1, DC - 2 + eb) if tq >= 1 else None)
                if key:
                    w.append(tok[key])
                for t4 in range(2):
                    p.vop("tensor_copy", atst[(eb % 2) * 2 + t4], pb[bs + t4][:],
                          waits=w if t4 == 0 else (), inc=(t4 == 1))
                tok[("a3_ev", tq, eb)] = p.c["v"]
                for t4 in range(2):
                    tok[("a3_stl", tq, eb)] = p.dma(
                        "sy", f"as{eb % 2}",
                        att_sc[:, eb, tq * 1024 + t4 * 512: tq * 1024 + (t4 + 1) * 512],
                        atst[(eb % 2) * 2 + t4],
                        waits=[("v", tok[("a3_ev", tq, eb)])])
                conv_step(p, tok, 0.80 + 0.12 * (tq * DC + eb + 1) / (4 * DC))

        wob = Aap[:, 0:32768]
        acol = [Aap[:, 32768 + i * 2048: 32768 + (i + 1) * 2048] for i in range(2)]
        sstg = [Bf[:, i * 2048:(i + 1) * 2048] for i in range(2)]
        tw = []
        for piece in range(2):
            src = wo_d.rearrange("(c p) f -> p c f", p=128)[
                :, piece * 8:(piece + 1) * 8, :]
            dgt = wob.rearrange("p (c f) -> p c f", f=2048)[
                :, piece * 8:(piece + 1) * 8, :]
            tw.append(p.dma("gp", "wc", dgt, src,
                            waits=[("pe", tok[("a3_mm", 3, DC - 1)])]))
        t_wo = tw[-1]
        for tb in range(NT):
            pg = tb % 2
            w = [tok[("a3_stl", 3, DC - 1)], tok[("a3_stl", 3, DC - 2)]]
            if tb >= 2:
                w.append(("pe", tok[("a4_mm", tb - 2)]))
            t_ac = p.dma("sy", f"ac{pg}",
                         acol[pg].rearrange("p (c t) -> p c t", t=128),
                         att_sc[:, :, tb * 128:(tb + 1) * 128], waits=w)
            bs = pg * 4
            for ec in range(DC):
                w = [t_wo, t_ac]
                if ec == 0:
                    if tb < 2:
                        w.append(("v", tok[("a3_ev", 3, DC - 1)]))
                    else:
                        w.append(("v", tok[("a4_ev", tb - 2)]))
                for db in range(4):
                    p.mm(pb[bs + db][:], acol[pg][:, ec * 128:(ec + 1) * 128],
                         wob[:, ec * 2048 + db * 512: ec * 2048 + (db + 1) * 512],
                         start=(ec == 0), stop=(ec == DC - 1),
                         waits=w if (ec == 0 and db == 0) else ())
            tok[("a4_mm", tb)] = p.mm_inc()
            w = [("pe", tok[("a4_mm", tb)])]
            if tb >= 2:
                w.append(tok[("a4_st", tb - 2)])
            for db in range(4):
                p.vop("tensor_copy", sstg[pg][:, db * 512:(db + 1) * 512],
                      pb[bs + db][:], waits=w if db == 0 else (), inc=(db == 3))
            tok[("a4_ev", tb)] = p.c["v"]
            tok[("a4_st", tb)] = p.dma("sy", f"s4{pg}", s_sc[tb], sstg[pg],
                                       waits=[("v", tok[("a4_ev", tb)])])
            conv_step(p, tok, 0.92 + 0.08 * (tb + 1) / NT)

        env.update({k: v for k, v in locals().items() if k != "p"})
        if upto == "attn":
            finish(p, tok)
            return

        ffn(p, tok)
        finish(p, tok)

    CONV_PIECES = []
    for _e in range(E):
        for _ib in range(IB):
            CONV_PIECES.append(("wgu", _e, _ib))
        for _blk in range(3):
            CONV_PIECES.append(("wd", _e, _blk))

    def conv_step(p: P, tok, frac):
        cbnc = [Aap[:, 53248 + i * 4096: 53248 + (i + 1) * 4096] for i in range(2)]
        target = int(len(CONV_PIECES) * min(frac, 1.0) + 1e-6)
        n = tok.get(("conv_n",), 0)
        while n < target:
            kind, e, i = CONV_PIECES[n]
            par = n % 2
            w = []
            if n >= 2:
                w.append(tok[("conv_wb", n - 2)])
            if kind == "wgu":
                ca = p.dma("gp", f"cva{par}",
                           cbnc[par].rearrange("p (c i) -> p c i", i=128)[:, 0:DC, :],
                           wg_d[e].rearrange("(c p) i -> p c i", p=128)[
                               :, :, i * 128:(i + 1) * 128], waits=w)
                cb2 = p.dma("gp", f"cvb{par}",
                            cbnc[par].rearrange("p (c i) -> p c i", i=128)[
                                :, DC:2 * DC, :],
                            wu_d[e].rearrange("(c p) i -> p c i", p=128)[
                                :, :, i * 128:(i + 1) * 128])
                dst = wff_sc(e)[:, i * 4096:(i + 1) * 4096]
            else:
                ca = p.dma("gp", f"cva{par}", cbnc[par],
                           wd_d[e].rearrange("(b p) d -> p b d", p=128)[
                               :, 2 * i:2 * i + 2, :], waits=w)
                cb2 = ca
                dst = wff_sc(e)[:, WDOFF + i * 4096:WDOFF + (i + 1) * 4096]
            tok[("conv_wb", n)] = p.dma("scd", f"cw{par}", dst, cbnc[par],
                                        waits=[ca, cb2])
            n += 1
            tok[("conv_n",)] = n
        return

    def ffn(p: P, tok):
        xnr = [Aap[:, i * 8192:(i + 1) * 8192] for i in range(2)]
        wgu = [Aap[:, 16384 + i * 4096: 16384 + (i + 1) * 4096] for i in range(2)]
        hq = Aap[:, 24576: 24576 + 12288]
        wdr = [Aap[:, 36864 + i * 12288: 36864 + (i + 1) * 12288] for i in range(2)]
        yac = Bf[:, 0:8192]
        cbc = Bap[:, 16384:18432]
        fb_g = [Bap[:, 18432 + i * 2048: 18432 + (i + 1) * 2048] for i in range(4)]
        fb_s = Cap[:, 4096:6144]
        fb_x = Bf[:, 13312:15360]
        tsil = [Cf[:, i * 512:(i + 1) * 512] for i in range(2)]
        tuu = [Cf[:, 1024 + i * 512: 1024 + (i + 1) * 512] for i in range(2)]

        a4mm_last = tok[("a4_mm", NT - 1)]
        a4ev_last = ("v", tok[("a4_ev", NT - 1)])
        s4_toks = [tok[("a4_st", NT - 1)], tok[("a4_st", NT - 2)]]

        env.update(dict(yac=yac, fb_g=fb_g, fb_s=fb_s, fb_x=fb_x,
                        s4_toks=s4_toks))
        conv_step(p, tok, 1.0)
        tok[("conv_done",)] = [("cw0", p.c["cw0"]), ("cw1", p.c["cw1"])]
        wgu_n = 0
        wd_n = 0
        NR_RUN = nranges
        gcnt = 0
        bankfree = {b: a4ev_last for b in range(4)}
        stage_free = {0: None, 1: None}
        for r in range(NR_RUN):
            w = [tok[("xnst", NT - 1)], tok[("xnst", NT - 2)]]
            if r < 2:
                w.append(("pe", a4mm_last))
            else:
                w.append(("pe", tok[("gu_mm_last", r - 2)]))
            for i4 in range(4):
                tok[("xnr", r)] = p.dma(
                    "sy", f"xr{r % 2}",
                    xnr[r % 2].rearrange("p (c t) -> p c t", t=512)[
                        :, :, i4 * 128:(i4 + 1) * 128],
                    xn_sc[r * 4 + i4].rearrange("p (c t) -> p c t", t=128),
                    waits=w if i4 == 0 else ())
            if r >= 1:
                fin_silu(p, tok, r - 1)
            for q4 in range(nquads):
                if r >= 1 and 1 <= q4 <= 4:
                    t4f = q4 - 1
                    if t4f >= 1:
                        fin_prod(p, tok, r - 1, t4f - 1)
                    fin_load(p, tok, r - 1, t4f)
                if r >= 1 and q4 == 5:
                    fin_prod(p, tok, r - 1, 3)
                prevq = (r, q4 - 1) if q4 >= 1 else \
                        ((r - 1, nquads - 1) if r >= 1 else None)
                w = []
                if prevq:
                    w.append(("pe", tok[("cb_mm", prevq[0], prevq[1], 3)]))
                t_oh = p.dma("sy", "oh1", oh_b[:],
                             oh_d[:, q4 * 512:(q4 + 1) * 512], waits=w)
                for j in range(4):
                    bank = 4 + (j % 2)
                    w = [t_oh, ("v", tok[("ctev", NT - 1)])]
                    if j < 2:
                        if prevq:
                            w.append(("v", tok[("dn_ev", prevq[0], prevq[1], 1)]))
                        else:
                            w.append(a4ev_last)
                    else:
                        w.append(("sc", tok[("cb_ev", r, q4, j - 2)]))
                    p.mm(pb[bank][:], oh_b[:, j * 128:(j + 1) * 128],
                         cT[:, r * 512:(r + 1) * 512], start=True, stop=True,
                         waits=w)
                    tok[("cb_mm", r, q4, j)] = p.mm_inc()
                    w = [("pe", tok[("cb_mm", r, q4, j)])]
                    if prevq:
                        w.append(("v", tok[("gu_tu", prevq[0], prevq[1], j, IB - 1)]))
                    tok[("cb_ev", r, q4, j)] = p.sop(
                        "copy", cbc[:, j * 512:(j + 1) * 512], pb[bank][:, 0:512],
                        waits=w)
                for j in range(4):
                    e = q4 * 4 + j
                    for ib in range(IB):
                        bufi = wgu_n % 2
                        wgu_n += 1
                        w = []
                        if wgu_n >= 3:
                            w.append(("pe", tok[("gu_3rd_done", wgu_n - 2)]))
                        elif r == 0:
                            w.append(("pe", a4mm_last))
                        if r == 0:
                            w += tok[("conv_done",)]
                        tok[("wgu", wgu_n)] = p.dma(
                            "sy", f"wg{bufi}", wgu[bufi],
                            wff_sc(e)[:, ib * 4096:(ib + 1) * 4096], waits=w)
                        if True:
                            gbank = (2 * gcnt) % 4
                            ubank = (2 * gcnt + 1) % 4
                            gcnt += 1
                            for gu, bank in ((0, gbank), (1, ubank)):
                                w = [tok[("wgu", wgu_n)],
                                     tok[("xnr", r)]]
                                bf = bankfree[bank]
                                if bf:
                                    w.append(bf)
                                off = gu * 2048
                                for dc in range(DC):
                                    p.mm(pb[bank][:],
                                         wgu[bufi][:, off + dc * 128: off + dc * 128 + 128],
                                         xnr[r % 2][:, dc * 512:(dc + 1) * 512],
                                         start=(dc == 0), stop=(dc == DC - 1),
                                         waits=w if dc == 0 else ())
                                tok[("gu_mmt", r, q4, j, ib, gu)] = p.mm_inc()
                            sf = stage_free[ib % 2]
                            w = [("pe", tok[("gu_mmt", r, q4, j, ib, 0)])]
                            if sf:
                                w.append(sf)
                            tok[("gu_sil", r, q4, j, ib)] = p.sop(
                                "activation", tsil[ib % 2], pb[gbank][:], AF.Silu,
                                waits=w)
                            w = [("pe", tok[("gu_mmt", r, q4, j, ib, 1)]),
                                 ("sc", tok[("gu_sil", r, q4, j, ib)])]
                            tok[("gu_tu", r, q4, j, ib)] = p.vop(
                                "tensor_tensor", tuu[ib % 2], pb[ubank][:],
                                cbc[:, j * 512:(j + 1) * 512], ALU.mult, waits=w)
                            w = [("v", tok[("gu_tu", r, q4, j, ib)])]
                            if prevq:
                                w.append(("pe", tok[("dn_mm", prevq[0], prevq[1], 1)]))
                            hv = p.gop(
                                "tensor_tensor",
                                hq[:, (j * IB + ib) * 512:(j * IB + ib + 1) * 512],
                                tsil[ib % 2], tuu[ib % 2], ALU.mult, waits=w)
                            tok[("gu_hv", r, q4, j, ib)] = hv
                            stage_free[ib % 2] = ("gpc", hv)
                            bankfree[gbank] = ("gpc", hv)
                            bankfree[ubank] = ("gpc", hv)
                        tok[("gu_3rd_done", wgu_n)] = tok[("gu_mmt", r, q4, j, ib, 1)]
                tok[("gu_mm_last", r)] = tok[("gu_mmt", r, nquads - 1, 3, IB - 1, 1)] \
                    if q4 == nquads - 1 else tok.get(("gu_mm_last", r), 0)
                for tp in range(2):
                    for j in range(4):
                        e = q4 * 4 + j
                        wdi = wd_n % 2
                        wd_n += 1
                        w = []
                        if wd_n >= 3:
                            w.append(("pe", tok[("dn_grp", wd_n - 2)]))
                        if r == 0:
                            w += tok[("conv_done",)]
                        tok[("wd_ld", wd_n)] = p.dma(
                            "sy", f"wd{wdi}", wdr[wdi],
                            wff_sc(e)[:, WDOFF:WDOFF + WDLEN], waits=w)
                        for ib in range(IB):
                            w0 = [tok[("wd_ld", wd_n)]]
                            if ib == 0:
                                w0.append(("gpc", tok[("gu_hv", r, q4, j, IB - 1)]))
                            if j == 0 and ib == 0:
                                if tp == 0:
                                    w0.append(("sc", tok[("cb_ev", r, q4, 3)]))
                                    w0.append(("gpc", tok[("gu_hv", r, q4, 3, IB - 1)]))
                                    if prevq is None:
                                        w0.append(a4ev_last)
                                else:
                                    w0.append(("v", tok[("dn_ev", r, q4, 0)]))
                            for t4 in (tp * 2, tp * 2 + 1):
                                for db in range(DB):
                                    p.mm(pb[(t4 - tp * 2) * 4 + db][:],
                                         hq[:, (j * IB + ib) * 512 + t4 * 128:
                                            (j * IB + ib) * 512 + (t4 + 1) * 128],
                                         wdr[wdi][:, ib * 2048 + db * 512:
                                                  ib * 2048 + (db + 1) * 512],
                                         start=(j == 0 and ib == 0),
                                         stop=(j == 3 and ib == IB - 1),
                                         waits=w0 if (t4 == tp * 2 and db == 0) else ())
                        tok[("dn_grp", wd_n)] = p.mm_inc()
                    tok[("dn_mm", r, q4, tp)] = tok[("dn_grp", wd_n)]
                    w = [("pe", tok[("dn_mm", r, q4, tp)])]
                    if q4 == 0 and tp == 0:
                        if r == 0:
                            w += s4_toks
                        else:
                            w.append(("sc", tok[("fin_g", r - 1, TB4 - 1)]))
                    first = True
                    for t4 in (tp * 2, tp * 2 + 1):
                        for db in range(DB):
                            dst = yac[:, t4 * 2048 + db * 512: t4 * 2048 + (db + 1) * 512]
                            last = (t4 == tp * 2 + 1 and db == DB - 1)
                            if q4 == 0:
                                p.vop("tensor_copy", dst,
                                      pb[(t4 - tp * 2) * 4 + db][:],
                                      waits=w if first else (), inc=last)
                            else:
                                p.vop("tensor_tensor", dst,
                                      pb[(t4 - tp * 2) * 4 + db][:], dst, ALU.add,
                                      waits=w if first else (), inc=last)
                            first = False
                    tok[("dn_ev", r, q4, tp)] = p.c["v"]
                bankfree[0] = bankfree[1] = bankfree[2] = bankfree[3] = \
                    ("v", tok[("dn_ev", r, q4, 1)])
        fin_silu(p, tok, NR_RUN - 1)
        for t4 in range(TB4):
            fin_load(p, tok, NR_RUN - 1, t4)
            fin_prod(p, tok, NR_RUN - 1, t4)
        return

    def fin_silu(p: P, tok, r):
        for t4 in range(TB4):
            w = [("v", tok[("dn_ev", r, nquads - 1, 1)])]
            if r >= 1:
                w.append(("v", tok[("fin_t", r - 1, t4)]))
            tok[("fin_g", r, t4)] = p.sop(
                "activation", env["fb_g"][t4],
                env["yac"][:, t4 * 2048:(t4 + 1) * 2048], AF.Silu, waits=w)

    def fin_load(p: P, tok, r, t4):
        gtb = r * 4 + t4
        w = [("sc", tok[("fin_g", r, t4)])] + env["s4_toks"]
        if t4 >= 1:
            w.append(("v", tok[("fin_t", r, t4 - 1)]))
        elif r >= 1:
            w.append(("v", tok[("fin_t", r - 1, TB4 - 1)]))
        if t4 >= 1:
            w.append(tok[("out_st", r, t4 - 1)])
        elif r >= 1:
            w.append(tok[("out_st", r - 1, TB4 - 1)])
        tok[("fin_sld", r, t4)] = p.dma("gp", "fsg", env["fb_s"],
                                        s_sc[gtb], waits=w)
        tok[("fin_xld", r, t4)] = p.dma(
            "gp", "fsg", env["fb_x"], x_d[gtb * 128:(gtb + 1) * 128, :])

    def fin_prod(p: P, tok, r, t4):
        gtb = r * 4 + t4
        p.vop("tensor_tensor", env["fb_g"][t4], env["fb_s"], env["fb_g"][t4],
              ALU.mult, waits=[tok[("fin_xld", r, t4)],
                               ("sc", tok[("fin_g", r, t4)])],
              drain=True, inc=False)
        tok[("fin_t", r, t4)] = p.vop("tensor_tensor", env["fb_x"],
                                      env["fb_g"][t4], env["fb_x"],
                                      ALU.add, drain=True)
        tok[("out_st", r, t4)] = p.dma(
            "scd", "os", out_d[gtb * 128:(gtb + 1) * 128, :], env["fb_x"],
            waits=[("v", tok[("fin_t", r, t4)])])

    def _at_slice(Aap_, fc, ec):
        o = fc * 2048 + ec * 128
        return Aap_[:, o:o + 128]

    def finish(p: P, tok):
        if dump:
            dbuf = Bf[:, 0:4096]
            dbufb = Bap[:, 8192:12288]
            bw = [(k, v) for k, v in p.c.items() if v > 0]
            first = True
            prev_tt = None
            t2 = None
            tt = None
            for nm in dump:
                if nm == "cT":
                    w = list(bw) if first else []
                    if tt is not None:
                        w.append(tt)
                    if t2 is not None:
                        w.append(("v", t2))
                    t = p.vop("tensor_copy", Bf[0:E, 0:TCORE], cT[:, :],
                              waits=w, drain=True)
                    first = False
                    prev_tt = p.dma("scd", "du2", dump_d[nm], Bf[0:E, 0:TCORE],
                                    waits=[("v", t)])
                    p.wait("scalar", *prev_tt)
                    t2 = t
                    tt = prev_tt
                    continue
                src, dt, shp = dump_specs[nm]
                rows, cols = shp
                nrt = rows // 128
                nct = (cols + 4095) // 4096
                for rt in range(nrt):
                    for ct in range(nct):
                        cw = min(4096, cols - ct * 4096)
                        sview = src[rt * 128:(rt + 1) * 128,
                                    ct * 4096:ct * 4096 + cw]
                        w = list(bw) if first else []
                        if t2 is not None:
                            w.append(("v", t2))
                        if tt is not None:
                            w.append(tt)
                        if dt == BF16:
                            t1 = p.dma("sy", "du1", dbufb[:, 0:cw], sview, waits=w)
                            t2 = p.vop("tensor_copy", dbuf[:, 0:cw],
                                       dbufb[:, 0:cw], waits=[t1],
                                       drain=True)
                        else:
                            t1 = p.dma("sy", "du1", dbuf[:, 0:cw], sview, waits=w)
                            t2 = p.vop("tensor_copy", dbuf[:, 0:cw],
                                       dbuf[:, 0:cw], waits=[t1],
                                       drain=True)
                        first = False
                        tt = p.dma("scd", "du2",
                                   dump_d[nm][rt * 128:(rt + 1) * 128,
                                              ct * 4096:ct * 4096 + cw],
                                   dbuf[:, 0:cw], waits=[("v", t2)])
                        p.wait("scalar", *tt)
                        prev_tt = tt
        for k, v in p.c.items():
            if v > 0:
                p.wait("sync", k, v)

    blk = ctx.enter_context(nc.Block())

    @blk.sync
    def _(eng):
        walk(P("sync", eng, sems))

    @blk.gpsimd
    def _(eng):
        walk(P("gpsimd", eng, sems))

    @blk.tensor
    def _(eng):
        walk(P("tensor", eng, sems))

    @blk.vector
    def _(eng):
        walk(P("vector", eng, sems))

    @blk.scalar
    def _(eng):
        walk(P("scalar", eng, sems))

    ctx.close()
    return nc


def make_consts():
    idf = np.eye(128, dtype=np.float32)
    idb = np.eye(128, dtype=ml_dtypes.bfloat16)
    oh = np.zeros((E, E * 128), dtype=ml_dtypes.bfloat16)
    for e in range(E):
        oh[e, e * 128:(e + 1) * 128] = 1
    return idf, idb, oh


def make_in_maps(L0_attn_x, L0_attn_Wq, L0_attn_Wk, L0_attn_Wv, L0_attn_Wo,
                 L0_ffn_gate_w, L0_ffn_w_gate, L0_ffn_w_up, L0_ffn_w_down):
    x = np.asarray(L0_attn_x, dtype=np.float32).reshape(-1, D)
    idf, idb, oh = make_consts()
    shared = {
        "wq": np.ascontiguousarray(np.asarray(L0_attn_Wq, np.float32)),
        "wk": np.ascontiguousarray(np.asarray(L0_attn_Wk, np.float32)),
        "wv": np.ascontiguousarray(np.asarray(L0_attn_Wv, np.float32)),
        "wo": np.ascontiguousarray(np.asarray(L0_attn_Wo, np.float32)),
        "gw": np.ascontiguousarray(np.asarray(L0_ffn_gate_w, np.float32)),
        "wg": np.ascontiguousarray(np.asarray(L0_ffn_w_gate, np.float32)),
        "wu": np.ascontiguousarray(np.asarray(L0_ffn_w_up, np.float32)),
        "wd": np.ascontiguousarray(np.asarray(L0_ffn_w_down, np.float32)),
        "idf": idf, "idb": idb, "oh": oh,
    }
    return [dict(shared, x=np.ascontiguousarray(x[c * TCORE:(c + 1) * TCORE]))
            for c in range(8)]


def kernel(L0_attn_x, L0_attn_Wq, L0_attn_Wk, L0_attn_Wv, L0_attn_Wo,
           L0_ffn_gate_w, L0_ffn_w_gate, L0_ffn_w_up, L0_ffn_w_down):
    in_maps = make_in_maps(L0_attn_x, L0_attn_Wq, L0_attn_Wk, L0_attn_Wv,
                           L0_attn_Wo, L0_ffn_gate_w, L0_ffn_w_gate,
                           L0_ffn_w_up, L0_ffn_w_down)
    nc = build_nc()
    res = run_bass_kernel_spmd(nc, in_maps, core_ids=list(range(8)))
    out = np.concatenate([res.results[c]["out"] for c in range(8)], axis=0)
    return out.reshape(2, 16384, D)


# version 08
# speedup vs baseline: 1.1027x; 1.0040x over previous
import numpy as np
import ml_dtypes
import concourse.bass as bass
import concourse.mybir as mybir
from concourse.bass_utils import run_bass_kernel_spmd

F32 = mybir.dt.float32
BF16 = mybir.dt.bfloat16
AF = mybir.ActivationFunctionType
ALU = mybir.AluOpType
AX = mybir.AxisListType

D = 2048
TCORE = 4096
NT = TCORE // 128
DC = 16
E = 32
I = 768
IB = 6
RNG = 512
NRANGE = TCORE // RNG
TB4 = RNG // 128
DB = 4
NQUAD = 8
RG = [[0, 1, 2, 3], [4, 5, 6, 7]]

WHALF = DC * 384
WGU_H = 2 * WHALF
WDOFF = 2 * WGU_H
WDLEN = IB * DB * 512
WCOLS = WDOFF + WDLEN


class P:
    def __init__(self, name, eng, sems):
        self.name = name
        self.eng = eng
        self.sems = sems
        self.c = {k: 0 for k in sems}
        self.last_mm = None
        self.maxw = {}

    def _emit_waits(self, waits):
        for (wq, wv) in waits:
            assert wv <= self.c[wq], (self.name, wq, wv, self.c[wq])
            if self.name == "tensor" and wq == "pe":
                continue
            if wv > self.maxw.get(wq, 0):
                self.maxw[wq] = wv
                self.eng.wait_ge(self.sems[wq], wv)

    def wait(self, who, q, val):
        assert val <= self.c[q], (who, q, val, self.c[q])
        if self.name == who:
            self._emit_waits([(q, val)])

    def dma(self, q, slot, out, in_, waits=(), **kw):
        eng_for = {"sy": "sync", "gp": "gpsimd", "scd": "scalar"}[q]
        prev = self.c[slot]
        self.c[slot] += 16
        if self.name == eng_for:
            self._emit_waits(waits)
            if q == "gp" and prev > 0:
                self._emit_waits([(slot, prev)])
            self.eng.dma_start(out=out, in_=in_, **kw).then_inc(self.sems[slot], 16)
        return (slot, self.c[slot])

    def mm(self, out, lhsT, rhs, start, stop, waits=()):
        if self.name == "tensor":
            self._emit_waits(waits)
            self.last_mm = self.eng.matmul(out, lhsT, rhs, start=start, stop=stop)

    def mm_inc(self):
        self.c["pe"] += 1
        if self.name == "tensor":
            self.last_mm.then_inc(self.sems["pe"], 1)
        return self.c["pe"]

    def transpose(self, out, in_, ident, waits=()):
        if self.name == "tensor":
            self._emit_waits(waits)
            self.last_mm = self.eng.transpose(out, in_, ident)
        return self.mm_inc()

    def vop(self, fn, *a, waits=(), drain=False, inc=True, **kw):
        if inc:
            self.c["v"] += 1
        if self.name == "vector":
            self._emit_waits(waits)
            if drain:
                self.eng.drain()
            ins = getattr(self.eng, fn)(*a, **kw)
            if inc:
                ins.then_inc(self.sems["v"], 1)
        return self.c["v"]

    def sop(self, fn, *a, waits=(), drain=False, inc=True, **kw):
        if inc:
            self.c["sc"] += 1
        if self.name == "scalar":
            self._emit_waits(waits)
            if drain:
                self.eng.drain()
            ins = getattr(self.eng, fn)(*a, **kw)
            if inc:
                ins.then_inc(self.sems["sc"], 1)
        return self.c["sc"]

    def gop(self, fn, *a, waits=(), inc=True, **kw):
        if inc:
            self.c["gpc"] += 1
        if self.name == "gpsimd":
            self._emit_waits(waits)
            ins = getattr(self.eng, fn)(*a, **kw)
            if inc:
                ins.then_inc(self.sems["gpc"], 1)
        return self.c["gpc"]

    def cc(self, kind, op, ins_, outs_, waits=()):
        self.c["cc"] += 1
        if self.name == "gpsimd":
            self._emit_waits(waits)
            self.eng.collective_compute(
                kind, op, replica_groups=RG, ins=ins_, outs=outs_
            ).then_inc(self.sems["cc"], 1)
        return self.c["cc"]


def build_nc(upto="full", dump=(), nranges=NRANGE, skip_attn=False, detect_races=True, nquads=NQUAD, lite=frozenset()):
    nc = bass.Bass("TRN2", target_bir_lowering=False, num_devices=8,
                   detect_race_conditions=detect_races)

    x_d = nc.dram_tensor("x", [TCORE, D], F32, kind="ExternalInput").ap()
    wq_d = nc.dram_tensor("wq", [D, D], F32, kind="ExternalInput").ap()
    wk_d = nc.dram_tensor("wk", [D, D], F32, kind="ExternalInput").ap()
    wv_d = nc.dram_tensor("wv", [D, D], F32, kind="ExternalInput").ap()
    wo_d = nc.dram_tensor("wo", [D, D], F32, kind="ExternalInput").ap()
    gw_d = nc.dram_tensor("gw", [E, D], F32, kind="ExternalInput").ap()
    wg_d = nc.dram_tensor("wg", [E, D, I], F32, kind="ExternalInput").ap()
    wu_d = nc.dram_tensor("wu", [E, D, I], F32, kind="ExternalInput").ap()
    wd_d = nc.dram_tensor("wd", [E, I, D], F32, kind="ExternalInput").ap()
    idf_d = nc.dram_tensor("idf", [128, 128], F32, kind="ExternalInput").ap()
    idb_d = nc.dram_tensor("idb", [128, 128], BF16, kind="ExternalInput").ap()
    oh_d = nc.dram_tensor("oh", [E, E * 128], BF16, kind="ExternalInput").ap()
    out_d = nc.dram_tensor("out", [TCORE, D], F32, kind="ExternalOutput").ap()

    xn_sc = nc.dram_tensor("xn_sc", [NT, 128, D], BF16).ap()
    q_sc = nc.dram_tensor("q_sc", [128, NT, D], BF16).ap()
    k_sc = nc.dram_tensor("k_sc", [128, NT, D], BF16).ap()
    v_sc = nc.dram_tensor("v_sc", [128, DC, TCORE], BF16).ap()
    att_sc = nc.dram_tensor("att_sc", [128, DC, TCORE], BF16).ap()
    s_sc = nc.dram_tensor("s_sc", [NT, 128, D], F32).ap()
    scc_in = nc.dram_tensor("scc_in", [D, D], F32).ap()
    scc_out = nc.dram_tensor("scc_out", [512, D], F32).ap()
    agg_in = nc.dram_tensor("agg_in", [512, D], BF16).ap()
    agg_out = nc.dram_tensor("agg_out", [D, D], BF16).ap()
    wff_a = nc.dram_tensor("wff_a", [E // 2, 128, WCOLS], BF16).ap()
    wff_b = nc.dram_tensor("wff_b", [E // 2, 128, WCOLS], BF16).ap()
    wff_sc = lambda e: (wff_a[e] if e < E // 2 else wff_b[e - E // 2])

    dump_d = {}
    dump_specs = {
        "xn_sc": (xn_sc.rearrange("a p d -> (a p) d"), BF16, [NT * 128, D]),
        "q_sc": (q_sc.rearrange("p c t -> p (c t)"), BF16, [128, NT * D]),
        "k_sc": (k_sc.rearrange("p c t -> p (c t)"), BF16, [128, NT * D]),
        "v_sc": (v_sc.rearrange("p c t -> p (c t)"), BF16, [128, DC * TCORE]),
        "att_sc": (att_sc.rearrange("p c t -> p (c t)"), BF16, [128, DC * TCORE]),
        "s_sc": (s_sc.rearrange("a p d -> (a p) d"), F32, [NT * 128, D]),
        "scc_in": (scc_in, F32, [D, D]),
        "agg_out": (agg_out, BF16, [D, D]),
        "wff0": (wff_sc(0), BF16, [128, WCOLS]),
    }
    for nm in dump:
        dump_d[nm] = nc.dram_tensor("dump_" + nm, dump_specs[nm][2] if nm != "cT"
                                    else [E, TCORE], F32, kind="ExternalOutput").ap()

    from contextlib import ExitStack
    ctx = ExitStack()
    sb = lambda n, s, d: ctx.enter_context(nc.sbuf_tensor(n, s, d))
    psa = lambda n, s, d: ctx.enter_context(nc.psum_tensor(n, s, d))
    mksem = lambda n: ctx.enter_context(nc.semaphore(n))

    A = sb("A", [128, 61440], BF16)
    Aap = A.ap()
    B = sb("B", [128, 30720], BF16)
    Bap = B.ap()
    Bf = Bap.bitcast(F32)
    C = sb("C", [128, 6144], BF16)
    Cap = C.ap()
    Cf = Cap.bitcast(F32)

    identf = sb("identf", [128, 128], F32)
    identb = sb("identb", [128, 128], BF16)
    gwT = sb("gwT", [128, DC * E], F32)
    cT = sb("cT", [E, TCORE], BF16)
    oh_b = sb("oh_b", [E, 512], BF16)
    gw_sb = None
    stat = sb("stat", [128, 16], F32)
    ls = sb("ls", [128, E], F32)
    elb = sb("elb", [128, E], F32)
    vmax8 = sb("vmax8", [128, 8], F32)
    zap = sb("zap", [128, E], F32)
    selm = sb("selm", [128, E], F32)
    e8 = sb("e8", [128, 8], F32)
    ctile = sb("ctile", [128, E], F32)

    pb = [psa(f"pb{i}", [128, 512], F32) for i in range(8)]
    pbb = [pb[i].ap().bitcast(BF16) for i in range(8)]

    DMA_SLOTS = ["cst", "xl0", "xl1", "xs0", "xs1", "ax0", "ax1", "aq0", "aq1",
                 "wc", "bx0", "bx1", "bv0", "bv1", "kk", "qq0", "qq1",
                 "ss0", "ss1", "sa0", "sa1", "sx0", "sx1", "ag0", "ag1", "ab0", "ab1", "vv0", "vv1", "as0",
                 "as1", "ac0", "ac1", "s40", "s41", "xr0", "xr1", "oh1", "wg0",
                 "wg1", "wd0", "wd1", "wd2", "wd3", "wd4", "wd5",
                 "cva0", "cva1", "cvb0", "cvb1", "cw0", "cw1", "fs", "fsg", "os", "du1", "du2"]
    sems = {k: mksem(k) for k in ["pe", "v", "sc", "gpc", "cc"] + DMA_SLOTS}

    SUM, NM, NM2, VAR, SSQ, LNV, RSTD, BIAS, NM8, S8, RS8, EPS = range(12)
    st = lambda i: stat[:, i:i + 1]

    env = {}
    gw_sb = Bf[0:E, 13312:15360]

    def walk(p: P):
        tok = {}

        t_eps = p.vop("memset", st(EPS), 1e-5)
        t_idf = p.dma("sy", "cst", identf[:], idf_d)
        t_idb = p.dma("sy", "cst", identb[:], idb_d)
        t_gw = p.dma("sy", "cst", gw_sb[:], gw_d)
        ev_gwT = 0
        for dc in range(DC):
            bank = dc % 2
            w = [t_gw, t_idf]
            if dc >= 2:
                w.append(("v", ev_gwT))
            tk = p.transpose(pb[bank][:, 0:E],
                             gw_sb[:, dc * 128:(dc + 1) * 128],
                             identf[0:E, 0:E], waits=w)
            ev_gwT = p.vop("tensor_copy", gwT[:, dc * E:(dc + 1) * E],
                           pb[bank][:, 0:E], waits=[("pe", tk)])

        if upto != "prologue" and not skip_attn:
            wqh0 = Aap[:, 0:16384]
            wkh0 = Aap[:, 16384:32768]
            for (wsrc, dstbuf) in ((wq_d, wqh0), (wk_d, wkh0)):
                for piece in range(2):
                    s_ = wsrc.rearrange("(c p) f -> p c f", p=128)[
                        :, piece * 8:(piece + 1) * 8, 0:1024]
                    d_ = dstbuf.rearrange("p (c f) -> p c f", f=1024)[
                        :, piece * 8:(piece + 1) * 8, :]
                    tok[("a1a_w0",)] = p.dma("gp", "wc", d_, s_)

        xbuf = [Bf[:, 0:2048], Bf[:, 2048:4096]]
        xnf = [Bf[:, 4096:6144], Bf[:, 6144:8192]]
        sqb = Bf[:, 8192:10240]
        xnTf = [Bf[:, 10240:12288], Bf[:, 12288:14336]]
        xnTb = [Cap[:, 0:2048], Cap[:, 2048:4096]]

        for tb in range(NT):
            pg = tb % 2
            w = []
            if tb >= 2:
                w += [("v", tok[("xnf", tb - 2)]), ("sc", tok[("ssq", tb - 2)])]
            t_x = p.dma("sy", f"xl{pg}", xbuf[pg],
                        x_d[tb * 128:(tb + 1) * 128, :], waits=w)
            tok[("rsum", tb)] = p.vop("reduce_sum", st(SUM), xbuf[pg],
                                      waits=[t_x], axis=AX.X)
            w = [t_x]
            if tb >= 1:
                w.append(("sc", tok[("ssq", tb - 1)]))
            tok[("ssq", tb)] = p.sop("activation", sqb, xbuf[pg], AF.Square,
                                     waits=w, accum_out=st(SSQ))
            p.vop("tensor_scalar_mul", st(NM), st(SUM), -1.0 / D, drain=True,
                  inc=False)
            p.vop("tensor_mul", st(NM2), st(NM), st(NM), drain=True, inc=False)
            p.vop("tensor_scalar_mul", st(VAR), st(SSQ), 1.0 / D,
                  waits=[("sc", tok[("ssq", tb)])], drain=True, inc=False)
            tok[("var", tb)] = p.vop("tensor_sub", st(VAR), st(VAR), st(NM2),
                                     drain=True)
            p.sop("activation", st(LNV), st(VAR), AF.Ln, bias=st(EPS),
                  waits=[("v", tok[("var", tb)])], inc=False)
            tok[("rstd", tb)] = p.sop("activation", st(RSTD), st(LNV), AF.Exp,
                                      scale=-0.5, drain=True)
            w = [("sc", tok[("rstd", tb)])]
            if tb >= 2:
                w.append(("pe", tok[("tr", tb - 2)]))
            tok[("xnf", tb)] = p.vop("tensor_scalar", xnf[pg], xbuf[pg],
                                     st(NM), st(RSTD), ALU.add, ALU.mult,
                                     waits=w, drain=True)
            for dc in range(DC):
                bank = dc % 2
                w = [("v", tok[("xnf", tb)])]
                key = ("trev", tb, dc - 2) if dc >= 2 else \
                      (("trev", tb - 1, DC - 2 + dc) if tb >= 1 else None)
                if key:
                    w.append(("v", tok[key]))
                tk = p.transpose(pb[bank][:, 0:128],
                                 xnf[pg][:, dc * 128:(dc + 1) * 128],
                                 identf[:], waits=w)
                if dc == DC - 1:
                    tok[("tr", tb)] = tk
                w = [("pe", tk)]
                if dc == 0 and tb >= 2:
                    w += [("pe", tok[("rt", tb - 2)]), ("sc", tok[("xnTb", tb - 2)])]
                tok[("trev", tb, dc)] = p.vop(
                    "tensor_copy", xnTf[pg][:, dc * 128:(dc + 1) * 128],
                    pb[bank][:, 0:128], waits=w)
            w = [("v", tok[("trev", tb, DC - 1)])]
            if tb >= 2:
                w.append(tok[("xnst", tb - 2)])
            tok[("xnTb", tb)] = p.sop("copy", xnTb[pg], xnTf[pg], waits=w)
            tok[("xnst", tb)] = p.dma(
                "sy", f"xs{pg}", xn_sc[tb], xnTb[pg],
                waits=[("sc", tok[("xnTb", tb)])])
            for dc in range(DC):
                w = [("v", tok[("trev", tb, dc)])]
                if dc == 0 and tb >= 1:
                    w.append(("v", tok[("lg", tb - 1)]))
                p.mm(pb[2][:, 0:E], xnTf[pg][:, dc * 128:(dc + 1) * 128],
                     gwT[:, dc * E:(dc + 1) * E],
                     start=(dc == 0), stop=(dc == DC - 1), waits=w)
            tok[("rt", tb)] = p.mm_inc()
            w = [("pe", tok[("rt", tb)])]
            if tb >= 1:
                w.append(("sc", tok[("el", tb - 1)]))
            tok[("lg", tb)] = p.vop("tensor_copy", ls[:], pb[2][:, 0:E],
                                    waits=w, drain=True)
            p.vop("max", drain=True, inc=False, out=vmax8[:], in_=ls[:])
            p.vop("match_replace", drain=True, inc=False, out=zap[:],
                  in_to_replace=vmax8[:], in_values=ls[:], imm_value=-1e30)
            p.vop("tensor_tensor", selm[:], ls[:], zap[:], ALU.not_equal,
                  drain=True, inc=False)
            tok[("nm8", tb)] = p.vop("tensor_scalar_mul", st(NM8), vmax8[:, 0:1],
                                     -1.0, drain=True)
            tok[("e8", tb)] = p.sop("activation", e8[:], vmax8[:], AF.Exp,
                                    bias=st(NM8), waits=[("v", tok[("nm8", tb)])])
            p.vop("reduce_sum", st(S8), e8[:], axis=AX.X,
                  waits=[("sc", tok[("e8", tb)])], drain=True, inc=False)
            tok[("rs8", tb)] = p.vop("reciprocal", st(RS8), st(S8), drain=True)
            tok[("el", tb)] = p.sop("activation", elb[:], ls[:], AF.Exp,
                                    bias=st(NM8))
            w = [("sc", tok[("el", tb)]), ("v", tok[("rs8", tb)])]
            if tb >= 1:
                w.append(("v", tok[("ctev", tb - 1)]))
            tok[("c", tb)] = p.vop("scalar_tensor_tensor", waits=w, drain=True,
                                   out=ctile[:], in0=elb[:], scalar=st(RS8),
                                   in1=selm[:], op0=ALU.mult, op1=ALU.mult)
            tk = p.transpose(pb[3][0:E, 0:128], ctile[:], identf[:],
                             waits=[("v", tok[("c", tb)])])
            tok[("ctev", tb)] = p.vop("tensor_copy", cT[:, tb * 128:(tb + 1) * 128],
                                      pb[3][0:E, 0:128], waits=[("pe", tk)])
            if upto != "prologue":
                conv_step(p, tok, 0.30 * (tb + 1) / NT)

        env.update(locals())
        if upto == "prologue":
            finish(p, tok)
            return

        if skip_attn:
            tok[("a4_mm", NT - 1)] = tok[("rt", NT - 1)]
            tok[("a4_ev", NT - 1)] = tok[("ctev", NT - 1)]
            tok[("a4_st", NT - 1)] = tok[("xnst", NT - 1)]
            tok[("a4_st", NT - 2)] = tok[("xnst", NT - 2)]
            ffn(p, tok)
            finish(p, tok)
            return

        wqh = Aap[:, 0:16384]
        wkh = Aap[:, 16384:32768]
        xnt = [Aap[:, 32768 + i * 2048: 32768 + (i + 1) * 2048] for i in range(2)]
        qkst = [Cap[:, i * 1024:(i + 1) * 1024] for i in range(4)]

        for half in range(2):
            if half == 0:
                t_w = tok[("a1a_w0",)]
            else:
                for (wsrc, dstbuf) in ((wq_d, wqh), (wk_d, wkh)):
                    for piece in range(2):
                        src = wsrc.rearrange("(c p) f -> p c f", p=128)[
                            :, piece * 8:(piece + 1) * 8,
                            half * 1024:(half + 1) * 1024]
                        dgt = dstbuf.rearrange("p (c f) -> p c f", f=1024)[
                            :, piece * 8:(piece + 1) * 8, :]
                        t_w = p.dma("gp", "wc", dgt, src,
                                    waits=[("pe", tok[("a1a_mm", 0, NT - 1)])])
            for tb in range(NT):
                pg = tb % 2
                w = [tok[("xnst", tb)]]
                prev = ("a1a_mm", half, tb - 2) if tb >= 2 else \
                       (("a1a_mm", half - 1, NT - 2 + tb) if half >= 1 else None)
                if prev:
                    w.append(("pe", tok[prev]))
                t_xn = p.dma("sy", f"ax{pg}", xnt[pg], xn_sc[tb], waits=w)
                bs = pg * 4
                for dc in range(DC):
                    w = [t_w, t_xn]
                    if dc == 0:
                        if half == 0 and tb == 0:
                            w.append(("v", tok[("ctev", NT - 1)]))
                        prev = ("a1a_ev", half, tb - 2) if tb >= 2 else \
                               (("a1a_ev", half - 1, NT - 2 + tb) if half >= 1 else None)
                        if prev:
                            w.append(("v", tok[prev]))
                    lb0 = 32768 + pg * 2048 + dc * 128
                    lhs = Aap[:, lb0:lb0 + 128]
                    first = True
                    for (wo0, bo) in ((0, 0), (16384, 2)):
                        for fb in range(2):
                            rb0 = wo0 + dc * 1024 + fb * 512
                            p.mm(pb[bs + bo + fb][:], lhs,
                                 Aap[:, rb0:rb0 + 512],
                                 start=(dc == 0), stop=(dc == DC - 1),
                                 waits=w if (dc == 0 and first) else ())
                            first = False
                tok[("a1a_mm", half, tb)] = p.mm_inc()
                w = [("pe", tok[("a1a_mm", half, tb)])]
                prev = ("a1a_st", half, tb - 2) if tb >= 2 else \
                       (("a1a_st", half - 1, NT - 2 + tb) if half >= 1 else None)
                if prev:
                    w.append(tok[prev])
                if half == 0 and tb < 2:
                    w += [tok[("xnst", NT - 1)], tok[("xnst", NT - 2)]]
                p.vop("tensor_copy", qkst[pg * 2][:, 0:512], pb[bs + 0][:],
                      waits=w, inc=False)
                p.vop("tensor_copy", qkst[pg * 2][:, 512:1024], pb[bs + 1][:],
                      inc=False)
                p.vop("tensor_copy", qkst[pg * 2 + 1][:, 0:512], pb[bs + 2][:],
                      inc=False)
                tok[("a1a_ev", half, tb)] = p.vop(
                    "tensor_copy", qkst[pg * 2 + 1][:, 512:1024], pb[bs + 3][:])
                p.dma("scd", f"aq{pg}",
                      q_sc[:, tb, half * 1024:(half + 1) * 1024],
                      qkst[pg * 2], waits=[("v", tok[("a1a_ev", half, tb)])])
                tok[("a1a_st", half, tb)] = p.dma(
                    "scd", f"aq{pg}",
                    k_sc[:, tb, half * 1024:(half + 1) * 1024],
                    qkst[pg * 2 + 1])
                conv_step(p, tok, 0.30 + 0.25 * (half * NT + tb + 1) / (2 * NT))

        if upto == "a1a":
            finish(p, tok)
            return

        wvh = Aap[:, 0:16384]
        xn5 = [Aap[:, 16384 + i * 8192: 16384 + (i + 1) * 8192] for i in range(2)]
        vst = [Cap[:, i * 512:(i + 1) * 512] for i in range(8)]
        for half in range(2):
            tw = []
            for piece in range(2):
                src = wv_d.rearrange("(c p) (b f) -> p c b f", p=128, f=128)[
                    :, piece * 8:(piece + 1) * 8, half * 8:(half + 1) * 8, :]
                dgt = wvh.rearrange("p (c b f) -> p c b f", b=8, f=128)[
                    :, piece * 8:(piece + 1) * 8, :, :]
                w = []
                if half == 0:
                    w.append(("pe", tok[("a1a_mm", 1, NT - 1)]))
                else:
                    w.append(("pe", tok[("a1b_mm", 0, 7)]))
                tw.append(p.dma("gp", "wc", dgt, src, waits=w))
            t_w = tw[-1]
            for t5 in range(8):
                pg = t5 % 2
                w = [tok[("xnst", NT - 1)], tok[("xnst", NT - 2)]]
                prev = ("a1b_mm", half, t5 - 2) if t5 >= 2 else \
                       (("a1b_mm", half - 1, 6 + t5) if half >= 1 else None)
                if prev:
                    w.append(("pe", tok[prev]))
                if half == 0 and t5 < 2:
                    w.append(("pe", tok[("a1a_mm", 1, NT - 1)]))
                for i4 in range(4):
                    t_xn = p.dma(
                        "sy", f"bx{pg}",
                        xn5[pg].rearrange("p (c t) -> p c t", t=512)[
                            :, :, i4 * 128:(i4 + 1) * 128],
                        xn_sc[t5 * 4 + i4].rearrange("p (c t) -> p c t", t=128),
                        waits=w if i4 == 0 else ())
                for dc in range(DC):
                    for fb in range(8):
                        w = [t_w, t_xn]
                        if dc == 0:
                            key = ("a1b_ev", half, t5 - 1, fb) if t5 >= 1 else \
                                  (("a1b_ev", half - 1, 7, fb) if half >= 1 else None)
                            if key:
                                w.append(("v", tok[key]))
                            if half == 0 and t5 == 0:
                                w.append(("v", tok[("a1a_ev", 1, NT - 1)]))
                        vb0 = (dc * 8 + fb) * 128
                        nb0 = 16384 + pg * 8192 + dc * 512
                        p.mm(pb[fb][:],
                             Aap[:, vb0:vb0 + 128],
                             Aap[:, nb0:nb0 + 512],
                             start=(dc == 0), stop=(dc == DC - 1),
                             waits=w if dc == 0 else ())
                tok[("a1b_mm", half, t5)] = p.mm_inc()
                for fb in range(8):
                    w = [("pe", tok[("a1b_mm", half, t5)])]
                    key = ("a1b_st", half, t5 - 1, 7) if t5 >= 1 else \
                          (("a1b_st", half - 1, 7, 7) if half >= 1 else None)
                    if key:
                        w.append(tok[key])
                    if half == 0 and t5 == 0:
                        w += [tok[("a1a_st", 1, NT - 1)], tok[("a1a_st", 1, NT - 2)]]
                    tok[("a1b_ev", half, t5, fb)] = p.vop(
                        "tensor_copy", vst[fb], pb[fb][:], waits=w)
                    tok[("a1b_st", half, t5, fb)] = p.dma(
                        "sy", f"bv{pg}",
                        v_sc[:, half * 8 + fb, t5 * 512:(t5 + 1) * 512],
                        vst[fb], waits=[("v", tok[("a1b_ev", half, t5, fb)])])
                conv_step(p, tok, 0.55 + 0.10 * (half * 8 + t5 + 1) / 16)

        if upto == "a1b":
            finish(p, tok)
            return

        kb = Aap[:, 0:32768]
        qb = [Aap[:, 32768 + i * 8192: 32768 + (i + 1) * 8192] for i in range(2)]
        sst = [Bf[:, i * 512:(i + 1) * 512] for i in range(8)]
        for kh in range(2):
            w = [tok[("a1b_st", 1, 7, 7)], tok[("a1b_st", 1, 6, 7)],
                 tok[("a1a_st", 1, NT - 1)], tok[("a1a_st", 1, NT - 2)]]
            if kh == 1:
                w.append(("pe", tok[("sc_mm", 0, 3, 3)]))
            t_k = p.dma("sy", "kk", kb.rearrange("p (c f) -> p c f", f=2048),
                        k_sc[:, kh * 16:(kh + 1) * 16, :], waits=w)
            for eq in range(4):
                pg = eq % 2
                w = [tok[("a1a_st", 1, NT - 1)], tok[("a1a_st", 1, NT - 2)]]
                prev = ("sc_mm", kh, eq - 2, 3) if eq >= 2 else \
                       (("sc_mm", kh - 1, 2 + eq, 3) if kh >= 1 else None)
                if prev:
                    w.append(("pe", tok[prev]))
                t_q = p.dma("sy", f"qq{pg}",
                            qb[pg].rearrange("p (c e) -> p c e", e=512),
                            q_sc[:, kh * 16:(kh + 1) * 16, eq * 512:(eq + 1) * 512],
                            waits=w)
                for fb in range(4):
                    bs = ((eq * 4 + fb) % 2) * 4
                    for tc in range(16):
                        w = [t_k, t_q]
                        if tc == 0:
                            key = ("sc_ev", kh, eq, fb - 2) if fb >= 2 else \
                                  (("sc_ev", kh, eq - 1, 2 + fb) if eq >= 1 else
                                   (("sc_ev", kh - 1, 3, 2 + fb) if kh >= 1 else None))
                            if key:
                                w.append(("v", tok[key]))
                            elif fb < 2:
                                w.append(("v", tok[("a1b_ev", 1, 7, 7)]))
                        for eb in range(4):
                            qb0 = 32768 + pg * 8192 + tc * 512 + eb * 128
                            kb0 = tc * 2048 + fb * 512
                            p.mm(pb[bs + eb][:],
                                 Aap[:, qb0:qb0 + 128],
                                 Aap[:, kb0:kb0 + 512],
                                 start=(tc == 0), stop=(tc == 15),
                                 waits=w if (tc == 0 and eb == 0) else ())
                    tok[("sc_mm", kh, eq, fb)] = p.mm_inc()
                    w = [("pe", tok[("sc_mm", kh, eq, fb)])]
                    key = ("sc_fin", kh, eq, fb - 2) if fb >= 2 else \
                          (("sc_fin", kh, eq - 1, 2 + fb) if eq >= 1 else
                           (("sc_fin", kh - 1, 3, 2 + fb) if kh >= 1 else None))
                    if key:
                        w.append(tok[key])
                    if kh == 0 and eq == 0 and fb < 2:
                        w += [tok[("a1b_st", 1, 7, 7)], tok[("a1b_st", 1, 6, 7)]]
                    gpar = (eq * 4 + fb) % 2
                    for eb in range(4):
                        p.vop("tensor_copy", sst[gpar * 4 + eb], pb[bs + eb][:],
                              waits=w if eb == 0 else (), inc=(eb == 3))
                    tok[("sc_ev", kh, eq, fb)] = p.c["v"]
                    for eb in range(4):
                        dst = scc_in[(eq * 4 + eb) * 128:(eq * 4 + eb + 1) * 128,
                                     fb * 512:(fb + 1) * 512]
                        wv = [("v", tok[("sc_ev", kh, eq, fb)])]
                        if kh == 0:
                            tt = p.dma("sy", f"ss{gpar}", dst,
                                       sst[gpar * 4 + eb], waits=wv)
                            tok[("sc_fin", kh, eq, fb)] = tt
                        else:
                            wv += tok[("ss_all")]
                            tt = p.dma("gp", f"sa{gpar}", dst,
                                       sst[gpar * 4 + eb], waits=wv,
                                       accum_op=ALU.add)
                            tok[("sc_fin", kh, eq, fb)] = tt
                    conv_step(p, tok, 0.65 + 0.12 *
                              (kh * 16 + eq * 4 + fb + 1) / 32)
            if kh == 0:
                tok[("ss_all")] = [("ss0", p.c["ss0"]), ("ss1", p.c["ss1"])]

        if upto == "a2":
            finish(p, tok)
            return

        t_rs = p.cc("ReduceScatter", ALU.add, [scc_in], [scc_out],
                    waits=[("ss0", p.c["ss0"]), ("ss1", p.c["ss1"]),
                           ("sa0", p.c["sa0"]), ("sa1", p.c["sa1"])])
        smx = [Bf[:, i * 2048:(i + 1) * 2048] for i in range(2)]
        smy = [Bf[:, 4096 + i * 2048: 4096 + (i + 1) * 2048] for i in range(2)]
        smo = [Bap[:, 16384 + i * 2048: 16384 + (i + 1) * 2048] for i in range(2)]
        for i in range(4):
            pg = i % 2
            w = [("cc", t_rs)]
            if i >= 2:
                w.append(tok[("ag_st", i - 2)])
            t_s = p.dma("sy", f"sx{pg}", smx[pg],
                        scc_out[i * 128:(i + 1) * 128, :], waits=w)
            p.vop("reduce_max", st(SUM), smx[pg], axis=AX.X,
                  waits=[t_s], drain=True, inc=False)
            tok[("smm", i)] = p.vop("tensor_scalar_mul", st(NM), st(SUM), -0.125,
                                    drain=True)
            tok[("sme", i)] = p.sop("activation", smy[pg], smx[pg], AF.Exp,
                                    scale=0.125, bias=st(NM), accum_out=st(S8),
                                    waits=[("v", tok[("smm", i)])])
            tok[("smr", i)] = p.vop("reciprocal", st(RS8), st(S8),
                                    waits=[("sc", tok[("sme", i)])])
            tok[("smA", i)] = p.vop("tensor_scalar_mul", smo[pg], smy[pg], st(RS8),
                                    drain=True)
            tok[("ag_st", i)] = p.dma("scd", f"ag{pg}",
                                      agg_in[i * 128:(i + 1) * 128, :],
                                      smo[pg], waits=[("v", tok[("smA", i)])])
        t_ag = p.cc("AllGather", ALU.bypass, [agg_in], [agg_out],
                    waits=[("ag0", p.c["ag0"]), ("ag1", p.c["ag1"])])

        ATs = Aap[:, 0:32768]
        abuf = [Aap[:, 32768 + i * 2048: 32768 + (i + 1) * 2048] for i in range(2)]
        for ec in range(DC):
            pg = ec % 2
            w = [("cc", t_ag)]
            if ec >= 2:
                w.append(("v", tok[("at_ev", ec - 2, DC - 1)]))
            t_a = p.dma("sy", f"ab{pg}", abuf[pg],
                        agg_out[ec * 128:(ec + 1) * 128, :], waits=w)
            for fc in range(DC):
                bank = fc % 2
                w = [t_a, t_idb]
                key = ("at_ev", ec, fc - 2) if fc >= 2 else \
                      (("at_ev", ec - 1, DC - 2 + fc) if ec >= 1 else None)
                if key:
                    w.append(("v", tok[key]))
                ab0 = 32768 + pg * 2048 + fc * 128
                tk = p.transpose(pbb[bank][:, 0:128],
                                 Aap[:, ab0:ab0 + 128], identb[:],
                                 waits=w)
                tok[("at_ev", ec, fc)] = p.vop(
                    "tensor_copy", _at_slice(Aap, fc, ec),
                    pbb[bank][:, 0:128], waits=[("pe", tk)])
            conv_step(p, tok, 0.77 + 0.03 * (ec + 1) / DC)

        vhb = Aap[:, 36864: 36864 + 16384]
        atst = [Cap[:, i * 512:(i + 1) * 512] for i in range(8)]
        t_at_last = tok[("at_ev", DC - 1, DC - 1)]
        for tq in range(4):
            w = [tok[("a1b_st", 1, 7, 7)], tok[("a1b_st", 1, 6, 7)],
                 ("v", t_at_last)]
            if tq >= 1:
                w.append(("pe", tok[("a3_mm", tq - 1, DC - 1)]))
            t_v = p.dma("sy", f"vv{tq % 2}",
                        vhb.rearrange("p (c t) -> p c t", t=1024),
                        v_sc[:, :, tq * 1024:(tq + 1) * 1024], waits=w)
            for eb in range(DC):
                bs = (eb % 2) * 4
                for fc in range(DC):
                    w = [t_v]
                    if fc == 0:
                        key = ("a3_ev", tq, eb - 2) if eb >= 2 else \
                              (("a3_ev", tq - 1, DC - 2 + eb) if tq >= 1 else None)
                        if key:
                            w.append(("v", tok[key]))
                    for t4 in range(2):
                        vb0 = 36864 + fc * 1024 + t4 * 512
                        p.mm(pb[bs + t4][:],
                             _at_slice(Aap, fc, eb),
                             Aap[:, vb0:vb0 + 512],
                             start=(fc == 0), stop=(fc == DC - 1),
                             waits=w if (fc == 0 and t4 == 0) else ())
                tok[("a3_mm", tq, eb)] = p.mm_inc()
                w = [("pe", tok[("a3_mm", tq, eb)])]
                key = ("a3_stl", tq, eb - 2) if eb >= 2 else \
                      (("a3_stl", tq - 1, DC - 2 + eb) if tq >= 1 else None)
                if key:
                    w.append(tok[key])
                for t4 in range(2):
                    p.vop("tensor_copy", atst[(eb % 2) * 2 + t4], pb[bs + t4][:],
                          waits=w if t4 == 0 else (), inc=(t4 == 1))
                tok[("a3_ev", tq, eb)] = p.c["v"]
                for t4 in range(2):
                    tok[("a3_stl", tq, eb)] = p.dma(
                        "sy", f"as{eb % 2}",
                        att_sc[:, eb, tq * 1024 + t4 * 512: tq * 1024 + (t4 + 1) * 512],
                        atst[(eb % 2) * 2 + t4],
                        waits=[("v", tok[("a3_ev", tq, eb)])])
                conv_step(p, tok, 0.80 + 0.12 * (tq * DC + eb + 1) / (4 * DC))

        wob = Aap[:, 0:32768]
        acol = [Aap[:, 32768 + i * 2048: 32768 + (i + 1) * 2048] for i in range(2)]
        sstg = [Bf[:, i * 2048:(i + 1) * 2048] for i in range(2)]
        tw = []
        for piece in range(2):
            src = wo_d.rearrange("(c p) f -> p c f", p=128)[
                :, piece * 8:(piece + 1) * 8, :]
            dgt = wob.rearrange("p (c f) -> p c f", f=2048)[
                :, piece * 8:(piece + 1) * 8, :]
            tw.append(p.dma("gp", "wc", dgt, src,
                            waits=[("pe", tok[("a3_mm", 3, DC - 1)])]))
        t_wo = tw[-1]
        for tb in range(NT):
            pg = tb % 2
            w = [tok[("a3_stl", 3, DC - 1)], tok[("a3_stl", 3, DC - 2)]]
            if tb >= 2:
                w.append(("pe", tok[("a4_mm", tb - 2)]))
            t_ac = p.dma("sy", f"ac{pg}",
                         acol[pg].rearrange("p (c t) -> p c t", t=128),
                         att_sc[:, :, tb * 128:(tb + 1) * 128], waits=w)
            bs = pg * 4
            for ec in range(DC):
                w = [t_wo, t_ac]
                if ec == 0:
                    if tb < 2:
                        w.append(("v", tok[("a3_ev", 3, DC - 1)]))
                    else:
                        w.append(("v", tok[("a4_ev", tb - 2)]))
                for db in range(4):
                    ac0 = 32768 + pg * 2048 + ec * 128
                    wb0 = ec * 2048 + db * 512
                    p.mm(pb[bs + db][:], Aap[:, ac0:ac0 + 128],
                         Aap[:, wb0:wb0 + 512],
                         start=(ec == 0), stop=(ec == DC - 1),
                         waits=w if (ec == 0 and db == 0) else ())
            tok[("a4_mm", tb)] = p.mm_inc()
            w = [("pe", tok[("a4_mm", tb)])]
            if tb >= 2:
                w.append(tok[("a4_st", tb - 2)])
            for db in range(4):
                p.vop("tensor_copy", sstg[pg][:, db * 512:(db + 1) * 512],
                      pb[bs + db][:], waits=w if db == 0 else (), inc=(db == 3))
            tok[("a4_ev", tb)] = p.c["v"]
            tok[("a4_st", tb)] = p.dma("sy", f"s4{pg}", s_sc[tb], sstg[pg],
                                       waits=[("v", tok[("a4_ev", tb)])])
            conv_step(p, tok, 0.92 + 0.08 * (tb + 1) / NT)

        env.update({k: v for k, v in locals().items() if k != "p"})
        if upto == "attn":
            finish(p, tok)
            return

        ffn(p, tok)
        finish(p, tok)

    CONV_PIECES = []
    for _e in range(E):
        for _ib in range(IB):
            CONV_PIECES.append(("wgu", _e, _ib))
        for _blk in range(3):
            CONV_PIECES.append(("wd", _e, _blk))

    def conv_step(p: P, tok, frac):
        cbnc = [Aap[:, 53248 + i * 4096: 53248 + (i + 1) * 4096] for i in range(2)]
        target = int(len(CONV_PIECES) * min(frac, 1.0) + 1e-6)
        n = tok.get(("conv_n",), 0)
        while n < target:
            kind, e, i = CONV_PIECES[n]
            par = n % 2
            w = []
            if n >= 2:
                w.append(tok[("conv_wb", n - 2)])
            if kind == "wgu":
                ca = p.dma("gp", f"cva{par}",
                           cbnc[par].rearrange("p (c i) -> p c i", i=128)[:, 0:DC, :],
                           wg_d[e].rearrange("(c p) i -> p c i", p=128)[
                               :, :, i * 128:(i + 1) * 128], waits=w)
                cb2 = p.dma("gp", f"cvb{par}",
                            cbnc[par].rearrange("p (c i) -> p c i", i=128)[
                                :, DC:2 * DC, :],
                            wu_d[e].rearrange("(c p) i -> p c i", p=128)[
                                :, :, i * 128:(i + 1) * 128])
                dst = wff_sc(e)[:, i * 4096:(i + 1) * 4096]
            else:
                ca = p.dma("gp", f"cva{par}", cbnc[par],
                           wd_d[e].rearrange("(b p) d -> p b d", p=128)[
                               :, 2 * i:2 * i + 2, :], waits=w)
                cb2 = ca
                dst = wff_sc(e)[:, WDOFF + i * 4096:WDOFF + (i + 1) * 4096]
            tok[("conv_wb", n)] = p.dma("scd", f"cw{par}", dst, cbnc[par],
                                        waits=[ca, cb2])
            n += 1
            tok[("conv_n",)] = n
        return

    def ffn(p: P, tok):
        xnr = [Aap[:, i * 8192:(i + 1) * 8192] for i in range(2)]
        wgu = [Aap[:, 16384 + i * 4096: 16384 + (i + 1) * 4096] for i in range(2)]
        hq = Aap[:, 24576: 24576 + 12288]
        wdr = [Aap[:, 36864 + i * 12288: 36864 + (i + 1) * 12288] for i in range(2)]
        yac = Bf[:, 0:8192]
        cbc = Bap[:, 16384:18432]
        fb_g = [Bap[:, 18432 + i * 2048: 18432 + (i + 1) * 2048] for i in range(4)]
        fb_s = Cap[:, 4096:6144]
        fb_x = Bf[:, 13312:15360]
        tsil = [Cf[:, i * 512:(i + 1) * 512] for i in range(2)]
        tuu = [Cf[:, 1024 + i * 512: 1024 + (i + 1) * 512] for i in range(2)]

        a4mm_last = tok[("a4_mm", NT - 1)]
        a4ev_last = ("v", tok[("a4_ev", NT - 1)])
        s4_toks = [tok[("a4_st", NT - 1)], tok[("a4_st", NT - 2)]]

        env.update(dict(yac=yac, fb_g=fb_g, fb_s=fb_s, fb_x=fb_x,
                        s4_toks=s4_toks))
        conv_step(p, tok, 1.0)
        tok[("conv_done",)] = [("cw0", p.c["cw0"]), ("cw1", p.c["cw1"])]
        wgu_n = 0
        wd_n = 0
        NR_RUN = nranges
        gcnt = 0
        bankfree = {b: a4ev_last for b in range(4)}
        stage_free = {0: None, 1: None}
        for r in range(NR_RUN):
            w = [tok[("xnst", NT - 1)], tok[("xnst", NT - 2)]]
            if r < 2:
                w.append(("pe", a4mm_last))
            else:
                w.append(("pe", tok[("gu_mm_last", r - 2)]))
            for i4 in range(4):
                if "xdma" in lite:
                    tok[("xnr", r)] = p.dma(
                        "sy", f"xr{r % 2}",
                        Aap[:, (r % 2) * 8192:(r % 2) * 8192 + 64],
                        xn_sc[r * 4 + i4][:, 0:64],
                        waits=w if i4 == 0 else ())
                else:
                    tok[("xnr", r)] = p.dma(
                        "sy", f"xr{r % 2}",
                        xnr[r % 2].rearrange("p (c t) -> p c t", t=512)[
                            :, :, i4 * 128:(i4 + 1) * 128],
                        xn_sc[r * 4 + i4].rearrange("p (c t) -> p c t", t=128),
                        waits=w if i4 == 0 else ())
            if r >= 1:
                fin_silu(p, tok, r - 1)
            for q4 in range(nquads):
                if r >= 1 and 1 <= q4 <= 4:
                    t4f = q4 - 1
                    if t4f >= 1:
                        fin_prod(p, tok, r - 1, t4f - 1)
                    fin_load(p, tok, r - 1, t4f)
                if r >= 1 and q4 == 5:
                    fin_prod(p, tok, r - 1, 3)
                prevq = (r, q4 - 1) if q4 >= 1 else \
                        ((r - 1, nquads - 1) if r >= 1 else None)
                w = []
                if prevq:
                    w.append(("pe", tok[("cb_mm", prevq[0], prevq[1], 3)]))
                t_oh = p.dma("sy", "oh1", oh_b[:],
                             oh_d[:, q4 * 512:(q4 + 1) * 512], waits=w)
                for j in range(4):
                    bank = 4 + (j % 2)
                    w = [t_oh, ("v", tok[("ctev", NT - 1)])]
                    if j < 2:
                        if prevq:
                            w.append(("v", tok[("dn_ev", prevq[0], prevq[1], 1)]))
                        else:
                            w.append(a4ev_last)
                    else:
                        w.append(("sc", tok[("cb_ev", r, q4, j - 2)]))
                    if "cb" in lite:
                        p.mm(pb[bank][:, 0:64], oh_b[:, j * 128:(j + 1) * 128],
                             cT[:, r * 512:r * 512 + 64], start=True, stop=True,
                             waits=w)
                    else:
                        p.mm(pb[bank][:], oh_b[:, j * 128:(j + 1) * 128],
                             cT[:, r * 512:(r + 1) * 512], start=True, stop=True,
                             waits=w)
                    tok[("cb_mm", r, q4, j)] = p.mm_inc()
                    w = [("pe", tok[("cb_mm", r, q4, j)])]
                    if prevq:
                        w.append(("v", tok[("gu_tu", prevq[0], prevq[1], j, IB - 1)]))
                    tok[("cb_ev", r, q4, j)] = p.sop(
                        "copy", cbc[:, j * 512:(j + 1) * 512], pb[bank][:, 0:512],
                        waits=w)
                for j in range(4):
                    e = q4 * 4 + j
                    for ib in range(IB):
                        bufi = wgu_n % 2
                        wgu_n += 1
                        w = []
                        if wgu_n >= 3:
                            w.append(("pe", tok[("gu_3rd_done", wgu_n - 2)]))
                        elif r == 0:
                            w.append(("pe", a4mm_last))
                        if r == 0:
                            w += tok[("conv_done",)]
                        if "wdma" in lite:
                            tok[("wgu", wgu_n)] = p.dma(
                                "sy", f"wg{bufi}",
                                Aap[:, 16384 + bufi * 4096:16384 + bufi * 4096 + 64],
                                wff_sc(e)[:, ib * 4096:ib * 4096 + 64], waits=w)
                        else:
                            tok[("wgu", wgu_n)] = p.dma(
                                "sy", f"wg{bufi}", wgu[bufi],
                                wff_sc(e)[:, ib * 4096:(ib + 1) * 4096], waits=w)
                        if True:
                            gbank = (2 * gcnt) % 4
                            ubank = (2 * gcnt + 1) % 4
                            gcnt += 1
                            for gu, bank in ((0, gbank), (1, ubank)):
                                w = [tok[("wgu", wgu_n)],
                                     tok[("xnr", r)]]
                                bf = bankfree[bank]
                                if bf:
                                    w.append(bf)
                                off = gu * 2048
                                for dc in range(DC):
                                    if "gu" in lite:
                                        p.mm(pb[bank][:, 0:64],
                                             wgu[bufi][:, off + dc * 128: off + dc * 128 + 128],
                                             xnr[r % 2][:, dc * 512:dc * 512 + 64],
                                             start=(dc == 0), stop=(dc == DC - 1),
                                             waits=w if dc == 0 else ())
                                    else:
                                        wb0 = 16384 + bufi * 4096 + off
                                        xb0 = (r % 2) * 8192
                                        p.mm(pb[bank][:],
                                             Aap[:, wb0 + dc * 128: wb0 + dc * 128 + 128],
                                             Aap[:, xb0 + dc * 512: xb0 + (dc + 1) * 512],
                                             start=(dc == 0), stop=(dc == DC - 1),
                                             waits=w if dc == 0 else ())
                                tok[("gu_mmt", r, q4, j, ib, gu)] = p.mm_inc()
                            sf = stage_free[ib % 2]
                            w = [("pe", tok[("gu_mmt", r, q4, j, ib, 0)])]
                            if sf:
                                w.append(sf)
                            if "sil" in lite:
                                tok[("gu_sil", r, q4, j, ib)] = p.sop(
                                    "activation", Cf[:, (ib % 2) * 512:(ib % 2) * 512 + 64],
                                    pb[gbank][:, 0:64], AF.Silu, waits=w)
                            else:
                                tok[("gu_sil", r, q4, j, ib)] = p.sop(
                                    "activation", tsil[ib % 2], pb[gbank][:], AF.Silu,
                                    waits=w)
                            w = [("pe", tok[("gu_mmt", r, q4, j, ib, 1)]),
                                 ("sc", tok[("gu_sil", r, q4, j, ib)])]
                            if "tu" in lite:
                                tok[("gu_tu", r, q4, j, ib)] = p.vop(
                                    "tensor_tensor",
                                    Cf[:, 1024 + (ib % 2) * 512:1024 + (ib % 2) * 512 + 64],
                                    pb[ubank][:, 0:64],
                                    cbc[:, j * 512:j * 512 + 64], ALU.mult, waits=w)
                            else:
                                tok[("gu_tu", r, q4, j, ib)] = p.vop(
                                    "tensor_tensor", tuu[ib % 2], pb[ubank][:],
                                    cbc[:, j * 512:(j + 1) * 512], ALU.mult, waits=w)
                            w = [("v", tok[("gu_tu", r, q4, j, ib)])]
                            if prevq:
                                w.append(("pe", tok[("dn_mm", prevq[0], prevq[1], 1)]))
                            if "hv" in lite:
                                hv = p.gop(
                                    "tensor_tensor",
                                    hq[:, (j * IB + ib) * 512:(j * IB + ib) * 512 + 64],
                                    Cf[:, (ib % 2) * 512:(ib % 2) * 512 + 64],
                                    Cf[:, 1024 + (ib % 2) * 512:1024 + (ib % 2) * 512 + 64],
                                    ALU.mult, waits=w)
                            else:
                                hv = p.gop(
                                    "tensor_tensor",
                                    hq[:, (j * IB + ib) * 512:(j * IB + ib + 1) * 512],
                                    tsil[ib % 2], tuu[ib % 2], ALU.mult, waits=w)
                            tok[("gu_hv", r, q4, j, ib)] = hv
                            stage_free[ib % 2] = ("gpc", hv)
                            bankfree[gbank] = ("gpc", hv)
                            bankfree[ubank] = ("gpc", hv)
                        tok[("gu_3rd_done", wgu_n)] = tok[("gu_mmt", r, q4, j, ib, 1)]
                tok[("gu_mm_last", r)] = tok[("gu_mmt", r, nquads - 1, 3, IB - 1, 1)] \
                    if q4 == nquads - 1 else tok.get(("gu_mm_last", r), 0)
                for tp in range(2):
                    for j in range(4):
                        e = q4 * 4 + j
                        wdi = wd_n % 2
                        wd_n += 1
                        w = []
                        if wd_n >= 3:
                            w.append(("pe", tok[("dn_grp", wd_n - 2)]))
                        if r == 0:
                            w += tok[("conv_done",)]
                        if "wdma" in lite:
                            tok[("wd_ld", wd_n)] = p.dma(
                                "sy", f"wd{wdi}",
                                Aap[:, 36864 + wdi * 12288:36864 + wdi * 12288 + 64],
                                wff_sc(e)[:, WDOFF:WDOFF + 64], waits=w)
                        else:
                            tok[("wd_ld", wd_n)] = p.dma(
                                "sy", f"wd{wdi}", wdr[wdi],
                                wff_sc(e)[:, WDOFF:WDOFF + WDLEN], waits=w)
                        for ib in range(IB):
                            w0 = [tok[("wd_ld", wd_n)]]
                            if ib == 0:
                                w0.append(("gpc", tok[("gu_hv", r, q4, j, IB - 1)]))
                            if j == 0 and ib == 0:
                                if tp == 0:
                                    w0.append(("sc", tok[("cb_ev", r, q4, 3)]))
                                    w0.append(("gpc", tok[("gu_hv", r, q4, 3, IB - 1)]))
                                    if prevq is None:
                                        w0.append(a4ev_last)
                                else:
                                    w0.append(("v", tok[("dn_ev", r, q4, 0)]))
                            for t4 in (tp * 2, tp * 2 + 1):
                                for db in range(DB):
                                    if "dn" in lite:
                                        p.mm(pb[(t4 - tp * 2) * 4 + db][:, 0:64],
                                             hq[:, (j * IB + ib) * 512 + t4 * 128:
                                                (j * IB + ib) * 512 + (t4 + 1) * 128],
                                             wdr[wdi][:, ib * 2048 + db * 512:
                                                      ib * 2048 + db * 512 + 64],
                                             start=(j == 0 and ib == 0),
                                             stop=(j == 3 and ib == IB - 1),
                                             waits=w0 if (t4 == tp * 2 and db == 0) else ())
                                    else:
                                        hb0 = 24576 + (j * IB + ib) * 512 + t4 * 128
                                        db0 = 36864 + wdi * 12288 + ib * 2048 + db * 512
                                        p.mm(pb[(t4 - tp * 2) * 4 + db][:],
                                             Aap[:, hb0:hb0 + 128],
                                             Aap[:, db0:db0 + 512],
                                             start=(j == 0 and ib == 0),
                                             stop=(j == 3 and ib == IB - 1),
                                             waits=w0 if (t4 == tp * 2 and db == 0) else ())
                        tok[("dn_grp", wd_n)] = p.mm_inc()
                    tok[("dn_mm", r, q4, tp)] = tok[("dn_grp", wd_n)]
                    w = [("pe", tok[("dn_mm", r, q4, tp)])]
                    if q4 == 0 and tp == 0:
                        if r == 0:
                            w += s4_toks
                        else:
                            w.append(("sc", tok[("fin_g", r - 1, TB4 - 1)]))
                    first = True
                    for t4 in (tp * 2, tp * 2 + 1):
                        for db in range(DB):
                            nw = 64 if "ye" in lite else 512
                            dst = yac[:, t4 * 2048 + db * 512: t4 * 2048 + db * 512 + nw]
                            last = (t4 == tp * 2 + 1 and db == DB - 1)
                            if q4 == 0:
                                p.vop("tensor_copy", dst,
                                      pb[(t4 - tp * 2) * 4 + db][:, 0:nw],
                                      waits=w if first else (), inc=last)
                            else:
                                p.vop("tensor_tensor", dst,
                                      pb[(t4 - tp * 2) * 4 + db][:, 0:nw], dst, ALU.add,
                                      waits=w if first else (), inc=last)
                            first = False
                    tok[("dn_ev", r, q4, tp)] = p.c["v"]
                bankfree[0] = bankfree[1] = bankfree[2] = bankfree[3] = \
                    ("v", tok[("dn_ev", r, q4, 1)])
        fin_silu(p, tok, NR_RUN - 1)
        for t4 in range(TB4):
            fin_load(p, tok, NR_RUN - 1, t4)
            fin_prod(p, tok, NR_RUN - 1, t4)
        return

    def fin_silu(p: P, tok, r):
        for t4 in range(TB4):
            w = [("v", tok[("dn_ev", r, nquads - 1, 1)])]
            if r >= 1:
                w.append(("v", tok[("fin_t", r - 1, t4)]))
            tok[("fin_g", r, t4)] = p.sop(
                "activation", env["fb_g"][t4],
                env["yac"][:, t4 * 2048:(t4 + 1) * 2048], AF.Silu, waits=w)

    def fin_load(p: P, tok, r, t4):
        gtb = r * 4 + t4
        w = [("sc", tok[("fin_g", r, t4)])] + env["s4_toks"]
        if t4 >= 1:
            w.append(("v", tok[("fin_t", r, t4 - 1)]))
        elif r >= 1:
            w.append(("v", tok[("fin_t", r - 1, TB4 - 1)]))
        if t4 >= 1:
            w.append(tok[("out_st", r, t4 - 1)])
        elif r >= 1:
            w.append(tok[("out_st", r - 1, TB4 - 1)])
        tok[("fin_sld", r, t4)] = p.dma("gp", "fsg", env["fb_s"],
                                        s_sc[gtb], waits=w)
        tok[("fin_xld", r, t4)] = p.dma(
            "gp", "fsg", env["fb_x"], x_d[gtb * 128:(gtb + 1) * 128, :])

    def fin_prod(p: P, tok, r, t4):
        gtb = r * 4 + t4
        p.vop("tensor_tensor", env["fb_g"][t4], env["fb_s"], env["fb_g"][t4],
              ALU.mult, waits=[tok[("fin_xld", r, t4)],
                               ("sc", tok[("fin_g", r, t4)])],
              drain=True, inc=False)
        tok[("fin_t", r, t4)] = p.vop("tensor_tensor", env["fb_x"],
                                      env["fb_g"][t4], env["fb_x"],
                                      ALU.add, drain=True)
        tok[("out_st", r, t4)] = p.dma(
            "scd", "os", out_d[gtb * 128:(gtb + 1) * 128, :], env["fb_x"],
            waits=[("v", tok[("fin_t", r, t4)])])

    def _at_slice(Aap_, fc, ec):
        o = fc * 2048 + ec * 128
        return Aap_[:, o:o + 128]

    def finish(p: P, tok):
        if dump:
            dbuf = Bf[:, 0:4096]
            dbufb = Bap[:, 8192:12288]
            bw = [(k, v) for k, v in p.c.items() if v > 0]
            first = True
            prev_tt = None
            t2 = None
            tt = None
            for nm in dump:
                if nm == "cT":
                    w = list(bw) if first else []
                    if tt is not None:
                        w.append(tt)
                    if t2 is not None:
                        w.append(("v", t2))
                    t = p.vop("tensor_copy", Bf[0:E, 0:TCORE], cT[:, :],
                              waits=w, drain=True)
                    first = False
                    prev_tt = p.dma("scd", "du2", dump_d[nm], Bf[0:E, 0:TCORE],
                                    waits=[("v", t)])
                    p.wait("scalar", *prev_tt)
                    t2 = t
                    tt = prev_tt
                    continue
                src, dt, shp = dump_specs[nm]
                rows, cols = shp
                nrt = rows // 128
                nct = (cols + 4095) // 4096
                for rt in range(nrt):
                    for ct in range(nct):
                        cw = min(4096, cols - ct * 4096)
                        sview = src[rt * 128:(rt + 1) * 128,
                                    ct * 4096:ct * 4096 + cw]
                        w = list(bw) if first else []
                        if t2 is not None:
                            w.append(("v", t2))
                        if tt is not None:
                            w.append(tt)
                        if dt == BF16:
                            t1 = p.dma("sy", "du1", dbufb[:, 0:cw], sview, waits=w)
                            t2 = p.vop("tensor_copy", dbuf[:, 0:cw],
                                       dbufb[:, 0:cw], waits=[t1],
                                       drain=True)
                        else:
                            t1 = p.dma("sy", "du1", dbuf[:, 0:cw], sview, waits=w)
                            t2 = p.vop("tensor_copy", dbuf[:, 0:cw],
                                       dbuf[:, 0:cw], waits=[t1],
                                       drain=True)
                        first = False
                        tt = p.dma("scd", "du2",
                                   dump_d[nm][rt * 128:(rt + 1) * 128,
                                              ct * 4096:ct * 4096 + cw],
                                   dbuf[:, 0:cw], waits=[("v", t2)])
                        p.wait("scalar", *tt)
                        prev_tt = tt
        for k, v in p.c.items():
            if v > 0:
                p.wait("sync", k, v)

    blk = ctx.enter_context(nc.Block())

    @blk.sync
    def _(eng):
        walk(P("sync", eng, sems))

    @blk.gpsimd
    def _(eng):
        walk(P("gpsimd", eng, sems))

    @blk.tensor
    def _(eng):
        walk(P("tensor", eng, sems))

    @blk.vector
    def _(eng):
        walk(P("vector", eng, sems))

    @blk.scalar
    def _(eng):
        walk(P("scalar", eng, sems))

    ctx.close()
    return nc


def make_consts():
    idf = np.eye(128, dtype=np.float32)
    idb = np.eye(128, dtype=ml_dtypes.bfloat16)
    oh = np.zeros((E, E * 128), dtype=ml_dtypes.bfloat16)
    for e in range(E):
        oh[e, e * 128:(e + 1) * 128] = 1
    return idf, idb, oh


def make_in_maps(L0_attn_x, L0_attn_Wq, L0_attn_Wk, L0_attn_Wv, L0_attn_Wo,
                 L0_ffn_gate_w, L0_ffn_w_gate, L0_ffn_w_up, L0_ffn_w_down):
    x = np.asarray(L0_attn_x, dtype=np.float32).reshape(-1, D)
    idf, idb, oh = make_consts()
    shared = {
        "wq": np.ascontiguousarray(np.asarray(L0_attn_Wq, np.float32)),
        "wk": np.ascontiguousarray(np.asarray(L0_attn_Wk, np.float32)),
        "wv": np.ascontiguousarray(np.asarray(L0_attn_Wv, np.float32)),
        "wo": np.ascontiguousarray(np.asarray(L0_attn_Wo, np.float32)),
        "gw": np.ascontiguousarray(np.asarray(L0_ffn_gate_w, np.float32)),
        "wg": np.ascontiguousarray(np.asarray(L0_ffn_w_gate, np.float32)),
        "wu": np.ascontiguousarray(np.asarray(L0_ffn_w_up, np.float32)),
        "wd": np.ascontiguousarray(np.asarray(L0_ffn_w_down, np.float32)),
        "idf": idf, "idb": idb, "oh": oh,
    }
    return [dict(shared, x=np.ascontiguousarray(x[c * TCORE:(c + 1) * TCORE]))
            for c in range(8)]


def kernel(L0_attn_x, L0_attn_Wq, L0_attn_Wk, L0_attn_Wv, L0_attn_Wo,
           L0_ffn_gate_w, L0_ffn_w_gate, L0_ffn_w_up, L0_ffn_w_down):
    in_maps = make_in_maps(L0_attn_x, L0_attn_Wq, L0_attn_Wk, L0_attn_Wv,
                           L0_attn_Wo, L0_ffn_gate_w, L0_ffn_w_gate,
                           L0_ffn_w_up, L0_ffn_w_down)
    nc = build_nc()
    res = run_bass_kernel_spmd(nc, in_maps, core_ids=list(range(8)))
    out = np.concatenate([res.results[c]["out"] for c in range(8)], axis=0)
    return out.reshape(2, 16384, D)
